# version 20
# baseline (speedup 1.0000x reference)
import numpy as np
import ml_dtypes
import concourse.bass as bass
import concourse.mybir as mybir
from concourse.bass_utils import run_bass_kernel_spmd
from concourse.aot_env import allow_ici_all_to_all
from contextlib import ExitStack

F32 = mybir.dt.float32
BF16 = mybir.dt.bfloat16
F8 = mybir.dt.float8e4
U8 = mybir.dt.uint8
AF = mybir.ActivationFunctionType
OP = mybir.AluOpType
AX = mybir.AxisListType

D = 2048
NH = 16
DH = 128
S = 4096
B = 2
NCORES = 8
NHL = 4
NCH = D // 128
NTB = S // 128
NG = S // 512
EPS = 1e-5
NEG = -1.0e30
NDMA = 6

_DEBUG_TAPS = False
_PHASE_LIMIT = 99
_SKIP_BOUND = False
_SKIP_VT = False
_SKIP_25 = False
_NG_LIMIT = NG
_SKIP_PROJ = False
_PROJ_N = 3
_SKIP_EVAC = False
_SKIP_SPLIT = False
_P1_BARRIER = False


def build_program():
    nc = bass.Bass("TRN2", target_bir_lowering=False, detect_race_conditions=False)

    xb = nc.dram_tensor("xb", [S, D], F32, kind="ExternalInput").ap()
    xres = nc.dram_tensor("xres", [B, 512, D], F32, kind="ExternalInput").ap()
    wq = nc.dram_tensor("wq", [D, 512], F32, kind="ExternalInput").ap()
    wk = nc.dram_tensor("wk", [D, 512], F32, kind="ExternalInput").ap()
    wv = nc.dram_tensor("wv", [D, 512], F32, kind="ExternalInput").ap()
    bq = nc.dram_tensor("bq", [512], F32, kind="ExternalInput").ap()
    bk = nc.dram_tensor("bk", [512], F32, kind="ExternalInput").ap()
    bv = nc.dram_tensor("bv", [512], F32, kind="ExternalInput").ap()
    wo = nc.dram_tensor("wo", [D, D], F32, kind="ExternalInput").ap()
    bo = nc.dram_tensor("bo", [D], F32, kind="ExternalInput").ap()
    lng = nc.dram_tensor("lng", [D], F32, kind="ExternalInput").ap()
    lnb = nc.dram_tensor("lnb", [D], F32, kind="ExternalInput").ap()
    w8 = nc.dram_tensor("w8", [D, D], F8, kind="ExternalInput").ap()
    wsc = nc.dram_tensor("wsc", [1, 1], F32, kind="ExternalInput").ap()
    isc = nc.dram_tensor("isc", [1, 1], BF16, kind="ExternalInput").ap()
    c32t = nc.dram_tensor("c32t", [1, 1], F32, kind="ExternalInput").ap()
    identf = nc.dram_tensor("identf", [128, 128], F32, kind="ExternalInput").ap()
    identb = nc.dram_tensor("identb", [128, 128], BF16, kind="ExternalInput").ap()
    ident8 = nc.dram_tensor("ident8", [128, 128], F8, kind="ExternalInput").ap()
    dmask = nc.dram_tensor("dmask", [128, 128], F32, kind="ExternalInput").ap()
    onesf = nc.dram_tensor("onesf", [1, 128], F32, kind="ExternalInput").ap()

    out = nc.dram_tensor("out", [B, 512, D], F32, kind="ExternalOutput").ap()

    xnf_hi = nc.dram_tensor("xnf_hi", [NG, 128, NCH * 512], BF16, kind="Internal").ap()
    xnf_lo = nc.dram_tensor("xnf_lo", [NG, 128, NCH * 512], BF16, kind="Internal").ap()
    a2a_in = [nc.dram_tensor(f"a2a_in{l}", [8, 128, 1024], BF16, kind="Internal")
              for l in range(NHL)]
    a2a_out = [nc.dram_tensor(f"a2a_out{l}", [8, 128, 1024], BF16, kind="Internal")
               for l in range(NHL)]

    dbg = {}
    if _DEBUG_TAPS:
        dbg["xnf_hi"] = nc.dram_tensor("dbg_xnf_hi", [NG, 128, NCH * 512], BF16, kind="ExternalOutput").ap()
        dbg["xnf_lo"] = nc.dram_tensor("dbg_xnf_lo", [NG, 128, NCH * 512], BF16, kind="ExternalOutput").ap()
        dbg["qk"] = nc.dram_tensor("dbg_qk", [4, 128, S], BF16, kind="ExternalOutput").ap()
        dbg["av0"] = nc.dram_tensor("dbg_av0", [8, 128, 1024], BF16, kind="ExternalOutput").ap()
        dbg["x1"] = nc.dram_tensor("dbg_x1", [8, 128, D], F32, kind="ExternalOutput").ap()
        dbg["ysb"] = nc.dram_tensor("dbg_ysb", [B, 128, 4 * D], F32, kind="ExternalOutput").ap()
        dbg["rcv"] = nc.dram_tensor("dbg_rcv", [16, 128, 512], F32, kind="ExternalOutput").ap()
        dbg["st1"] = nc.dram_tensor("dbg_st1", [NTB, 128, 8], F32, kind="ExternalOutput").ap()

    ctx = ExitStack()
    qD = [ctx.enter_context(nc.semaphore(f"qD{i}")) for i in range(NDMA)]
    qG = ctx.enter_context(nc.semaphore("qG"))
    sT = ctx.enter_context(nc.semaphore("sT"))
    sV = ctx.enter_context(nc.semaphore("sV"))
    sA = ctx.enter_context(nc.semaphore("sA"))
    sC = ctx.enter_context(nc.semaphore("sC"))
    esem = dict(sT=sT, sV=sV, sA=sA, sC=sC)
    cnt = dict(sT=0, sV=0, sA=0, sC=0)
    dcnt = [0] * NDMA
    dturn = [0]
    gcnt = [0]

    ps0 = ctx.enter_context(nc.psum_tensor("ps0", [128, 512], F32))
    ps1 = ctx.enter_context(nc.psum_tensor("ps1", [128, 512], F32))
    ps2 = ctx.enter_context(nc.psum_tensor("ps2", [128, 512], F32))
    psT32 = ctx.enter_context(nc.psum_tensor("psT32", [128, 512], F32))

    sb = lambda name, shape, dt: ctx.enter_context(nc.sbuf_tensor(name, shape, dt))
    c_identf = sb("c_identf", [128, 128], F32)
    c_identb = sb("c_identb", [128, 128], BF16)
    c_ident8 = sb("c_ident8", [128, 128], F8)
    c_dmask = sb("c_dmask", [128, 128], F32)
    c_onesf = sb("c_onesf", [1, 128], F32)
    c_onesc = sb("c_onesc", [128, 1], BF16)
    c_zerob = sb("c_zerob", [128, 128], BF16)
    c_lng = sb("c_lng", [128, NCH], F32)
    c_lnb = sb("c_lnb", [128, NCH], F32)
    c_bq = sb("c_bq", [128, NHL], F32)
    c_bk = sb("c_bk", [128, NHL], F32)
    c_bv = sb("c_bv", [128, NHL], F32)
    c_sc = sb("c_sc", [1, 8], F32)
    c_iscb = sb("c_iscb", [1, 1], BF16)
    rcb = sb("rcb", [128, 1], F32)
    invsb = sb("invsb", [128, 1], F32)
    yscb = sb("yscb", [128, 1], F32)
    negm11 = sb("negm11", [1, 1], F32)
    negmb = sb("negmb", [128, 1], F32)
    bnd = sb("bnd", [1, 8], F32)
    stat = sb("stat", [128, 16], F32)
    bnbuf = sb("bnbuf", [128, 48], F32)
    mvbuf = sb("mvbuf", [128, 4], F32)

    phase_ops = []

    def emit(eng, fn):
        phase_ops.append((eng, fn))

    def inc(sem_name, n=1):
        cnt[sem_name] += n
        return cnt[sem_name]

    def wait(eng, sem_name, val):
        assert val <= cnt[sem_name], f"wait {sem_name}>={val}, emitted {cnt[sem_name]}"
        s = esem[sem_name]
        emit(eng, lambda e, s=s, v=val: e.wait_ge(s, v))

    def dma(eng_name, outap, inap, **kw):
        if eng_name == "gpsimd":
            gcnt[0] += 16
            tok = ("G", gcnt[0])
            emit(eng_name, lambda e, o=outap, i=inap, kw=kw: e.dma_start(out=o, in_=i, **kw).then_inc(qG, 16))
            return tok
        i = dturn[0] % NDMA
        dturn[0] += 1
        dcnt[i] += 16
        tok = (i, dcnt[i])
        s = qD[i]
        emit(eng_name, lambda e, o=outap, ia=inap, s=s, kw=kw: e.dma_start(out=o, in_=ia, **kw).then_inc(s, 16))
        return tok

    def wait_dma(eng, tok):
        i, v = tok
        s = qG if i == "G" else qD[i]
        emit(eng, lambda e, s=s, v=v: e.wait_ge(s, v))

    def wait_all_dma(eng):
        for i in range(NDMA):
            if dcnt[i]:
                v = dcnt[i]
                emit(eng, lambda e, s=qD[i], v=v: e.wait_ge(s, v))
        if gcnt[0]:
            v = gcnt[0]
            emit(eng, lambda e, v=v: e.wait_ge(qG, v))

    def op(eng_name, build, sem_name=None, n=1):
        if eng_name in ("vector", "scalar"):
            chain = "sV" if eng_name == "vector" else "sA"
            pre = cnt[chain]
            cs = esem[chain]
            if pre:
                emit(eng_name, lambda e, s=cs, v=pre: e.wait_ge(s, v))
            v = inc(chain)
            emit(eng_name, lambda e, b=build, s=cs: b(e).then_inc(s, 1))
            return v
        if sem_name is None:
            emit(eng_name, lambda e, b=build: b(e))
            return None
        s = esem[sem_name]
        v = inc(sem_name, n)
        emit(eng_name, lambda e, b=build, s=s, n=n: b(e).then_inc(s, n))
        return v

    def flush(block):
        ops = list(phase_ops)
        phase_ops.clear()
        for eng_name in ("sync", "tensor", "vector", "scalar", "gpsimd"):
            mine = [fn for e, fn in ops if e == eng_name]
            if not mine:
                continue
            getattr(block, eng_name)(lambda eng, mine=mine: [fn(eng) for fn in mine])

    def barrier():
        for eng in ("sync", "tensor", "vector", "scalar", "gpsimd"):
            for sname in ("sT", "sV", "sA"):
                if cnt[sname]:
                    wait(eng, sname, cnt[sname])
            wait_all_dma(eng)

    def rsqrt_col(acol, dst, t0, t1):
        va = op("scalar", lambda e, a=acol, d=dst: e.activation(d, a, AF.Sqrt), "sA")
        wait("vector", "sA", va)
        op("vector", lambda e, d=dst: e.reciprocal(d, d))
        for _ in range(1):
            op("vector", lambda e, d=dst, t=t0: e.tensor_mul(t, d, d))
            op("vector", lambda e, a=acol, t=t0, u=t1: e.tensor_mul(u, a, t))
            op("vector", lambda e, u=t1: e.tensor_scalar(u, u, -0.5, 1.5, OP.mult, OP.add))
            op("vector", lambda e, d=dst, u=t1: e.tensor_mul(d, d, u))

    def ln_stats(tile_ap, sc0, double):
        bs = bnbuf[:, 24 * (sc0 // 8):24 * (sc0 // 8) + 24]
        mv = mvbuf[:, 2 * (sc0 // 8):2 * (sc0 // 8) + 2]
        for k in range(4):
            op("vector", lambda e, k=k, t=tile_ap, b=bs: e.bn_stats(
                b[:, 6 * k:6 * k + 6], t[:, 512 * k:512 * (k + 1)]))
        op("vector", lambda e, b=bs, m=mv: e.bn_aggr(m, b))
        a_ = stat[:, sc0 + 0:sc0 + 1]
        t0 = stat[:, sc0 + 1:sc0 + 2]
        t1 = stat[:, sc0 + 2:sc0 + 3]
        st_ = stat[:, sc0 + 3:sc0 + 4]
        b1_ = stat[:, sc0 + 4:sc0 + 5]
        var = mv[:, 1:2]
        if double:
            op("vector", lambda e, v=var, t=t0: e.tensor_scalar(t, v, EPS, EPS * EPS, OP.mult, OP.add))
            va = op("vector", lambda e, v=var, t=t0, a=a_: e.tensor_add(a, v, t), "sV")
        else:
            va = op("vector", lambda e, v=var, a=a_: e.tensor_scalar(a, v, 1.0, EPS, OP.mult, OP.add), "sV")
        wait("scalar", "sV", va)
        rsqrt_col(a_, st_, t0, t1)
        op("vector", lambda e, m=mv, s=st_, b=b1_: e.tensor_mul(b, m[:, 0:1], s))
        vb = op("vector", lambda e, b=b1_: e.tensor_scalar_mul(b, b, -1.0), "sV")
        return st_, b1_, vb

    block = ctx.enter_context(nc.Block())

    dma("sync", c_identf[:], identf)
    dma("sync", c_identb[:], identb)
    dma("sync", c_ident8[:], ident8)
    dma("sync", c_dmask[:], dmask)
    dma("sync", c_onesf[:], onesf)
    dma("sync", c_lng[:], lng.rearrange("(c p) -> p c", p=128), allow_slow_non_contiguous=True)
    dma("sync", c_lnb[:], lnb.rearrange("(c p) -> p c", p=128), allow_slow_non_contiguous=True)
    dma("sync", c_bq[:], bq.rearrange("(h p) -> p h", p=128), allow_slow_non_contiguous=True)
    dma("sync", c_bk[:], bk.rearrange("(h p) -> p h", p=128), allow_slow_non_contiguous=True)
    dma("sync", c_bv[:], bv.rearrange("(h p) -> p h", p=128), allow_slow_non_contiguous=True)
    t1_ = dma("sync", c_sc[:, 0:1], c32t)
    t2_ = dma("sync", c_sc[:, 2:3], wsc)
    t3_ = dma("sync", c_iscb[:], isc)
    for t in (t1_, t2_, t3_):
        wait_dma("vector", t)
    op("vector", lambda e: e.memset(c_onesc[:], 1.0))
    op("vector", lambda e: e.memset(c_zerob[:], 0.0))
    op("vector", lambda e: e.tensor_copy(c_sc[:, 3:4], c_iscb[:]))
    op("vector", lambda e: e.reciprocal(c_sc[:, 1:2], c_sc[:, 0:1]))
    op("vector", lambda e: e.reciprocal(c_sc[:, 4:5], c_sc[:, 3:4]))
    vs = op("vector", lambda e: e.tensor_mul(c_sc[:, 5:6], c_sc[:, 3:4], c_sc[:, 2:3]), "sV")
    wait("tensor", "sV", vs)
    op("tensor", lambda e: e.matmul(ps1[0:128, 0:1], c_onesf[:], c_sc[:, 1:2], start=True, stop=True))
    op("tensor", lambda e: e.matmul(ps1[0:128, 1:2], c_onesf[:], c_sc[:, 4:5], start=True, stop=True))
    vt = op("tensor", lambda e: e.matmul(ps1[0:128, 2:3], c_onesf[:], c_sc[:, 5:6], start=True, stop=True), "sT")
    wait("vector", "sT", vt)
    op("vector", lambda e: e.tensor_copy(rcb[:], ps1[0:128, 0:1]))
    op("vector", lambda e: e.tensor_copy(invsb[:], ps1[0:128, 1:2]))
    vp0 = op("vector", lambda e: e.tensor_copy(yscb[:], ps1[0:128, 2:3]), "sV")

    ctxA = ExitStack()
    sbA = lambda name, shape, dt: ctxA.enter_context(nc.sbuf_tensor(name, shape, dt))
    ps3 = ctxA.enter_context(nc.psum_tensor("ps3", [128, 512], F32))
    ps4 = ctxA.enter_context(nc.psum_tensor("ps4", [128, 512], F32))
    ps5 = ctxA.enter_context(nc.psum_tensor("ps5", [128, 512], F32))
    psTb = ctxA.enter_context(nc.psum_tensor("psTb", [128, 1024], BF16))

    xt = [sbA("xt0", [128, D], F32), sbA("xt1", [128, D], F32)]
    ynorm = sbA("ynorm", [128, D], F32)
    xnf32 = [sbA("xnf32a", [128, D], F32), sbA("xnf32b", [128, D], F32)]
    xnb_hi = [sbA("xnb_hi0", [128, NCH * 512], BF16), sbA("xnb_hi1", [128, NCH * 512], BF16)]
    xnb_lo = [sbA("xnb_lo0", [128, NCH * 512], BF16), sbA("xnb_lo1", [128, NCH * 512], BF16)]
    w32t = ynorm
    wqh = [sbA("wq_hi", [128, D], BF16), sbA("wq_lo", [128, D], BF16)]
    wkh = [sbA("wk_hi", [128, D], BF16), sbA("wk_lo", [128, D], BF16)]
    wvh = [sbA("wv_hi", [128, D], BF16), sbA("wv_lo", [128, D], BF16)]
    qf32 = {"q": sbA("qf32q", [128, 512], F32), "k": sbA("qf32k", [128, 512], F32),
            "v": sbA("qf32v", [128, 512], F32)}
    qsqb = sbA("qsqb", [128, 512], BF16)
    Qhi = sbA("Qhi", [128, S], BF16)
    Qlo = sbA("Qlo", [128, S], BF16)
    Khi = sbA("Khi", [128, S], BF16)
    Klo = sbA("Klo", [128, S], BF16)
    vfs = [sbA("vfs_hi", [128, 512], BF16), sbA("vfs_lo", [128, 512], BF16)]
    vaug = sbA("vaug", [128, NTB * 257], BF16)
    p32b = [sbA("p32b0", [128, 512], F32), sbA("p32b1", [128, 512], F32)]
    phib = [sbA("phib0", [128, 512], BF16), sbA("phib1", [128, 512], BF16)]
    plob = [sbA("plob0", [128, 512], BF16), sbA("plob1", [128, 512], BF16)]
    avt32 = [sbA(f"avt32{q}", [128, 128], F32) for q in range(4)]
    avbh = [sbA(f"avbh{q}", [128, 128], BF16) for q in range(4)]
    avbl = [sbA(f"avbl{q}", [128, 128], BF16) for q in range(4)]
    rsumc = [sbA("rsumc0", [128, 1], F32), sbA("rsumc1", [128, 1], F32)]
    avfst = sbA("avfst", [128, 1024], BF16)

    qkvbias = ctxA.enter_context(nc.sbuf_tensor("qkvbias", [128, 12], F32))
    psav_free_w = [None]
    bias_read = [None]
    qkv_evac = {}
    split_prev = {}
    vtev_done = {}
    xnb_consumed = [None, None]
    stg_tok = [None]
    pend_epi = [None]
    epi_ev_prev = [None]
    epi_wait_tr = [None] * 4
    qkv_mm_end = [None]
    attn_end = [None]
    bound_state = {"read": None, "mm": None}
    qkv_tail = []

    def emit_w_section(h):
        wsplit_last = None
        for wi, (wdram, wsb) in enumerate(((wq, wqh), (wk, wkh), (wv, wvh))):
            tw = dma("sync", w32t[:].rearrange("p (c f) -> p c f", c=NCH),
                     wdram.rearrange("(c p) n -> p c n", p=128)[:, :, 128 * h:128 * (h + 1)])
            wait_dma("tensor", tw)
            if psav_free_w[0] is not None:
                wait("tensor", "sV", psav_free_w[0])
                psav_free_w[0] = None
            if bias_read[0] is not None:
                wait("tensor", "sV", bias_read[0])
            vbm_ = None
            for ch2 in range(NCH):
                vbm_ = op("tensor", lambda e, c=ch2, st=(ch2 == 0), sp=(ch2 == NCH - 1): e.matmul(
                    ps3[0:128, 0:1], w32t[:, 128 * c:128 * (c + 1)], c_lnb[:, c:c + 1],
                    start=st, stop=sp), "sT" if ch2 == NCH - 1 else None)
            wait("vector", "sT", vbm_)
            bcol_d = qkvbias[:, 3 * h + wi:3 * h + wi + 1]
            bsrc = {0: c_bq, 1: c_bk, 2: c_bv}[wi][:, h:h + 1]
            bias_read[0] = op("vector", lambda e, d=bcol_d, b=bsrc: e.tensor_tensor(
                d, ps3[0:128, 0:1], b, OP.add), "sV")
            wait_dma("vector", tw)
            for ch2 in range(NCH):
                op("vector", lambda e, c=ch2: e.tensor_scalar(
                    w32t[:, 128 * c:128 * (c + 1)], w32t[:, 128 * c:128 * (c + 1)],
                    c_lng[:, c:c + 1], None, OP.mult))
            op("vector", lambda e, w=wsb: e.tensor_copy(w[0][:], w32t[:]))
            wsplit_last = op("vector", lambda e, w=wsb: e.tensor_tensor(
                w[1][:], w32t[:], w[0][:], OP.subtract), "sV")
            wait("sync", "sV", wsplit_last)
        op("vector", lambda e: e.memset(bnd[:, 0:2], 0.0))
        for kt in range(NTB - 1):
            op("vector", lambda e, kt=kt: e.memset(vaug[:, 257 * kt:257 * kt + 1], 1.0))
        vones = op("vector", lambda e, kt=NTB - 1: e.memset(vaug[:, 257 * kt:257 * kt + 1], 1.0), "sV")
        wait("tensor", "sV", wsplit_last)
        wait("tensor", "sV", vones)
        bound_state["read"] = None
        bound_state["mm"] = None
        return wsplit_last

    def emit_qkv_group(h, tg, xh_b, xl_b):
        for wi in range(3):
            emit_qkv_proj(h, tg, xh_b, xl_b, wi)

    def emit_qkv_proj(h, tg, xh_b, xl_b, wi, split_gate=None):
        for psx, wsb, name in (((ps0, wqh, "q"), (ps1, wkh, "k"), (ps2, wvh, "v"))[wi],):
            if split_gate is not None:
                wait("tensor", "sV", split_gate)
            if name in qkv_evac:
                wait("tensor", "sA", qkv_evac[name])
            for ch in range(NCH):
                wh = wsb[0][:, 128 * ch:128 * (ch + 1)]
                wl = wsb[1][:, 128 * ch:128 * (ch + 1)]
                xh = xh_b[:, 512 * ch:512 * (ch + 1)]
                xl = xl_b[:, 512 * ch:512 * (ch + 1)]
                op("tensor", lambda e, p=psx, w=wh, x=xh, st=(ch == 0): e.matmul(p[:], w, x, start=st, stop=False))
                op("tensor", lambda e, p=psx, w=wh, x=xl: e.matmul(p[:], w, x, start=False, stop=False))
                if ch < NCH - 1:
                    op("tensor", lambda e, p=psx, w=wl, x=xh: e.matmul(p[:], w, x, start=False, stop=False))
            vmm = op("tensor", lambda e, p=psx, w=wsb[1][:, 128 * (NCH - 1):], x=xh_b[:, 512 * (NCH - 1):]: e.matmul(
                p[:], w, x, start=False, stop=True), "sT")
            xnb_consumed[tg % 2] = vmm

            wait("scalar", "sT", vmm)
            if name in split_prev:
                wait("scalar", "sV", split_prev[name])
            bcol = qkvbias[:, 3 * h + {"q": 0, "k": 1, "v": 2}[name]:
                           3 * h + {"q": 0, "k": 1, "v": 2}[name] + 1]
            fb = qf32[name]
            ve = op("scalar", lambda e, p=psx, b=bcol, f=fb: e.activation(
                f[:], p[:], AF.Identity, bias=b), "sA")
            qkv_evac[name] = ve
            wait("vector", "sA", ve)
            if name == "q":
                hi_d = Qhi[:, 512 * tg:512 * (tg + 1)]
                lo_d = Qlo[:, 512 * tg:512 * (tg + 1)]
                op("vector", lambda e, d=hi_d, f=fb: e.tensor_scalar(d, f[:], rcb[:, 0:1], None, OP.mult))
                vspl = op("vector", lambda e, d=lo_d, s=hi_d, f=fb: e.scalar_tensor_tensor(
                    d, f[:], rcb[:, 0:1], s, OP.mult, OP.subtract), "sV")
            elif name == "k":
                hi_d = Khi[:, 512 * tg:512 * (tg + 1)]
                lo_d = Klo[:, 512 * tg:512 * (tg + 1)]
                op("vector", lambda e, d=hi_d, f=fb: e.tensor_copy(d, f[:]))
                vspl = op("vector", lambda e, d=lo_d, s=hi_d, f=fb: e.tensor_tensor(
                    d, f[:], s, OP.subtract), "sV")
            else:
                op("vector", lambda e, f=fb: e.tensor_copy(vfs[0][:], f[:]))
                vspl = op("vector", lambda e, f=fb: e.tensor_tensor(
                    vfs[1][:], f[:], vfs[0][:], OP.subtract), "sV")
            split_prev[name] = vspl

            if name in ("q", "k"):
                wait("scalar", "sV", vspl)
                if bound_state["mm"] is not None:
                    wait("scalar", "sT", bound_state["mm"])
                vq = op("scalar", lambda e, s=hi_d: e.activation(qsqb[:], s, AF.Square), "sA")
                wait("tensor", "sA", vq)
                if bound_state["read"] is not None:
                    wait("tensor", "sV", bound_state["read"])
                vmm2 = op("tensor", lambda e: e.matmul(ps4[0:1, 0:512], c_onesc[:], qsqb[:],
                                                       start=True, stop=True), "sT")
                bound_state["mm"] = vmm2
                wait("vector", "sT", vmm2)
                bslot = bnd[0:1, 0:1] if name == "q" else bnd[0:1, 1:2]
                op("vector", lambda e: e.reduce_max(bnd[0:1, 4:5], ps4[0:1, 0:512], AX.X))
                bound_state["read"] = op("vector", lambda e, b=bslot: e.tensor_max(b, b, bnd[0:1, 4:5]), "sV")
            else:
                def vtail(tg=tg, vspl=vspl):
                    wait("tensor", "sV", vspl)
                    for j in range(4):
                        kt = 4 * tg + j
                        prev = vtev_done.get("last")
                        if prev is not None:
                            wait("tensor", "sV", prev)
                        vt0 = op("tensor", lambda e, j=j: e.transpose(
                            psTb[:, 0:128], vfs[0][:, 128 * j:128 * (j + 1)], c_identb[:]), "sT")
                        vt1 = op("tensor", lambda e, j=j: e.transpose(
                            psTb[:, 128:256], vfs[1][:, 128 * j:128 * (j + 1)], c_identb[:]), "sT")
                        wait("vector", "sT", vt1)
                        op("vector", lambda e, kt=kt: e.tensor_copy(
                            vaug[:, 257 * kt + 1:257 * kt + 129], psTb[:, 0:128]))
                        vtev_done["last"] = op("vector", lambda e, kt=kt: e.tensor_copy(
                            vaug[:, 257 * kt + 129:257 * kt + 257], psTb[:, 128:256]), "sV")
                qkv_tail.append(vtail)
            if name != "v" and qkv_tail:
                qkv_tail.pop(0)()

    def flush_qkv_tails():
        while qkv_tail:
            qkv_tail.pop(0)()

    a_done = {}
    split_done = {}
    stage_tok = {}
    evac_done = {}
    pend_qkv = []
    emit_w_section(0)
    for i in range(NTB):
        par = i % 2
        g = i // 4
        if _P1_BARRIER:
            barrier()
        if i >= 2:
            wait("sync", "sA", a_done[i - 2])
        tx = dma("sync", xt[par][:], xb[128 * i:128 * (i + 1), :])
        wait_dma("vector", tx)
        st_, b1_, vb = ln_stats(xt[par][:], 8 * par, double=True)
        wait("scalar", "sV", vb)
        if _DEBUG_TAPS:
            wait("sync", "sV", vb)
            dma("sync", dbg["st1"][i][:, 0:2], mvbuf[:, 2 * par:2 * par + 2])
            dma("sync", dbg["st1"][i][:, 2:7], stat[:, 8 * par:8 * par + 5], allow_slow_non_contiguous=True)
        a_done[i] = op("scalar", lambda e, p=par, s=st_, b=b1_: e.activation(
            ynorm[:], xt[p][:], AF.Identity, bias=b, scale=s), "sA")
        tpar = i % 2
        if i >= 2:
            wait("scalar", "sV", split_done[i - 2])
        if i % 4 == 0 and g >= 2:
            wait("vector", "sT", xnb_consumed[g % 2])
        for grp in range(4):
            bank = psT32 if grp % 2 == 0 else ps5
            if grp == 0:
                wait("tensor", "sA", a_done[i])
            if grp < 2 and i > 0:
                wait("tensor", "sA", evac_done[(i - 1, 2 + grp)])
            if grp >= 2:
                wait("tensor", "sA", evac_done[(i, grp - 2)])
            vtr = None
            for j in range(4):
                ch = 4 * grp + j
                vtr = op("tensor", lambda e, c=ch, j=j, b=bank: e.transpose(
                    b[:, 128 * j:128 * (j + 1)], ynorm[:, 128 * c:128 * (c + 1)], c_identf[:]),
                    "sT" if j == 3 else None)
            wait("scalar", "sT", vtr)
            evac_done[(i, grp)] = op("scalar", lambda e, g2=grp, b=bank, p=tpar: e.activation(
                xnf32[p][:, 512 * g2:512 * (g2 + 1)], b[:, 0:512], AF.Identity), "sA")
        wait("vector", "sA", evac_done[(i, 3)])
        if g > 1 and i % 4 == 0:
            wait_dma("vector", stage_tok[g - 2][0])
            wait_dma("vector", stage_tok[g - 2][1])
        dh = xnb_hi[g % 2][:].rearrange("p (c f) -> p c f", c=NCH)[:, :, 128 * (i % 4):128 * (i % 4) + 128]
        dl = xnb_lo[g % 2][:].rearrange("p (c f) -> p c f", c=NCH)[:, :, 128 * (i % 4):128 * (i % 4) + 128]
        op("vector", lambda e, d=dh, p=tpar: e.tensor_copy(d, xnf32[p][:]))
        split_done[i] = op("vector", lambda e, d=dl, sh=dh, p=tpar: e.tensor_tensor(
            d, xnf32[p][:], sh, OP.subtract), "sV")
        if i % 4 == 3:
            wait("sync", "sV", split_done[i])
            th = dma("sync", xnf_hi[g], xnb_hi[g % 2][:])
            tl = dma("sync", xnf_lo[g], xnb_lo[g % 2][:])
            stage_tok[g] = (th, tl)
            if _DEBUG_TAPS:
                dma("sync", dbg["xnf_hi"][g], xnb_hi[g % 2][:])
                dma("sync", dbg["xnf_lo"][g], xnb_lo[g % 2][:])
            for wi_ in range(3):
                pend_qkv.append((g, g % 2, wi_, split_done[i] if wi_ == 0 else None))
        if pend_qkv:
            g_, xp_, wi_, sg_ = pend_qkv.pop(0)
            emit_qkv_proj(0, g_, xnb_hi[xp_], xnb_lo[xp_], wi_, split_gate=sg_)
    while pend_qkv:
        g_, xp_, wi_, sg_ = pend_qkv.pop(0)
        emit_qkv_proj(0, g_, xnb_hi[xp_], xnb_lo[xp_], wi_, split_gate=sg_)
    flush_qkv_tails()

    flush(block)
    barrier()
    flush(block)

    def early_out(*stacks):
        ob = stacks[0].enter_context(nc.sbuf_tensor("early_outb", [128, D], F32))
        vz = op("vector", lambda e: e.memset(ob[:], 0.0), "sV")
        wait("sync", "sV", vz)
        toks = []
        for bb in range(B):
            for tb in range(4):
                toks.append(dma("sync", out[bb][128 * tb:128 * (tb + 1), :], ob[:]))
        wait_all_dma("gpsimd")
        flush(block)
        for st_ in stacks:
            st_.close()
        ctx.close()
        return nc

    if _PHASE_LIMIT == 1:
        return early_out(ctxA)

    for h in range(NHL):
        if h >= 1:
            if qkv_mm_end[0] is not None:
                wait("vector", "sT", qkv_mm_end[0])
            if attn_end[0] is not None:
                wait("vector", "sT", attn_end[0])
            emit_w_section(h)
            bound_state["read"] = psav_free[0] if h >= 2 else None
            for tg in range(_NG_LIMIT):
                xpar = tg % 2
                xh_b, xl_b = xnb_hi[xpar], xnb_lo[xpar]
                if xnb_consumed[xpar] is not None:
                    wait("sync", "sT", xnb_consumed[xpar])
                if h == 1:
                    wait_dma("sync", stage_tok[tg][0])
                    wait_dma("sync", stage_tok[tg][1])
                th = dma("sync", xh_b[:], xnf_hi[tg])
                tl = dma("sync", xl_b[:], xnf_lo[tg])
                wait_dma("tensor", th)
                wait_dma("tensor", tl)
                emit_qkv_group(h, tg, xh_b, xl_b)
            flush_qkv_tails()
        qkv_mm_end[0] = cnt["sT"]
        wait("vector", "sA", cnt["sA"])
        if _SKIP_25 or _SKIP_BOUND:
            vneg = op("vector", lambda e: e.memset(negmb[:], -1.0), "sV")
        if not (_SKIP_25 or _SKIP_BOUND):
            vbn = op("vector", lambda e: e.tensor_mul(bnd[0:1, 2:3], bnd[0:1, 0:1], bnd[0:1, 1:2]), "sV")
            wait("scalar", "sV", vbn)
            vsq = op("scalar", lambda e: e.activation(bnd[0:1, 3:4], bnd[0:1, 2:3], AF.Sqrt), "sA")
            wait("vector", "sA", vsq)
            vnm = op("vector", lambda e: e.tensor_scalar(negm11[:], bnd[0:1, 3:4], -1.03, -0.01, OP.mult, OP.add), "sV")
            wait("tensor", "sV", vnm)
            vbm = op("tensor", lambda e: e.matmul(ps3[0:128, 0:1], c_onesf[:], negm11[:], start=True, stop=True), "sT")
            wait("vector", "sT", vbm)
            vneg = op("vector", lambda e: e.tensor_copy(negmb[:], ps3[0:128, 0:1]), "sV")

        if _PHASE_LIMIT == 1.3:
            flush(block); barrier(); flush(block)
            return early_out(ctxA)

        psAV = [ps2, ps3, ps4, ps5]
        wait("tensor", "sA", cnt["sA"])
        wait("tensor", "sV", cnt["sV"])
        wait("scalar", "sV", vneg)

        exp_done = {}
        split3 = {}
        avmm_done = {}
        psav_free = [None]
        kseq = 0

        def emit_epi_tr(qb):
            eh, eqg, vvq = pend_epi[0]
            wait("tensor", "sV", vvq[qb])
            if epi_ev_prev[0] is not None:
                wait("tensor", "sV", epi_ev_prev[0])
            op("tensor", lambda e, t=avbh[qb]: e.transpose(psTb[:, 0:128], t[:], c_identb[:]))
            vt1 = op("tensor", lambda e, t=avbl[qb]: e.transpose(psTb[:, 128:256], t[:], c_identb[:]), "sT")
            epi_wait_tr[qb] = vt1
            wait("vector", "sT", vt1)
            if qb == 0 and stg_tok[0] is not None:
                wait_dma("vector", stg_tok[0])
            op("vector", lambda e, qb=qb: e.tensor_copy(
                avfst[:, 128 * qb:128 * (qb + 1)], psTb[:, 0:128]))
            epi_ev_prev[0] = op("vector", lambda e, qb=qb: e.tensor_copy(
                avfst[:, 512 + 128 * qb:512 + 128 * (qb + 1)], psTb[:, 128:256]), "sV")
            if qb == 3:
                wait("sync", "sV", epi_ev_prev[0])
                stg_tok[0] = dma("sync", a2a_in[eh].ap()[eqg], avfst[:])
                if _DEBUG_TAPS and eh == 0:
                    dma("sync", dbg["av0"][eqg], avfst[:])
                pend_epi[0] = None

        sc_done = {}

        def emit_scores_T(qg, kt, kseq):
            psS = (ps0, ps1, psT32)[kseq % 3]
            diag_j = kt - 4 * qg if kt >= 4 * qg else -1
            off = 128 * diag_j if diag_j > 0 else 0
            if kseq >= 3:
                wait("tensor", "sA", exp_done[kseq - 3])
            kh = Khi[:, 128 * kt:128 * (kt + 1)]
            kl = Klo[:, 128 * kt:128 * (kt + 1)]
            qh_ = Qhi[:, 512 * qg + off:512 * (qg + 1)]
            ql_ = Qlo[:, 512 * qg + off:512 * (qg + 1)]
            pslc = psS[:, off:512]
            op("tensor", lambda e, p=pslc, a=kh, b=qh_: e.matmul(p, a, b, start=True, stop=False))
            op("tensor", lambda e, p=pslc, a=kh, b=ql_: e.matmul(p, a, b, start=False, stop=False))
            sc_done[kseq] = op("tensor", lambda e, p=pslc, a=kl, b=qh_: e.matmul(p, a, b, start=False, stop=True), "sT")

        def emit_softmax_VS(qg, kt, kseq):
            par = kseq % 2
            psS = (ps0, ps1, psT32)[kseq % 3]
            diag_j = kt - 4 * qg if kt >= 4 * qg else -1
            off = 128 * diag_j if diag_j > 0 else 0
            vsc_ = sc_done[kseq]
            if diag_j >= 0:
                wait("vector", "sT", vsc_)
                dslc = psS[:, 128 * diag_j:128 * (diag_j + 1)]
                vmk = op("vector", lambda e, d=dslc: e.tensor_tensor(d, d, c_dmask[:], OP.add), "sV")
                wait("scalar", "sV", vmk)
            else:
                wait("scalar", "sT", vsc_)
            if kseq >= 2:
                wait("scalar", "sV", split3[kseq - 2])
                wait("scalar", "sT", avmm_done[kseq - 2])
            ncols = 512 - off
            op("scalar", lambda e, p=psS[:, off:512], b=p32b[par], o=off, n=ncols: e.activation(
                b[:, o:o + n], p, AF.Exp, bias=negmb[:, 0:1]))
            exp_done[kseq] = op("scalar", lambda e, p=psS[:, off:512], b=phib[par], o=off, n=ncols: e.activation(
                b[:, o:o + n], p, AF.Exp, bias=negmb[:, 0:1]), "sA")
            wait("vector", "sA", exp_done[kseq])
            split3[kseq] = op("vector", lambda e, b=p32b[par], s=phib[par], d=plob[par], o=off, n=ncols: e.tensor_tensor(
                d[:, o:o + n], b[:, o:o + n], s[:, o:o + n], OP.subtract), "sV")

        def emit_av(qg, kt, kseq, nkt):
            par = kseq % 2
            diag_j = kt - 4 * qg if kt >= 4 * qg else -1
            if kt == 0 and psav_free[0] is not None:
                wait("tensor", "sV", psav_free[0])
            wait("tensor", "sV", split3[kseq])
            jmin = diag_j if diag_j >= 0 else 0
            for qb in range(jmin, 4):
                qb_last = (kt == 4 * qg + qb)
                phslc = phib[par][:, 128 * qb:128 * (qb + 1)]
                plslc = plob[par][:, 128 * qb:128 * (qb + 1)]
                vslc = vaug[:, 257 * kt:257 * kt + 257]
                vslc129 = vaug[:, 257 * kt:257 * kt + 129]
                if qb_last and kt == 0:
                    op("tensor", lambda e, p=psAV[qb], a=phslc, v=vslc: e.matmul(
                        p[:, 0:257], a, v, start=True, stop=False))
                    op("tensor", lambda e, p=psAV[qb], a=plslc, v=vslc129: e.matmul(
                        p[:, 0:129], a, v, start=False, stop=False))
                    if qb == 3:
                        avmm_done[kseq] = op("tensor", lambda e, p=psAV[qb], v=vslc: e.matmul(
                            p[:, 0:257], c_zerob[:], v, start=False, stop=True), "sT")
                    else:
                        op("tensor", lambda e, p=psAV[qb], v=vslc: e.matmul(
                            p[:, 0:257], c_zerob[:], v, start=False, stop=True))
                elif qb_last:
                    op("tensor", lambda e, p=psAV[qb], a=plslc, v=vslc129: e.matmul(
                        p[:, 0:129], a, v, start=False, stop=False))
                    if qb == 3:
                        avmm_done[kseq] = op("tensor", lambda e, p=psAV[qb], a=phslc, v=vslc: e.matmul(
                            p[:, 0:257], a, v, start=False, stop=True), "sT")
                    else:
                        op("tensor", lambda e, p=psAV[qb], a=phslc, v=vslc: e.matmul(
                            p[:, 0:257], a, v, start=False, stop=True))
                else:
                    op("tensor", lambda e, p=psAV[qb], a=phslc, v=vslc, st=(kt == 0): e.matmul(
                        p[:, 0:257], a, v, start=st, stop=False))
                    if qb == 3:
                        avmm_done[kseq] = op("tensor", lambda e, p=psAV[qb], a=plslc, v=vslc129: e.matmul(
                            p[:, 0:129], a, v, start=False, stop=False), "sT")
                    else:
                        op("tensor", lambda e, p=psAV[qb], a=plslc, v=vslc129: e.matmul(
                            p[:, 0:129], a, v, start=False, stop=False))

        for qg in range(NG):
            nkt = 4 * qg + 4
            base = kseq
            for kt in range(nkt):
                emit_scores_T(qg, kt, base + kt)
                if pend_epi[0] is not None and 1 <= kt <= 4:
                    emit_epi_tr(kt - 1)
                if kt >= 2:
                    emit_av(qg, kt - 2, base + kt - 2, nkt)
                emit_softmax_VS(qg, kt, base + kt)
            emit_av(qg, nkt - 2, base + nkt - 2, nkt)
            emit_av(qg, nkt - 1, base + nkt - 1, nkt)
            kseq = base + nkt

            wait("vector", "sT", avmm_done[kseq - 1])
            epi_psav = None
            vvq = [None] * 4
            for qb in range(4):
                if epi_wait_tr[qb] is not None:
                    wait("vector", "sT", epi_wait_tr[qb])
                op("vector", lambda e, p=psAV[qb], r=rsumc[qb % 2]: e.reciprocal(r[:], p[:, 0:1]))
                op("vector", lambda e, p=psAV[qb], t=avt32[qb]: e.tensor_copy(t[:], p[:, 1:129]))
                epi_psav = op("vector", lambda e, p=psAV[qb], t=avt32[qb]: e.tensor_tensor(
                    t[:], t[:], p[:, 129:257], OP.add), "sV")
                op("vector", lambda e, t=avt32[qb], r=rsumc[qb % 2]: e.tensor_scalar(
                    t[:], t[:], r[:][:, 0:1], None, OP.mult))
                op("vector", lambda e, t=avt32[qb], d=avbh[qb]: e.tensor_copy(d[:], t[:]))
                vvq[qb] = op("vector", lambda e, t=avt32[qb], hh=avbh[qb], d=avbl[qb]: e.tensor_tensor(
                    d[:], t[:], hh[:], OP.subtract), "sV")
            psav_free[0] = epi_psav
            pend_epi[0] = (h, qg, vvq)

        if _PHASE_LIMIT == 1.6:
            flush(block); barrier(); flush(block)
            return early_out(ctxA)

        for qb in range(4):
            emit_epi_tr(qb)

        wait_all_dma("gpsimd")
        def mk_cc(l):
            def go(e):
                with allow_ici_all_to_all():
                    return e.collective_compute(
                        "AllToAll", OP.bypass,
                        replica_groups=[list(range(NCORES))],
                        ins=[a2a_in[l].ap().opt()],
                        outs=[a2a_out[l].ap().opt()])
            return go
        op("gpsimd", mk_cc(h), "sC")
        attn_end[0] = cnt["sT"]
        psav_free_w[0] = psav_free[0]
        vtev_done["last"] = epi_ev_prev[0]

        flush(block)
        if _PHASE_LIMIT == 2:
            break

    if _PHASE_LIMIT == 2:
        return early_out(ctxA)

    ctxA.close()
    ctxB0 = ExitStack()
    x1slab = ctxB0.enter_context(nc.sbuf_tensor("x1slab", [128, 8 * D], F32))
    stln = ctxB0.enter_context(nc.sbuf_tensor("stln", [128, 16], F32))
    ctxB1 = ExitStack()
    sbB = lambda name, shape, dt: ctxB1.enter_context(nc.sbuf_tensor(name, shape, dt))
    avrhi = sbB("avrhi", [128, NCH * 512], BF16)
    avrlo = sbB("avrlo", [128, NCH * 512], BF16)
    wo32 = [sbB("wo32a", [128, 4 * 512], F32)]
    wohi = sbB("wohi", [128, NCH * 512], BF16)
    wolo = sbB("wolo", [128, NCH * 512], BF16)
    ysball = sbB("ysball", [128, 4 * D], F32)
    bo_bc = sbB("bo_bc", [128, D], F32)
    bosb = sbB("bosb", [1, D], F32)
    ynorm2 = sbB("ynorm2", [128, D], F32)

    barrier()

    tbo = dma("sync", bosb[:], bo.unsqueeze(0))
    wait_dma("tensor", tbo)
    vbcp = None
    for og in range(4):
        if vbcp is not None:
            wait("tensor", "sV", vbcp)
        vbm = op("tensor", lambda e, og=og: e.matmul(
            ps1[:, 0:512], c_onesf[:], bosb[0:1, 512 * og:512 * (og + 1)], start=True, stop=True), "sT")
        wait("vector", "sT", vbm)
        vbcp = op("vector", lambda e, og=og: e.tensor_copy(bo_bc[:, 512 * og:512 * (og + 1)], ps1[:, 0:512]), "sV")

    x1_toks = [None] * 8
    for bb in range(B):
        for tb in range(4):
            slab = 4 * bb + tb
            x1_toks[slab] = dma("gpsimd", x1slab[:, D * slab:D * (slab + 1)],
                                xres[bb][128 * tb:128 * (tb + 1), :])

    def emit_avr(eng_name, bb):
        toks = []
        for l in range(NHL):
            dsth = avrhi[:].rearrange("p (c f) -> p c f", c=NCH)[:, l::4, :]
            dstl = avrlo[:].rearrange("p (c f) -> p c f", c=NCH)[:, l::4, :]
            srcs = a2a_out[l].ap()[4 * bb:4 * bb + 4].transpose([1, 0, 2])
            toks.append(dma(eng_name, dsth, srcs[:, :, 0:512]))
            toks.append(dma(eng_name, dstl, srcs[:, :, 512:1024]))
        return toks

    banks = [ps0, ps1, ps2, psT32]
    mm_mark = {}
    split_mark = {}
    evac_mark = [None]
    avr_toks = [None, None]

    def emit_tail(bb, tb):
        yslc = ysball[:, D * tb:D * (tb + 1)]
        st_, b1_, vb = ln_stats(yslc, 0, double=False)
        wait("scalar", "sV", vb)
        vn = op("scalar", lambda e, y=yslc, s=st_, b=b1_: e.activation(
            ynorm2[:], y, AF.Identity, bias=b, scale=s), "sA")
        slab = 4 * bb + tb
        wait("vector", "sA", vn)
        wait_dma("vector", x1_toks[slab])
        op("vector", lambda e, s2=slab: e.tensor_tensor(
            x1slab[:, D * s2:D * (s2 + 1)], x1slab[:, D * s2:D * (s2 + 1)], ynorm2[:], OP.add), "sV")
        st2_, b2_, _vb2 = ln_stats(x1slab[:, D * slab:D * (slab + 1)], 0, double=False)
        op("vector", lambda e, s2=slab, s=st2_: e.tensor_copy(
            stln[:, 2 * s2:2 * s2 + 2], stat[:, 3:5]), "sV")
        if _DEBUG_TAPS:
            dma("sync", dbg["x1"][slab], x1slab[:, D * slab:D * (slab + 1)])

    for s in range(32):
        bb, og, qtr = s // 16, (s % 16) // 4, s % 4
        if s > 0:
            wait("sync", "sV", split_mark[s - 1])
        if s == 2:
            wait("sync", "sC", cnt["sC"])
            avr_toks[0] = emit_avr("sync", 0)
        tl_ = dma("sync", wo32[0][:].rearrange("p (c f) -> p c f", c=4),
                  wo.rearrange("(c p) o -> p c o", p=128)[:, qtr::4,
                                                         512 * og:512 * (og + 1)])
        if s >= 4:
            wait("vector", "sT", mm_mark[s - 4])
        wait_dma("vector", tl_)
        op("vector", lambda e, q=qtr: e.tensor_copy(
            wohi[:, 2048 * q:2048 * (q + 1)], wo32[0][:]))
        split_mark[s] = op("vector", lambda e, q=qtr: e.tensor_tensor(
            wolo[:, 2048 * q:2048 * (q + 1)], wo32[0][:], wohi[:, 2048 * q:2048 * (q + 1)], OP.subtract), "sV")
        wait("tensor", "sV", split_mark[s])
        if s == 0:
            wait("tensor", "sV", vbcp)
        if og == 0:
            if bb == 0:
                for j in (2 * qtr, 2 * qtr + 1):
                    emit("tensor", lambda e, j=j: e.wait_ge(
                        qG if avr_toks[0][j][0] == "G" else qD[avr_toks[0][j][0]],
                        avr_toks[0][j][1]))
            else:
                for j in (2 * qtr, 2 * qtr + 1):
                    wait_dma("tensor", avr_toks[1][j])
        if qtr == 0 and evac_mark[0] is not None:
            wait("tensor", "sV", evac_mark[0])
        for tb in range(4):
            pso = banks[tb]
            for k in range(4):
                chp = 4 * k + qtr
                slot = 4 * qtr + k
                ah = avrhi[:, 512 * chp + 128 * tb:512 * chp + 128 * (tb + 1)]
                al = avrlo[:, 512 * chp + 128 * tb:512 * chp + 128 * (tb + 1)]
                wh = wohi[:, 512 * slot:512 * (slot + 1)]
                wl = wolo[:, 512 * slot:512 * (slot + 1)]
                op("tensor", lambda e, p=pso, a=ah, w=wh, st=(qtr == 0 and k == 0): e.matmul(
                    p[:], a, w, start=st, stop=False))
                op("tensor", lambda e, p=pso, a=ah, w=wl: e.matmul(p[:], a, w, start=False, stop=False))
                stp = (qtr == 3 and k == 3)
                if k == 3 and tb == 3:
                    mm_mark[s] = op("tensor", lambda e, p=pso, a=al, w=wh, stp=stp: e.matmul(
                        p[:], a, w, start=False, stop=stp), "sT")
                else:
                    op("tensor", lambda e, p=pso, a=al, w=wh, stp=stp: e.matmul(
                        p[:], a, w, start=False, stop=stp))
        if bb == 1 and og == 0:
            emit_tail(0, qtr)
        if qtr == 3:
            wait("vector", "sT", mm_mark[s])
            for tb in range(4):
                evac_mark[0] = op("vector", lambda e, p=banks[tb], tb=tb, og=og: e.scalar_tensor_tensor(
                    ysball[:, D * tb + 512 * og:D * tb + 512 * (og + 1)], p[:], 1.0,
                    bo_bc[:, 512 * og:512 * (og + 1)], OP.mult, OP.add), "sV")
            if og == 3:
                if _DEBUG_TAPS:
                    wait("sync", "sV", cnt["sV"])
                    dma("sync", dbg["ysb"][bb], ysball[:])
                    if bb == 0:
                        for l in range(NHL):
                            for si in range(4):
                                dma("sync", dbg["rcv"][4 * si + l], a2a_out[l].ap()[si])
                if bb == 0:
                    toks = []
                    for l in range(NHL):
                        wait("gpsimd", "sT", mm_mark[12 + l])
                        dsth = avrhi[:].rearrange("p (c f) -> p c f", c=NCH)[:, l::4, :]
                        dstl = avrlo[:].rearrange("p (c f) -> p c f", c=NCH)[:, l::4, :]
                        srcs = a2a_out[l].ap()[4:8].transpose([1, 0, 2])
                        toks.append(dma("gpsimd", dsth, srcs[:, :, 0:512]))
                        toks.append(dma("gpsimd", dstl, srcs[:, :, 512:1024]))
                    avr_toks[1] = toks
                else:
                    for tb in range(4):
                        emit_tail(1, tb)

    flush(block)
    barrier()
    flush(block)

    if _PHASE_LIMIT == 3:
        return early_out(ctxB1, ctxB0)

    ctxB1.close()
    ctxB2 = ExitStack()
    sb2 = lambda name, shape, dt: ctxB2.enter_context(nc.sbuf_tensor(name, shape, dt))
    w8row = sb2("w8row", [128, D], F8)
    w8rb = sb2("w8rb", [128, D], BF16)
    w8T = sb2("w8T", [128, NCH * D], BF16)
    x2nb = sb2("x2nb", [128, D], F32)
    tqb = sb2("tqb", [128, D], F32)
    q8b = sb2("q8b", [128, D], U8)
    qbb = sb2("qbb", [128, D], BF16)
    xqf = sb2("xqf", [128, NCH * 128], BF16)
    xqf2 = sb2("xqf2", [128, NCH * 128], BF16)
    ysb2 = sb2("ysb2", [128, D], F32)
    outb = sb2("outb", [128, D], F32)
    psT8 = ctxB2.enter_context(nc.psum_tensor("psT8", [128, 1024], BF16))
    psT8b = ctxB2.enter_context(nc.psum_tensor("psT8b", [128, 1024], BF16))

    xq_bufs = [xqf, xqf2]
    t8_ev = [None, None]
    q_mark = {}
    vtt_last = {}
    xq_ev = {}
    mm_mark2 = {}
    ffn_evac = [None, None]
    tail_vb = {}
    tod_tok = {}

    def emit_qdq_v(slab):
        x1s = x1slab[:, D * slab:D * (slab + 1)]
        if slab > 0:
            wait("scalar", "sV", q_mark[slab - 1])
        vn = op("scalar", lambda e, y=x1s, s=stln[:, 2 * slab:2 * slab + 1],
                b=stln[:, 2 * slab + 1:2 * slab + 2]: e.activation(
            x2nb[:], y, AF.Identity, bias=b, scale=s), "sA")
        wait("vector", "sA", vn)
        op("vector", lambda e: e.tensor_scalar(tqb[:], x2nb[:], invsb[:, 0:1], 448.0, OP.mult, OP.min))
        if slab > 0:
            wait("vector", "sT", vtt_last[slab - 1])
        op("vector", lambda e: e.tensor_scalar(q8b[:].bitcast(F8), tqb[:], -448.0, None, OP.max))
        q_mark[slab] = op("vector", lambda e: e.tensor_copy(qbb[:], q8b[:].bitcast(F8)), "sV")

    emit_qdq_v(0)

    w8T_done = None
    prev_vup = None
    prev_vtt = None
    for oc in range(NCH):
        if prev_vup is not None:
            wait("sync", "sV", prev_vup)
        tw = dma("sync", w8row[:], w8[128 * oc:128 * (oc + 1), :])
        wait_dma("vector", tw)
        if prev_vtt is not None:
            wait("vector", "sT", prev_vtt)
        prev_vup = vup = op("vector", lambda e: e.tensor_copy(w8rb[:], w8row[:]), "sV")
        wait("tensor", "sV", vup)
        for icq in range(4):
            bank = (psT8, psT8b)[icq % 2]
            if t8_ev[icq % 2] is not None:
                wait("tensor", "sV", t8_ev[icq % 2])
            vtt = None
            for j in range(4):
                ic = 4 * icq + j
                vtt = op("tensor", lambda e, ic=ic, j=j, b=bank: e.transpose(
                    b[:, 128 * j:128 * (j + 1)], w8rb[:, 128 * ic:128 * (ic + 1)], c_identb[:]),
                    "sT" if j == 3 else None)
            wait("vector", "sT", vtt)
            t8_ev[icq % 2] = w8T_done = op("vector", lambda e, icq=icq, oc=oc, b=bank: e.tensor_copy(
                w8T[:].rearrange("p (c f) -> p c f", c=NCH)[:, 4 * icq:4 * (icq + 1),
                                                           128 * oc:128 * (oc + 1)],
                b[:, 0:512].rearrange("p (c f) -> p c f", c=4)), "sV")
        prev_vtt = vtt

    def emit_xT(slab):
        par = slab % 2
        if slab >= 2:
            wait("vector", "sT", mm_mark2[(slab - 2, 3)])
        wait("tensor", "sV", q_mark[slab])
        for icq in range(4):
            bank = (psT8, psT8b)[icq % 2]
            if t8_ev[icq % 2] is not None:
                wait("tensor", "sV", t8_ev[icq % 2])
            vtt = None
            for j in range(4):
                ic = 4 * icq + j
                vtt = op("tensor", lambda e, ic=ic, j=j, b=bank: e.transpose(
                    b[:, 128 * j:128 * (j + 1)], qbb[:, 128 * ic:128 * (ic + 1)], c_identb[:]),
                    "sT" if j == 3 else None)
            wait("vector", "sT", vtt)
            t8_ev[icq % 2] = xq_ev[slab] = op("vector", lambda e, icq=icq, b=bank, par=par: e.tensor_copy(
                xq_bufs[par][:, 512 * icq:512 * (icq + 1)], b[:, 0:512]), "sV")
        vtt_last[slab] = vtt

    def emit_mm(slab, og):
        par = slab % 2
        psf = (ps0, ps2)[og % 2]
        if slab == 0 and og == 0:
            wait("tensor", "sV", w8T_done)
        if og == 0:
            wait("tensor", "sV", xq_ev[slab])
        if ffn_evac[og % 2] is not None:
            wait("tensor", "sA", ffn_evac[og % 2])
        vmf = None
        for ic in range(NCH):
            if ic < NCH - 1:
                op("tensor", lambda e, p=psf, ic=ic, og=og, par=par, st=(ic == 0): e.matmul(
                    p[:], xq_bufs[par][:, 128 * ic:128 * (ic + 1)],
                    w8T[:, D * ic + 512 * og:D * ic + 512 * (og + 1)], start=st, stop=False))
            else:
                vmf = op("tensor", lambda e, p=psf, ic=ic, og=og, par=par: e.matmul(
                    p[:], xq_bufs[par][:, 128 * ic:128 * (ic + 1)],
                    w8T[:, D * ic + 512 * og:D * ic + 512 * (og + 1)], start=False, stop=True), "sT")
        mm_mark2[(slab, og)] = vmf
        wait("scalar", "sT", vmf)
        if og == 0 and slab >= 1:
            wait("scalar", "sV", tail_vb[slab - 1])
        ffn_evac[og % 2] = op("scalar", lambda e, p=psf, og=og: e.activation(
            ysb2[:, 512 * og:512 * (og + 1)], p[:], AF.Identity, scale=yscb[:, 0:1]), "sA")

    def emit_tail2(slab):
        bb, tb = slab // 4, slab % 4
        x1s = x1slab[:, D * slab:D * (slab + 1)]
        wait("vector", "sA", ffn_evac[1])
        st_, b1_, vb = ln_stats(ysb2[:], 8, double=False)
        tail_vb[slab] = vb
        wait("scalar", "sV", vb)
        if slab >= 1:
            wait_dma("scalar", tod_tok[slab - 1])
        vn2 = op("scalar", lambda e, s=st_, b=b1_: e.activation(
            outb[:], ysb2[:], AF.Identity, bias=b, scale=s), "sA")
        wait("vector", "sA", vn2)
        vout = op("vector", lambda e, y=x1s: e.tensor_tensor(outb[:], outb[:], y, OP.add), "sV")
        wait("sync", "sV", vout)
        tod_tok[slab] = dma("sync", out[bb][128 * tb:128 * (tb + 1), :], outb[:])

    for it in range(9):
        if 0 < it < 8:
            emit_qdq_v(it)
        if it > 0:
            emit_mm(it - 1, 0)
            emit_mm(it - 1, 1)
        if it < 8:
            emit_xT(it)
        if it > 0:
            emit_mm(it - 1, 2)
            emit_mm(it - 1, 3)
            emit_tail2(it - 1)

    wait_all_dma("gpsimd")
    flush(block)

    ctxB2.close()
    ctxB0.close()
    ctx.close()
    return nc


_NC_CACHE = None


def kernel(L0_attn_X, L0_attn_ln_g, L0_attn_ln_b, L0_attn_Wq, L0_attn_bq,
           L0_attn_Wk, L0_attn_bk, L0_attn_Wv, L0_attn_bv, L0_attn_Wo,
           L0_attn_bo, L0_ffn_weight, L0_ffn_weight_scale, L0_ffn_input_scale):
    global _NC_CACHE
    X = np.ascontiguousarray(L0_attn_X, dtype=np.float32)
    Wq = np.ascontiguousarray(L0_attn_Wq, dtype=np.float32)
    Wk = np.ascontiguousarray(L0_attn_Wk, dtype=np.float32)
    Wv = np.ascontiguousarray(L0_attn_Wv, dtype=np.float32)
    Wo = np.ascontiguousarray(L0_attn_Wo, dtype=np.float32)
    w8v = np.ascontiguousarray(L0_ffn_weight).view(ml_dtypes.float8_e4m3)
    c32 = np.float32(np.sqrt(np.float32(128.0), dtype=np.float32))

    identf = np.eye(128, dtype=np.float32)
    identb = np.eye(128, dtype=ml_dtypes.bfloat16)
    ident8 = np.eye(128).astype(ml_dtypes.float8_e4m3)
    pp, ff = np.arange(128)[:, None], np.arange(128)[None, :]
    dmaskv = np.where(pp <= ff, np.float32(0.0), np.float32(NEG)).astype(np.float32)
    onesfv = np.ones((1, 128), dtype=np.float32)

    in_maps = []
    for c in range(NCORES):
        b, hg = c // 4, c % 4
        in_maps.append(dict(
            xb=X[b],
            xres=np.ascontiguousarray(X[:, 512 * c:512 * (c + 1), :]),
            wq=np.ascontiguousarray(Wq[:, 512 * hg:512 * (hg + 1)]),
            wk=np.ascontiguousarray(Wk[:, 512 * hg:512 * (hg + 1)]),
            wv=np.ascontiguousarray(Wv[:, 512 * hg:512 * (hg + 1)]),
            bq=np.ascontiguousarray(L0_attn_bq[512 * hg:512 * (hg + 1)]).astype(np.float32),
            bk=np.ascontiguousarray(L0_attn_bk[512 * hg:512 * (hg + 1)]).astype(np.float32),
            bv=np.ascontiguousarray(L0_attn_bv[512 * hg:512 * (hg + 1)]).astype(np.float32),
            wo=Wo,
            bo=np.ascontiguousarray(L0_attn_bo, dtype=np.float32),
            lng=np.ascontiguousarray(L0_attn_ln_g, dtype=np.float32),
            lnb=np.ascontiguousarray(L0_attn_ln_b, dtype=np.float32),
            w8=w8v,
            wsc=np.asarray(L0_ffn_weight_scale, dtype=np.float32).reshape(1, 1),
            isc=np.asarray(L0_ffn_input_scale).reshape(1, 1),
            c32t=np.asarray(c32, dtype=np.float32).reshape(1, 1),
            identf=identf, identb=identb, ident8=ident8,
            dmask=dmaskv, onesf=onesfv,
        ))

    if _NC_CACHE is None:
        _NC_CACHE = build_program()
    res = run_bass_kernel_spmd(_NC_CACHE, in_maps, core_ids=list(range(NCORES)))

    outf = np.empty((B, S, D), dtype=np.float32)
    for c in range(NCORES):
        outf[:, 512 * c:512 * (c + 1), :] = res.results[c]["out"]
    return outf


# version 21
# speedup vs baseline: 1.0550x; 1.0550x over previous
import numpy as np
import ml_dtypes
import concourse.bass as bass
import concourse.mybir as mybir
from concourse.bass_utils import run_bass_kernel_spmd
from concourse.aot_env import allow_ici_all_to_all
from contextlib import ExitStack

F32 = mybir.dt.float32
BF16 = mybir.dt.bfloat16
F8 = mybir.dt.float8e4
U8 = mybir.dt.uint8
AF = mybir.ActivationFunctionType
OP = mybir.AluOpType
AX = mybir.AxisListType

D = 2048
NH = 16
DH = 128
S = 4096
B = 2
NCORES = 8
NHL = 4
NCH = D // 128
NTB = S // 128
NG = S // 512
EPS = 1e-5
NEG = -1.0e30
NDMA = 6

_DEBUG_TAPS = False
_PHASE_LIMIT = 99
_SKIP_BOUND = False
_SKIP_VT = False
_SKIP_25 = False
_NG_LIMIT = NG
_SKIP_PROJ = False
_PROJ_N = 3
_SKIP_EVAC = False
_SKIP_SPLIT = False
_P1_BARRIER = False


def build_program():
    nc = bass.Bass("TRN2", target_bir_lowering=False, detect_race_conditions=False)

    xb = nc.dram_tensor("xb", [S, D], F32, kind="ExternalInput").ap()
    xres = nc.dram_tensor("xres", [B, 512, D], F32, kind="ExternalInput").ap()
    wq = nc.dram_tensor("wq", [D, 512], F32, kind="ExternalInput").ap()
    wk = nc.dram_tensor("wk", [D, 512], F32, kind="ExternalInput").ap()
    wv = nc.dram_tensor("wv", [D, 512], F32, kind="ExternalInput").ap()
    bq = nc.dram_tensor("bq", [512], F32, kind="ExternalInput").ap()
    bk = nc.dram_tensor("bk", [512], F32, kind="ExternalInput").ap()
    bv = nc.dram_tensor("bv", [512], F32, kind="ExternalInput").ap()
    wo = nc.dram_tensor("wo", [D, D], F32, kind="ExternalInput").ap()
    bo = nc.dram_tensor("bo", [D], F32, kind="ExternalInput").ap()
    lng = nc.dram_tensor("lng", [D], F32, kind="ExternalInput").ap()
    lnb = nc.dram_tensor("lnb", [D], F32, kind="ExternalInput").ap()
    w8 = nc.dram_tensor("w8", [D, D], F8, kind="ExternalInput").ap()
    wsc = nc.dram_tensor("wsc", [1, 1], F32, kind="ExternalInput").ap()
    isc = nc.dram_tensor("isc", [1, 1], BF16, kind="ExternalInput").ap()
    c32t = nc.dram_tensor("c32t", [1, 1], F32, kind="ExternalInput").ap()
    identf = nc.dram_tensor("identf", [128, 128], F32, kind="ExternalInput").ap()
    identb = nc.dram_tensor("identb", [128, 128], BF16, kind="ExternalInput").ap()
    ident8 = nc.dram_tensor("ident8", [128, 128], F8, kind="ExternalInput").ap()
    dmask = nc.dram_tensor("dmask", [128, 128], F32, kind="ExternalInput").ap()
    onesf = nc.dram_tensor("onesf", [1, 128], F32, kind="ExternalInput").ap()

    out = nc.dram_tensor("out", [B, 512, D], F32, kind="ExternalOutput").ap()

    xnf_hi = nc.dram_tensor("xnf_hi", [NG, 128, NCH * 512], BF16, kind="Internal").ap()
    xnf_lo = nc.dram_tensor("xnf_lo", [NG, 128, NCH * 512], BF16, kind="Internal").ap()
    a2a_in = [nc.dram_tensor(f"a2a_in{l}", [8, 128, 1024], BF16, kind="Internal")
              for l in range(NHL)]
    a2a_out = [nc.dram_tensor(f"a2a_out{l}", [8, 128, 1024], BF16, kind="Internal")
               for l in range(NHL)]

    dbg = {}
    if _DEBUG_TAPS:
        dbg["xnf_hi"] = nc.dram_tensor("dbg_xnf_hi", [NG, 128, NCH * 512], BF16, kind="ExternalOutput").ap()
        dbg["xnf_lo"] = nc.dram_tensor("dbg_xnf_lo", [NG, 128, NCH * 512], BF16, kind="ExternalOutput").ap()
        dbg["qk"] = nc.dram_tensor("dbg_qk", [4, 128, S], BF16, kind="ExternalOutput").ap()
        dbg["av0"] = nc.dram_tensor("dbg_av0", [8, 128, 1024], BF16, kind="ExternalOutput").ap()
        dbg["x1"] = nc.dram_tensor("dbg_x1", [8, 128, D], F32, kind="ExternalOutput").ap()
        dbg["ysb"] = nc.dram_tensor("dbg_ysb", [B, 128, 4 * D], F32, kind="ExternalOutput").ap()
        dbg["rcv"] = nc.dram_tensor("dbg_rcv", [16, 128, 512], F32, kind="ExternalOutput").ap()
        dbg["st1"] = nc.dram_tensor("dbg_st1", [NTB, 128, 8], F32, kind="ExternalOutput").ap()

    ctx = ExitStack()
    qD = [ctx.enter_context(nc.semaphore(f"qD{i}")) for i in range(NDMA)]
    qG = ctx.enter_context(nc.semaphore("qG"))
    sT = ctx.enter_context(nc.semaphore("sT"))
    sV = ctx.enter_context(nc.semaphore("sV"))
    sA = ctx.enter_context(nc.semaphore("sA"))
    sC = ctx.enter_context(nc.semaphore("sC"))
    esem = dict(sT=sT, sV=sV, sA=sA, sC=sC)
    cnt = dict(sT=0, sV=0, sA=0, sC=0)
    dcnt = [0] * NDMA
    dturn = [0]
    gcnt = [0]

    ps0 = ctx.enter_context(nc.psum_tensor("ps0", [128, 512], F32))
    ps1 = ctx.enter_context(nc.psum_tensor("ps1", [128, 512], F32))
    ps2 = ctx.enter_context(nc.psum_tensor("ps2", [128, 512], F32))
    psT32 = ctx.enter_context(nc.psum_tensor("psT32", [128, 512], F32))

    sb = lambda name, shape, dt: ctx.enter_context(nc.sbuf_tensor(name, shape, dt))
    c_identf = sb("c_identf", [128, 128], F32)
    c_identb = sb("c_identb", [128, 128], BF16)
    c_ident8 = sb("c_ident8", [128, 128], F8)
    c_dmask = sb("c_dmask", [128, 128], F32)
    c_onesf = sb("c_onesf", [1, 128], F32)
    c_onesc = sb("c_onesc", [128, 1], BF16)
    c_zerob = sb("c_zerob", [128, 128], BF16)
    c_lng = sb("c_lng", [128, NCH], F32)
    c_lnb = sb("c_lnb", [128, NCH], F32)
    c_bq = sb("c_bq", [128, NHL], F32)
    c_bk = sb("c_bk", [128, NHL], F32)
    c_bv = sb("c_bv", [128, NHL], F32)
    c_sc = sb("c_sc", [1, 8], F32)
    c_iscb = sb("c_iscb", [1, 1], BF16)
    rcb = sb("rcb", [128, 1], F32)
    invsb = sb("invsb", [128, 1], F32)
    yscb = sb("yscb", [128, 1], F32)
    negm11 = sb("negm11", [1, 1], F32)
    negmb = sb("negmb", [128, 1], F32)
    bnd = sb("bnd", [1, 8], F32)
    stat = sb("stat", [128, 16], F32)
    bnbuf = sb("bnbuf", [128, 48], F32)
    mvbuf = sb("mvbuf", [128, 4], F32)

    phase_ops = []

    def emit(eng, fn):
        phase_ops.append((eng, fn))

    def inc(sem_name, n=1):
        cnt[sem_name] += n
        return cnt[sem_name]

    def wait(eng, sem_name, val):
        assert val <= cnt[sem_name], f"wait {sem_name}>={val}, emitted {cnt[sem_name]}"
        s = esem[sem_name]
        emit(eng, lambda e, s=s, v=val: e.wait_ge(s, v))

    def dma(eng_name, outap, inap, **kw):
        if eng_name == "gpsimd":
            gcnt[0] += 16
            tok = ("G", gcnt[0])
            emit(eng_name, lambda e, o=outap, i=inap, kw=kw: e.dma_start(out=o, in_=i, **kw).then_inc(qG, 16))
            return tok
        i = dturn[0] % NDMA
        dturn[0] += 1
        dcnt[i] += 16
        tok = (i, dcnt[i])
        s = qD[i]
        emit(eng_name, lambda e, o=outap, ia=inap, s=s, kw=kw: e.dma_start(out=o, in_=ia, **kw).then_inc(s, 16))
        return tok

    def wait_dma(eng, tok):
        i, v = tok
        s = qG if i == "G" else qD[i]
        emit(eng, lambda e, s=s, v=v: e.wait_ge(s, v))

    def wait_all_dma(eng):
        for i in range(NDMA):
            if dcnt[i]:
                v = dcnt[i]
                emit(eng, lambda e, s=qD[i], v=v: e.wait_ge(s, v))
        if gcnt[0]:
            v = gcnt[0]
            emit(eng, lambda e, v=v: e.wait_ge(qG, v))

    def op(eng_name, build, sem_name=None, n=1):
        if eng_name in ("vector", "scalar"):
            chain = "sV" if eng_name == "vector" else "sA"
            pre = cnt[chain]
            cs = esem[chain]
            if pre:
                emit(eng_name, lambda e, s=cs, v=pre: e.wait_ge(s, v))
            v = inc(chain)
            emit(eng_name, lambda e, b=build, s=cs: b(e).then_inc(s, 1))
            return v
        if sem_name is None:
            emit(eng_name, lambda e, b=build: b(e))
            return None
        s = esem[sem_name]
        v = inc(sem_name, n)
        emit(eng_name, lambda e, b=build, s=s, n=n: b(e).then_inc(s, n))
        return v

    def flush(block):
        ops = list(phase_ops)
        phase_ops.clear()
        for eng_name in ("sync", "tensor", "vector", "scalar", "gpsimd"):
            mine = [fn for e, fn in ops if e == eng_name]
            if not mine:
                continue
            getattr(block, eng_name)(lambda eng, mine=mine: [fn(eng) for fn in mine])

    def barrier():
        for eng in ("sync", "tensor", "vector", "scalar", "gpsimd"):
            for sname in ("sT", "sV", "sA"):
                if cnt[sname]:
                    wait(eng, sname, cnt[sname])
            wait_all_dma(eng)

    def rsqrt_col(acol, dst, t0, t1):
        va = op("scalar", lambda e, a=acol, d=dst: e.activation(d, a, AF.Sqrt), "sA")
        wait("vector", "sA", va)
        op("vector", lambda e, d=dst: e.reciprocal(d, d))
        for _ in range(1):
            op("vector", lambda e, d=dst, t=t0: e.tensor_mul(t, d, d))
            op("vector", lambda e, a=acol, t=t0, u=t1: e.tensor_mul(u, a, t))
            op("vector", lambda e, u=t1: e.tensor_scalar(u, u, -0.5, 1.5, OP.mult, OP.add))
            op("vector", lambda e, d=dst, u=t1: e.tensor_mul(d, d, u))

    def ln_stats(tile_ap, sc0, double):
        bs = bnbuf[:, 24 * (sc0 // 8):24 * (sc0 // 8) + 24]
        mv = mvbuf[:, 2 * (sc0 // 8):2 * (sc0 // 8) + 2]
        for k in range(4):
            op("vector", lambda e, k=k, t=tile_ap, b=bs: e.bn_stats(
                b[:, 6 * k:6 * k + 6], t[:, 512 * k:512 * (k + 1)]))
        op("vector", lambda e, b=bs, m=mv: e.bn_aggr(m, b))
        a_ = stat[:, sc0 + 0:sc0 + 1]
        t0 = stat[:, sc0 + 1:sc0 + 2]
        t1 = stat[:, sc0 + 2:sc0 + 3]
        st_ = stat[:, sc0 + 3:sc0 + 4]
        b1_ = stat[:, sc0 + 4:sc0 + 5]
        var = mv[:, 1:2]
        if double:
            op("vector", lambda e, v=var, t=t0: e.tensor_scalar(t, v, EPS, EPS * EPS, OP.mult, OP.add))
            va = op("vector", lambda e, v=var, t=t0, a=a_: e.tensor_add(a, v, t), "sV")
        else:
            va = op("vector", lambda e, v=var, a=a_: e.tensor_scalar(a, v, 1.0, EPS, OP.mult, OP.add), "sV")
        wait("scalar", "sV", va)
        rsqrt_col(a_, st_, t0, t1)
        op("vector", lambda e, m=mv, s=st_, b=b1_: e.tensor_mul(b, m[:, 0:1], s))
        vb = op("vector", lambda e, b=b1_: e.tensor_scalar_mul(b, b, -1.0), "sV")
        return st_, b1_, vb

    block = ctx.enter_context(nc.Block())

    dma("sync", c_identf[:], identf)
    dma("sync", c_identb[:], identb)
    dma("sync", c_ident8[:], ident8)
    dma("sync", c_dmask[:], dmask)
    dma("sync", c_onesf[:], onesf)
    dma("sync", c_lng[:], lng.rearrange("(c p) -> p c", p=128), allow_slow_non_contiguous=True)
    dma("sync", c_lnb[:], lnb.rearrange("(c p) -> p c", p=128), allow_slow_non_contiguous=True)
    dma("sync", c_bq[:], bq.rearrange("(h p) -> p h", p=128), allow_slow_non_contiguous=True)
    dma("sync", c_bk[:], bk.rearrange("(h p) -> p h", p=128), allow_slow_non_contiguous=True)
    dma("sync", c_bv[:], bv.rearrange("(h p) -> p h", p=128), allow_slow_non_contiguous=True)
    t1_ = dma("sync", c_sc[:, 0:1], c32t)
    t2_ = dma("sync", c_sc[:, 2:3], wsc)
    t3_ = dma("sync", c_iscb[:], isc)
    for t in (t1_, t2_, t3_):
        wait_dma("vector", t)
    op("vector", lambda e: e.memset(c_onesc[:], 1.0))
    op("vector", lambda e: e.memset(c_zerob[:], 0.0))
    op("vector", lambda e: e.tensor_copy(c_sc[:, 3:4], c_iscb[:]))
    op("vector", lambda e: e.reciprocal(c_sc[:, 1:2], c_sc[:, 0:1]))
    op("vector", lambda e: e.reciprocal(c_sc[:, 4:5], c_sc[:, 3:4]))
    vs = op("vector", lambda e: e.tensor_mul(c_sc[:, 5:6], c_sc[:, 3:4], c_sc[:, 2:3]), "sV")
    wait("tensor", "sV", vs)
    op("tensor", lambda e: e.matmul(ps1[0:128, 0:1], c_onesf[:], c_sc[:, 1:2], start=True, stop=True))
    op("tensor", lambda e: e.matmul(ps1[0:128, 1:2], c_onesf[:], c_sc[:, 4:5], start=True, stop=True))
    vt = op("tensor", lambda e: e.matmul(ps1[0:128, 2:3], c_onesf[:], c_sc[:, 5:6], start=True, stop=True), "sT")
    wait("vector", "sT", vt)
    op("vector", lambda e: e.tensor_copy(rcb[:], ps1[0:128, 0:1]))
    op("vector", lambda e: e.tensor_copy(invsb[:], ps1[0:128, 1:2]))
    vp0 = op("vector", lambda e: e.tensor_copy(yscb[:], ps1[0:128, 2:3]), "sV")

    ctxA = ExitStack()
    sbA = lambda name, shape, dt: ctxA.enter_context(nc.sbuf_tensor(name, shape, dt))
    ps3 = ctxA.enter_context(nc.psum_tensor("ps3", [128, 512], F32))
    ps4 = ctxA.enter_context(nc.psum_tensor("ps4", [128, 512], F32))
    ps5 = ctxA.enter_context(nc.psum_tensor("ps5", [128, 512], F32))
    psTb = ctxA.enter_context(nc.psum_tensor("psTb", [128, 1024], BF16))

    xt = [sbA("xt0", [128, D], F32), sbA("xt1", [128, D], F32)]
    ynorm = sbA("ynorm", [128, D], F32)
    xnf32 = [sbA("xnf32a", [128, D], F32), sbA("xnf32b", [128, D], F32)]
    xnb_hi = [sbA("xnb_hi0", [128, NCH * 512], BF16), sbA("xnb_hi1", [128, NCH * 512], BF16)]
    xnb_lo = [sbA("xnb_lo0", [128, NCH * 512], BF16), sbA("xnb_lo1", [128, NCH * 512], BF16)]
    w32t = ynorm
    wqh = [sbA("wq_hi", [128, D], BF16), sbA("wq_lo", [128, D], BF16)]
    wkh = [sbA("wk_hi", [128, D], BF16), sbA("wk_lo", [128, D], BF16)]
    wvh = [sbA("wv_hi", [128, D], BF16), sbA("wv_lo", [128, D], BF16)]
    qf32 = {"q": sbA("qf32q", [128, 512], F32), "k": sbA("qf32k", [128, 512], F32),
            "v": sbA("qf32v", [128, 512], F32)}
    qsqb = sbA("qsqb", [128, 512], BF16)
    Qhi = sbA("Qhi", [128, S], BF16)
    Qlo = sbA("Qlo", [128, S], BF16)
    Khi = sbA("Khi", [128, S], BF16)
    Klo = sbA("Klo", [128, S], BF16)
    vfs = [sbA("vfs_hi", [128, 512], BF16), sbA("vfs_lo", [128, 512], BF16)]
    vaug = sbA("vaug", [128, NTB * 257], BF16)
    p32b = [sbA("p32b0", [128, 512], F32), sbA("p32b1", [128, 512], F32)]
    phib = [sbA("phib0", [128, 512], BF16), sbA("phib1", [128, 512], BF16)]
    plob = [sbA("plob0", [128, 512], BF16), sbA("plob1", [128, 512], BF16)]
    avt32 = [sbA(f"avt32{q}", [128, 128], F32) for q in range(4)]
    avbh = [sbA(f"avbh{q}", [128, 128], BF16) for q in range(4)]
    avbl = [sbA(f"avbl{q}", [128, 128], BF16) for q in range(4)]
    rsumc = [sbA("rsumc0", [128, 1], F32), sbA("rsumc1", [128, 1], F32)]
    avfst = sbA("avfst", [128, 1024], BF16)

    qkvbias = ctxA.enter_context(nc.sbuf_tensor("qkvbias", [128, 12], F32))
    psav_free_w = [None]
    bias_read = [None]
    qkv_evac = {}
    split_prev = {}
    vtev_done = {}
    xnb_consumed = [None, None]
    stg_tok = [None]
    pend_epi = [None]
    epi_ev_prev = [None]
    epi_wait_tr = [None] * 4
    qkv_mm_end = [None]
    attn_end = [None]
    bound_state = {"read": None, "mm": None}
    qkv_tail = []

    def emit_w_section(h):
        wsplit_last = None
        for wi, (wdram, wsb) in enumerate(((wq, wqh), (wk, wkh), (wv, wvh))):
            tw = dma("sync", w32t[:].rearrange("p (c f) -> p c f", c=NCH),
                     wdram.rearrange("(c p) n -> p c n", p=128)[:, :, 128 * h:128 * (h + 1)])
            wait_dma("tensor", tw)
            if psav_free_w[0] is not None:
                wait("tensor", "sV", psav_free_w[0])
                psav_free_w[0] = None
            if bias_read[0] is not None:
                wait("tensor", "sV", bias_read[0])
            vbm_ = None
            for ch2 in range(NCH):
                vbm_ = op("tensor", lambda e, c=ch2, st=(ch2 == 0), sp=(ch2 == NCH - 1): e.matmul(
                    ps3[0:128, 0:1], w32t[:, 128 * c:128 * (c + 1)], c_lnb[:, c:c + 1],
                    start=st, stop=sp), "sT" if ch2 == NCH - 1 else None)
            wait("vector", "sT", vbm_)
            bcol_d = qkvbias[:, 3 * h + wi:3 * h + wi + 1]
            bsrc = {0: c_bq, 1: c_bk, 2: c_bv}[wi][:, h:h + 1]
            bias_read[0] = op("vector", lambda e, d=bcol_d, b=bsrc: e.tensor_tensor(
                d, ps3[0:128, 0:1], b, OP.add), "sV")
            wait_dma("vector", tw)
            for ch2 in range(NCH):
                op("vector", lambda e, c=ch2: e.tensor_scalar(
                    w32t[:, 128 * c:128 * (c + 1)], w32t[:, 128 * c:128 * (c + 1)],
                    c_lng[:, c:c + 1], None, OP.mult))
            op("vector", lambda e, w=wsb: e.tensor_copy(w[0][:], w32t[:]))
            wsplit_last = op("vector", lambda e, w=wsb: e.tensor_tensor(
                w[1][:], w32t[:], w[0][:], OP.subtract), "sV")
            wait("sync", "sV", wsplit_last)
        op("vector", lambda e: e.memset(bnd[:, 0:2], 0.0))
        for kt in range(NTB - 1):
            op("vector", lambda e, kt=kt: e.memset(vaug[:, 257 * kt:257 * kt + 1], 1.0))
        vones = op("vector", lambda e, kt=NTB - 1: e.memset(vaug[:, 257 * kt:257 * kt + 1], 1.0), "sV")
        wait("tensor", "sV", wsplit_last)
        wait("tensor", "sV", vones)
        bound_state["read"] = None
        bound_state["mm"] = None
        return wsplit_last

    def emit_qkv_group(h, tg, xh_b, xl_b):
        for wi in range(3):
            emit_qkv_proj(h, tg, xh_b, xl_b, wi)

    def emit_qkv_proj(h, tg, xh_b, xl_b, wi, split_gate=None):
        for psx, wsb, name in (((ps0, wqh, "q"), (ps1, wkh, "k"), (ps2, wvh, "v"))[wi],):
            if split_gate is not None:
                wait("tensor", "sV", split_gate)
            if name in qkv_evac:
                wait("tensor", "sA", qkv_evac[name])
            for ch in range(NCH):
                wh = wsb[0][:, 128 * ch:128 * (ch + 1)]
                wl = wsb[1][:, 128 * ch:128 * (ch + 1)]
                xh = xh_b[:, 512 * ch:512 * (ch + 1)]
                xl = xl_b[:, 512 * ch:512 * (ch + 1)]
                op("tensor", lambda e, p=psx, w=wh, x=xh, st=(ch == 0): e.matmul(p[:], w, x, start=st, stop=False))
                op("tensor", lambda e, p=psx, w=wh, x=xl: e.matmul(p[:], w, x, start=False, stop=False))
                if ch < NCH - 1:
                    op("tensor", lambda e, p=psx, w=wl, x=xh: e.matmul(p[:], w, x, start=False, stop=False))
            vmm = op("tensor", lambda e, p=psx, w=wsb[1][:, 128 * (NCH - 1):], x=xh_b[:, 512 * (NCH - 1):]: e.matmul(
                p[:], w, x, start=False, stop=True), "sT")
            xnb_consumed[tg % 2] = vmm

            if qkv_tail:
                qkv_tail.pop(0)()

            wait("scalar", "sT", vmm)
            if name in split_prev:
                wait("scalar", "sV", split_prev[name])
            bcol = qkvbias[:, 3 * h + {"q": 0, "k": 1, "v": 2}[name]:
                           3 * h + {"q": 0, "k": 1, "v": 2}[name] + 1]
            fb = qf32[name]
            ve = op("scalar", lambda e, p=psx, b=bcol, f=fb: e.activation(
                f[:], p[:], AF.Identity, bias=b), "sA")
            qkv_evac[name] = ve
            wait("vector", "sA", ve)
            if name == "q":
                hi_d = Qhi[:, 512 * tg:512 * (tg + 1)]
                lo_d = Qlo[:, 512 * tg:512 * (tg + 1)]
                op("vector", lambda e, d=hi_d, f=fb: e.tensor_scalar(d, f[:], rcb[:, 0:1], None, OP.mult))
                vspl = op("vector", lambda e, d=lo_d, s=hi_d, f=fb: e.scalar_tensor_tensor(
                    d, f[:], rcb[:, 0:1], s, OP.mult, OP.subtract), "sV")
            elif name == "k":
                hi_d = Khi[:, 512 * tg:512 * (tg + 1)]
                lo_d = Klo[:, 512 * tg:512 * (tg + 1)]
                op("vector", lambda e, d=hi_d, f=fb: e.tensor_copy(d, f[:]))
                vspl = op("vector", lambda e, d=lo_d, s=hi_d, f=fb: e.tensor_tensor(
                    d, f[:], s, OP.subtract), "sV")
            else:
                op("vector", lambda e, f=fb: e.tensor_copy(vfs[0][:], f[:]))
                vspl = op("vector", lambda e, f=fb: e.tensor_tensor(
                    vfs[1][:], f[:], vfs[0][:], OP.subtract), "sV")
            split_prev[name] = vspl

            if name in ("q", "k"):
                if bound_state["mm"] is not None:
                    wait("vector", "sT", bound_state["mm"])
                vq = op("vector", lambda e, s=hi_d: e.tensor_mul(qsqb[:], s, s), "sV")
                def btail(vq=vq, name=name):
                    wait("tensor", "sV", vq)
                    if bound_state["read"] is not None:
                        wait("tensor", "sV", bound_state["read"])
                    vmm2 = op("tensor", lambda e: e.matmul(ps4[0:1, 0:512], c_onesc[:], qsqb[:],
                                                           start=True, stop=True), "sT")
                    bound_state["mm"] = vmm2
                    wait("vector", "sT", vmm2)
                    bslot = bnd[0:1, 0:1] if name == "q" else bnd[0:1, 1:2]
                    op("vector", lambda e: e.reduce_max(bnd[0:1, 4:5], ps4[0:1, 0:512], AX.X))
                    bound_state["read"] = op("vector", lambda e, b=bslot: e.tensor_max(b, b, bnd[0:1, 4:5]), "sV")
                qkv_tail.append(btail)
            else:
                def vtail(tg=tg, vspl=vspl):
                    wait("tensor", "sV", vspl)
                    for j in range(4):
                        kt = 4 * tg + j
                        prev = vtev_done.get("last")
                        if prev is not None:
                            wait("tensor", "sV", prev)
                        vt0 = op("tensor", lambda e, j=j: e.transpose(
                            psTb[:, 0:128], vfs[0][:, 128 * j:128 * (j + 1)], c_identb[:]), "sT")
                        vt1 = op("tensor", lambda e, j=j: e.transpose(
                            psTb[:, 128:256], vfs[1][:, 128 * j:128 * (j + 1)], c_identb[:]), "sT")
                        wait("vector", "sT", vt1)
                        op("vector", lambda e, kt=kt: e.tensor_copy(
                            vaug[:, 257 * kt + 1:257 * kt + 129], psTb[:, 0:128]))
                        vtev_done["last"] = op("vector", lambda e, kt=kt: e.tensor_copy(
                            vaug[:, 257 * kt + 129:257 * kt + 257], psTb[:, 128:256]), "sV")
                qkv_tail.append(vtail)

    def flush_qkv_tails():
        while qkv_tail:
            qkv_tail.pop(0)()

    a_done = {}
    split_done = {}
    stage_tok = {}
    evac_done = {}
    pend_qkv = []
    emit_w_section(0)
    for i in range(NTB):
        par = i % 2
        g = i // 4
        if _P1_BARRIER:
            barrier()
        if i >= 2:
            wait("sync", "sA", a_done[i - 2])
        tx = dma("sync", xt[par][:], xb[128 * i:128 * (i + 1), :])
        wait_dma("vector", tx)
        st_, b1_, vb = ln_stats(xt[par][:], 8 * par, double=True)
        wait("scalar", "sV", vb)
        if _DEBUG_TAPS:
            wait("sync", "sV", vb)
            dma("sync", dbg["st1"][i][:, 0:2], mvbuf[:, 2 * par:2 * par + 2])
            dma("sync", dbg["st1"][i][:, 2:7], stat[:, 8 * par:8 * par + 5], allow_slow_non_contiguous=True)
        a_done[i] = op("scalar", lambda e, p=par, s=st_, b=b1_: e.activation(
            ynorm[:], xt[p][:], AF.Identity, bias=b, scale=s), "sA")
        tpar = i % 2
        if i >= 2:
            wait("scalar", "sV", split_done[i - 2])
        if i % 4 == 0 and g >= 2:
            wait("vector", "sT", xnb_consumed[g % 2])
        for grp in range(4):
            bank = psT32 if grp % 2 == 0 else ps5
            if grp == 0:
                wait("tensor", "sA", a_done[i])
            if grp < 2 and i > 0:
                wait("tensor", "sA", evac_done[(i - 1, 2 + grp)])
            if grp >= 2:
                wait("tensor", "sA", evac_done[(i, grp - 2)])
            vtr = None
            for j in range(4):
                ch = 4 * grp + j
                vtr = op("tensor", lambda e, c=ch, j=j, b=bank: e.transpose(
                    b[:, 128 * j:128 * (j + 1)], ynorm[:, 128 * c:128 * (c + 1)], c_identf[:]),
                    "sT" if j == 3 else None)
            wait("scalar", "sT", vtr)
            evac_done[(i, grp)] = op("scalar", lambda e, g2=grp, b=bank, p=tpar: e.activation(
                xnf32[p][:, 512 * g2:512 * (g2 + 1)], b[:, 0:512], AF.Identity), "sA")
        wait("vector", "sA", evac_done[(i, 3)])
        if g > 1 and i % 4 == 0:
            wait_dma("vector", stage_tok[g - 2][0])
            wait_dma("vector", stage_tok[g - 2][1])
        dh = xnb_hi[g % 2][:].rearrange("p (c f) -> p c f", c=NCH)[:, :, 128 * (i % 4):128 * (i % 4) + 128]
        dl = xnb_lo[g % 2][:].rearrange("p (c f) -> p c f", c=NCH)[:, :, 128 * (i % 4):128 * (i % 4) + 128]
        op("vector", lambda e, d=dh, p=tpar: e.tensor_copy(d, xnf32[p][:]))
        split_done[i] = op("vector", lambda e, d=dl, sh=dh, p=tpar: e.tensor_tensor(
            d, xnf32[p][:], sh, OP.subtract), "sV")
        if i % 4 == 3:
            wait("sync", "sV", split_done[i])
            th = dma("sync", xnf_hi[g], xnb_hi[g % 2][:])
            tl = dma("sync", xnf_lo[g], xnb_lo[g % 2][:])
            stage_tok[g] = (th, tl)
            if _DEBUG_TAPS:
                dma("sync", dbg["xnf_hi"][g], xnb_hi[g % 2][:])
                dma("sync", dbg["xnf_lo"][g], xnb_lo[g % 2][:])
            for wi_ in range(3):
                pend_qkv.append((g, g % 2, wi_, split_done[i] if wi_ == 0 else None))
        if pend_qkv:
            g_, xp_, wi_, sg_ = pend_qkv.pop(0)
            emit_qkv_proj(0, g_, xnb_hi[xp_], xnb_lo[xp_], wi_, split_gate=sg_)
    while pend_qkv:
        g_, xp_, wi_, sg_ = pend_qkv.pop(0)
        emit_qkv_proj(0, g_, xnb_hi[xp_], xnb_lo[xp_], wi_, split_gate=sg_)
    flush_qkv_tails()

    flush(block)
    barrier()
    flush(block)

    def early_out(*stacks):
        ob = stacks[0].enter_context(nc.sbuf_tensor("early_outb", [128, D], F32))
        vz = op("vector", lambda e: e.memset(ob[:], 0.0), "sV")
        wait("sync", "sV", vz)
        toks = []
        for bb in range(B):
            for tb in range(4):
                toks.append(dma("sync", out[bb][128 * tb:128 * (tb + 1), :], ob[:]))
        wait_all_dma("gpsimd")
        flush(block)
        for st_ in stacks:
            st_.close()
        ctx.close()
        return nc

    if _PHASE_LIMIT == 1:
        return early_out(ctxA)

    for h in range(NHL):
        if h >= 1:
            if qkv_mm_end[0] is not None:
                wait("vector", "sT", qkv_mm_end[0])
            if attn_end[0] is not None:
                wait("vector", "sT", attn_end[0])
            emit_w_section(h)
            bound_state["read"] = psav_free[0] if h >= 2 else None
            for tg in range(_NG_LIMIT):
                xpar = tg % 2
                xh_b, xl_b = xnb_hi[xpar], xnb_lo[xpar]
                if xnb_consumed[xpar] is not None:
                    wait("sync", "sT", xnb_consumed[xpar])
                if h == 1:
                    wait_dma("sync", stage_tok[tg][0])
                    wait_dma("sync", stage_tok[tg][1])
                th = dma("sync", xh_b[:], xnf_hi[tg])
                tl = dma("sync", xl_b[:], xnf_lo[tg])
                wait_dma("tensor", th)
                wait_dma("tensor", tl)
                emit_qkv_group(h, tg, xh_b, xl_b)
            flush_qkv_tails()
        qkv_mm_end[0] = cnt["sT"]
        wait("vector", "sA", cnt["sA"])
        if _SKIP_25 or _SKIP_BOUND:
            vneg = op("vector", lambda e: e.memset(negmb[:], -1.0), "sV")
        if not (_SKIP_25 or _SKIP_BOUND):
            vbn = op("vector", lambda e: e.tensor_mul(bnd[0:1, 2:3], bnd[0:1, 0:1], bnd[0:1, 1:2]), "sV")
            wait("scalar", "sV", vbn)
            vsq = op("scalar", lambda e: e.activation(bnd[0:1, 3:4], bnd[0:1, 2:3], AF.Sqrt), "sA")
            wait("vector", "sA", vsq)
            vnm = op("vector", lambda e: e.tensor_scalar(negm11[:], bnd[0:1, 3:4], -1.03, -0.01, OP.mult, OP.add), "sV")
            wait("tensor", "sV", vnm)
            vbm = op("tensor", lambda e: e.matmul(ps3[0:128, 0:1], c_onesf[:], negm11[:], start=True, stop=True), "sT")
            wait("vector", "sT", vbm)
            vneg = op("vector", lambda e: e.tensor_copy(negmb[:], ps3[0:128, 0:1]), "sV")

        if _PHASE_LIMIT == 1.3:
            flush(block); barrier(); flush(block)
            return early_out(ctxA)

        psAV = [ps2, ps3, ps4, ps5]
        wait("tensor", "sA", cnt["sA"])
        wait("tensor", "sV", cnt["sV"])
        wait("scalar", "sV", vneg)

        exp_done = {}
        split3 = {}
        avmm_done = {}
        psav_free = [None]
        kseq = 0

        def emit_epi_tr(qb):
            eh, eqg, vvq = pend_epi[0]
            wait("tensor", "sV", vvq[qb])
            if epi_ev_prev[0] is not None:
                wait("tensor", "sV", epi_ev_prev[0])
            op("tensor", lambda e, t=avbh[qb]: e.transpose(psTb[:, 0:128], t[:], c_identb[:]))
            vt1 = op("tensor", lambda e, t=avbl[qb]: e.transpose(psTb[:, 128:256], t[:], c_identb[:]), "sT")
            epi_wait_tr[qb] = vt1
            wait("vector", "sT", vt1)
            if qb == 0 and stg_tok[0] is not None:
                wait_dma("vector", stg_tok[0])
            op("vector", lambda e, qb=qb: e.tensor_copy(
                avfst[:, 128 * qb:128 * (qb + 1)], psTb[:, 0:128]))
            epi_ev_prev[0] = op("vector", lambda e, qb=qb: e.tensor_copy(
                avfst[:, 512 + 128 * qb:512 + 128 * (qb + 1)], psTb[:, 128:256]), "sV")
            if qb == 3:
                wait("sync", "sV", epi_ev_prev[0])
                stg_tok[0] = dma("sync", a2a_in[eh].ap()[eqg], avfst[:])
                if _DEBUG_TAPS and eh == 0:
                    dma("sync", dbg["av0"][eqg], avfst[:])
                pend_epi[0] = None

        sc_done = {}

        def emit_scores_T(qg, kt, kseq):
            psS = (ps0, ps1, psT32)[kseq % 3]
            diag_j = kt - 4 * qg if kt >= 4 * qg else -1
            off = 128 * diag_j if diag_j > 0 else 0
            if kseq >= 3:
                wait("tensor", "sA", exp_done[kseq - 3])
            kh = Khi[:, 128 * kt:128 * (kt + 1)]
            kl = Klo[:, 128 * kt:128 * (kt + 1)]
            qh_ = Qhi[:, 512 * qg + off:512 * (qg + 1)]
            ql_ = Qlo[:, 512 * qg + off:512 * (qg + 1)]
            pslc = psS[:, off:512]
            op("tensor", lambda e, p=pslc, a=kh, b=qh_: e.matmul(p, a, b, start=True, stop=False))
            op("tensor", lambda e, p=pslc, a=kh, b=ql_: e.matmul(p, a, b, start=False, stop=False))
            sc_done[kseq] = op("tensor", lambda e, p=pslc, a=kl, b=qh_: e.matmul(p, a, b, start=False, stop=True), "sT")

        def emit_softmax_VS(qg, kt, kseq):
            par = kseq % 2
            psS = (ps0, ps1, psT32)[kseq % 3]
            diag_j = kt - 4 * qg if kt >= 4 * qg else -1
            off = 128 * diag_j if diag_j > 0 else 0
            vsc_ = sc_done[kseq]
            if diag_j >= 0:
                wait("vector", "sT", vsc_)
                dslc = psS[:, 128 * diag_j:128 * (diag_j + 1)]
                vmk = op("vector", lambda e, d=dslc: e.tensor_tensor(d, d, c_dmask[:], OP.add), "sV")
                wait("scalar", "sV", vmk)
            else:
                wait("scalar", "sT", vsc_)
            if kseq >= 2:
                wait("scalar", "sV", split3[kseq - 2])
                wait("scalar", "sT", avmm_done[kseq - 2])
            ncols = 512 - off
            op("scalar", lambda e, p=psS[:, off:512], b=p32b[par], o=off, n=ncols: e.activation(
                b[:, o:o + n], p, AF.Exp, bias=negmb[:, 0:1]))
            exp_done[kseq] = op("scalar", lambda e, p=psS[:, off:512], b=phib[par], o=off, n=ncols: e.activation(
                b[:, o:o + n], p, AF.Exp, bias=negmb[:, 0:1]), "sA")
            wait("vector", "sA", exp_done[kseq])
            split3[kseq] = op("vector", lambda e, b=p32b[par], s=phib[par], d=plob[par], o=off, n=ncols: e.tensor_tensor(
                d[:, o:o + n], b[:, o:o + n], s[:, o:o + n], OP.subtract), "sV")

        def emit_av(qg, kt, kseq, nkt):
            par = kseq % 2
            diag_j = kt - 4 * qg if kt >= 4 * qg else -1
            if kt == 0 and psav_free[0] is not None:
                wait("tensor", "sV", psav_free[0])
            wait("tensor", "sV", split3[kseq])
            jmin = diag_j if diag_j >= 0 else 0
            for qb in range(jmin, 4):
                qb_last = (kt == 4 * qg + qb)
                phslc = phib[par][:, 128 * qb:128 * (qb + 1)]
                plslc = plob[par][:, 128 * qb:128 * (qb + 1)]
                vslc = vaug[:, 257 * kt:257 * kt + 257]
                vslc129 = vaug[:, 257 * kt:257 * kt + 129]
                if qb_last and kt == 0:
                    op("tensor", lambda e, p=psAV[qb], a=phslc, v=vslc: e.matmul(
                        p[:, 0:257], a, v, start=True, stop=False))
                    op("tensor", lambda e, p=psAV[qb], a=plslc, v=vslc129: e.matmul(
                        p[:, 0:129], a, v, start=False, stop=False))
                    if qb == 3:
                        avmm_done[kseq] = op("tensor", lambda e, p=psAV[qb], v=vslc: e.matmul(
                            p[:, 0:257], c_zerob[:], v, start=False, stop=True), "sT")
                    else:
                        op("tensor", lambda e, p=psAV[qb], v=vslc: e.matmul(
                            p[:, 0:257], c_zerob[:], v, start=False, stop=True))
                elif qb_last:
                    op("tensor", lambda e, p=psAV[qb], a=plslc, v=vslc129: e.matmul(
                        p[:, 0:129], a, v, start=False, stop=False))
                    if qb == 3:
                        avmm_done[kseq] = op("tensor", lambda e, p=psAV[qb], a=phslc, v=vslc: e.matmul(
                            p[:, 0:257], a, v, start=False, stop=True), "sT")
                    else:
                        op("tensor", lambda e, p=psAV[qb], a=phslc, v=vslc: e.matmul(
                            p[:, 0:257], a, v, start=False, stop=True))
                else:
                    op("tensor", lambda e, p=psAV[qb], a=phslc, v=vslc, st=(kt == 0): e.matmul(
                        p[:, 0:257], a, v, start=st, stop=False))
                    if qb == 3:
                        avmm_done[kseq] = op("tensor", lambda e, p=psAV[qb], a=plslc, v=vslc129: e.matmul(
                            p[:, 0:129], a, v, start=False, stop=False), "sT")
                    else:
                        op("tensor", lambda e, p=psAV[qb], a=plslc, v=vslc129: e.matmul(
                            p[:, 0:129], a, v, start=False, stop=False))

        for qg in range(NG):
            nkt = 4 * qg + 4
            base = kseq
            for kt in range(nkt):
                emit_scores_T(qg, kt, base + kt)
                if pend_epi[0] is not None and 1 <= kt <= 4:
                    emit_epi_tr(kt - 1)
                if kt >= 2:
                    emit_av(qg, kt - 2, base + kt - 2, nkt)
                emit_softmax_VS(qg, kt, base + kt)
            emit_av(qg, nkt - 2, base + nkt - 2, nkt)
            emit_av(qg, nkt - 1, base + nkt - 1, nkt)
            kseq = base + nkt

            wait("vector", "sT", avmm_done[kseq - 1])
            epi_psav = None
            vvq = [None] * 4
            for qb in range(4):
                if epi_wait_tr[qb] is not None:
                    wait("vector", "sT", epi_wait_tr[qb])
                op("vector", lambda e, p=psAV[qb], r=rsumc[qb % 2]: e.reciprocal(r[:], p[:, 0:1]))
                op("vector", lambda e, p=psAV[qb], t=avt32[qb]: e.tensor_copy(t[:], p[:, 1:129]))
                epi_psav = op("vector", lambda e, p=psAV[qb], t=avt32[qb]: e.tensor_tensor(
                    t[:], t[:], p[:, 129:257], OP.add), "sV")
                op("vector", lambda e, t=avt32[qb], r=rsumc[qb % 2]: e.tensor_scalar(
                    t[:], t[:], r[:][:, 0:1], None, OP.mult))
                op("vector", lambda e, t=avt32[qb], d=avbh[qb]: e.tensor_copy(d[:], t[:]))
                vvq[qb] = op("vector", lambda e, t=avt32[qb], hh=avbh[qb], d=avbl[qb]: e.tensor_tensor(
                    d[:], t[:], hh[:], OP.subtract), "sV")
            psav_free[0] = epi_psav
            pend_epi[0] = (h, qg, vvq)

        if _PHASE_LIMIT == 1.6:
            flush(block); barrier(); flush(block)
            return early_out(ctxA)

        for qb in range(4):
            emit_epi_tr(qb)

        wait_all_dma("gpsimd")
        def mk_cc(l):
            def go(e):
                with allow_ici_all_to_all():
                    return e.collective_compute(
                        "AllToAll", OP.bypass,
                        replica_groups=[list(range(NCORES))],
                        ins=[a2a_in[l].ap().opt()],
                        outs=[a2a_out[l].ap().opt()])
            return go
        op("gpsimd", mk_cc(h), "sC")
        attn_end[0] = cnt["sT"]
        psav_free_w[0] = psav_free[0]
        vtev_done["last"] = epi_ev_prev[0]

        flush(block)
        if _PHASE_LIMIT == 2:
            break

    if _PHASE_LIMIT == 2:
        return early_out(ctxA)

    ctxA.close()
    ctxB0 = ExitStack()
    x1slab = ctxB0.enter_context(nc.sbuf_tensor("x1slab", [128, 8 * D], F32))
    stln = ctxB0.enter_context(nc.sbuf_tensor("stln", [128, 16], F32))
    ctxB1 = ExitStack()
    sbB = lambda name, shape, dt: ctxB1.enter_context(nc.sbuf_tensor(name, shape, dt))
    avrhi = sbB("avrhi", [128, NCH * 512], BF16)
    avrlo = sbB("avrlo", [128, NCH * 512], BF16)
    wo32 = [sbB("wo32a", [128, 4 * 512], F32)]
    wohi = sbB("wohi", [128, NCH * 512], BF16)
    wolo = sbB("wolo", [128, NCH * 512], BF16)
    ysball = sbB("ysball", [128, 4 * D], F32)
    bo_bc = sbB("bo_bc", [128, D], F32)
    bosb = sbB("bosb", [1, D], F32)
    ynorm2 = sbB("ynorm2", [128, D], F32)

    barrier()

    tbo = dma("sync", bosb[:], bo.unsqueeze(0))
    wait_dma("tensor", tbo)
    vbcp = None
    for og in range(4):
        if vbcp is not None:
            wait("tensor", "sV", vbcp)
        vbm = op("tensor", lambda e, og=og: e.matmul(
            ps1[:, 0:512], c_onesf[:], bosb[0:1, 512 * og:512 * (og + 1)], start=True, stop=True), "sT")
        wait("vector", "sT", vbm)
        vbcp = op("vector", lambda e, og=og: e.tensor_copy(bo_bc[:, 512 * og:512 * (og + 1)], ps1[:, 0:512]), "sV")

    x1_toks = [None] * 8
    for bb in range(B):
        for tb in range(4):
            slab = 4 * bb + tb
            x1_toks[slab] = dma("gpsimd", x1slab[:, D * slab:D * (slab + 1)],
                                xres[bb][128 * tb:128 * (tb + 1), :])

    def emit_avr(eng_name, bb):
        toks = []
        for l in range(NHL):
            dsth = avrhi[:].rearrange("p (c f) -> p c f", c=NCH)[:, l::4, :]
            dstl = avrlo[:].rearrange("p (c f) -> p c f", c=NCH)[:, l::4, :]
            srcs = a2a_out[l].ap()[4 * bb:4 * bb + 4].transpose([1, 0, 2])
            toks.append(dma(eng_name, dsth, srcs[:, :, 0:512]))
            toks.append(dma(eng_name, dstl, srcs[:, :, 512:1024]))
        return toks

    banks = [ps0, ps1, ps2, psT32]
    mm_mark = {}
    split_mark = {}
    evac_mark = [None]
    avr_toks = [None, None]

    def emit_tail(bb, tb):
        yslc = ysball[:, D * tb:D * (tb + 1)]
        st_, b1_, vb = ln_stats(yslc, 0, double=False)
        wait("scalar", "sV", vb)
        vn = op("scalar", lambda e, y=yslc, s=st_, b=b1_: e.activation(
            ynorm2[:], y, AF.Identity, bias=b, scale=s), "sA")
        slab = 4 * bb + tb
        wait("vector", "sA", vn)
        wait_dma("vector", x1_toks[slab])
        op("vector", lambda e, s2=slab: e.tensor_tensor(
            x1slab[:, D * s2:D * (s2 + 1)], x1slab[:, D * s2:D * (s2 + 1)], ynorm2[:], OP.add), "sV")
        st2_, b2_, _vb2 = ln_stats(x1slab[:, D * slab:D * (slab + 1)], 0, double=False)
        op("vector", lambda e, s2=slab, s=st2_: e.tensor_copy(
            stln[:, 2 * s2:2 * s2 + 2], stat[:, 3:5]), "sV")
        if _DEBUG_TAPS:
            dma("sync", dbg["x1"][slab], x1slab[:, D * slab:D * (slab + 1)])

    for s in range(32):
        bb, og, qtr = s // 16, (s % 16) // 4, s % 4
        if s > 0:
            wait("sync", "sV", split_mark[s - 1])
        if s == 2:
            wait("sync", "sC", cnt["sC"])
            avr_toks[0] = emit_avr("sync", 0)
        tl_ = dma("sync", wo32[0][:].rearrange("p (c f) -> p c f", c=4),
                  wo.rearrange("(c p) o -> p c o", p=128)[:, qtr::4,
                                                         512 * og:512 * (og + 1)])
        if s >= 4:
            wait("vector", "sT", mm_mark[s - 4])
        wait_dma("vector", tl_)
        op("vector", lambda e, q=qtr: e.tensor_copy(
            wohi[:, 2048 * q:2048 * (q + 1)], wo32[0][:]))
        split_mark[s] = op("vector", lambda e, q=qtr: e.tensor_tensor(
            wolo[:, 2048 * q:2048 * (q + 1)], wo32[0][:], wohi[:, 2048 * q:2048 * (q + 1)], OP.subtract), "sV")
        wait("tensor", "sV", split_mark[s])
        if s == 0:
            wait("tensor", "sV", vbcp)
        if og == 0:
            if bb == 0:
                for j in (2 * qtr, 2 * qtr + 1):
                    emit("tensor", lambda e, j=j: e.wait_ge(
                        qG if avr_toks[0][j][0] == "G" else qD[avr_toks[0][j][0]],
                        avr_toks[0][j][1]))
            else:
                for j in (2 * qtr, 2 * qtr + 1):
                    wait_dma("tensor", avr_toks[1][j])
        if qtr == 0 and evac_mark[0] is not None:
            wait("tensor", "sV", evac_mark[0])
        for tb in range(4):
            pso = banks[tb]
            for k in range(4):
                chp = 4 * k + qtr
                slot = 4 * qtr + k
                ah = avrhi[:, 512 * chp + 128 * tb:512 * chp + 128 * (tb + 1)]
                al = avrlo[:, 512 * chp + 128 * tb:512 * chp + 128 * (tb + 1)]
                wh = wohi[:, 512 * slot:512 * (slot + 1)]
                wl = wolo[:, 512 * slot:512 * (slot + 1)]
                op("tensor", lambda e, p=pso, a=ah, w=wh, st=(qtr == 0 and k == 0): e.matmul(
                    p[:], a, w, start=st, stop=False))
                op("tensor", lambda e, p=pso, a=ah, w=wl: e.matmul(p[:], a, w, start=False, stop=False))
                stp = (qtr == 3 and k == 3)
                if k == 3 and tb == 3:
                    mm_mark[s] = op("tensor", lambda e, p=pso, a=al, w=wh, stp=stp: e.matmul(
                        p[:], a, w, start=False, stop=stp), "sT")
                else:
                    op("tensor", lambda e, p=pso, a=al, w=wh, stp=stp: e.matmul(
                        p[:], a, w, start=False, stop=stp))
        if bb == 1 and og == 0:
            emit_tail(0, qtr)
        if qtr == 3:
            wait("vector", "sT", mm_mark[s])
            for tb in range(4):
                evac_mark[0] = op("vector", lambda e, p=banks[tb], tb=tb, og=og: e.scalar_tensor_tensor(
                    ysball[:, D * tb + 512 * og:D * tb + 512 * (og + 1)], p[:], 1.0,
                    bo_bc[:, 512 * og:512 * (og + 1)], OP.mult, OP.add), "sV")
            if og == 3:
                if _DEBUG_TAPS:
                    wait("sync", "sV", cnt["sV"])
                    dma("sync", dbg["ysb"][bb], ysball[:])
                    if bb == 0:
                        for l in range(NHL):
                            for si in range(4):
                                dma("sync", dbg["rcv"][4 * si + l], a2a_out[l].ap()[si])
                if bb == 0:
                    toks = []
                    for l in range(NHL):
                        wait("gpsimd", "sT", mm_mark[12 + l])
                        dsth = avrhi[:].rearrange("p (c f) -> p c f", c=NCH)[:, l::4, :]
                        dstl = avrlo[:].rearrange("p (c f) -> p c f", c=NCH)[:, l::4, :]
                        srcs = a2a_out[l].ap()[4:8].transpose([1, 0, 2])
                        toks.append(dma("gpsimd", dsth, srcs[:, :, 0:512]))
                        toks.append(dma("gpsimd", dstl, srcs[:, :, 512:1024]))
                    avr_toks[1] = toks
                else:
                    for tb in range(4):
                        emit_tail(1, tb)

    flush(block)
    barrier()
    flush(block)

    if _PHASE_LIMIT == 3:
        return early_out(ctxB1, ctxB0)

    ctxB1.close()
    ctxB2 = ExitStack()
    sb2 = lambda name, shape, dt: ctxB2.enter_context(nc.sbuf_tensor(name, shape, dt))
    w8row = sb2("w8row", [128, D], F8)
    w8rb = sb2("w8rb", [128, D], BF16)
    w8T = sb2("w8T", [128, NCH * D], BF16)
    x2nb = sb2("x2nb", [128, D], F32)
    tqb = sb2("tqb", [128, D], F32)
    q8b = sb2("q8b", [128, D], U8)
    qbb = sb2("qbb", [128, D], BF16)
    xqf = sb2("xqf", [128, NCH * 128], BF16)
    xqf2 = sb2("xqf2", [128, NCH * 128], BF16)
    ysb2 = sb2("ysb2", [128, D], F32)
    outb = sb2("outb", [128, D], F32)
    psT8 = ctxB2.enter_context(nc.psum_tensor("psT8", [128, 1024], BF16))
    psT8b = ctxB2.enter_context(nc.psum_tensor("psT8b", [128, 1024], BF16))

    xq_bufs = [xqf, xqf2]
    t8_ev = [None, None]
    q_mark = {}
    vtt_last = {}
    xq_ev = {}
    mm_mark2 = {}
    ffn_evac = [None, None]
    tail_vb = {}
    tod_tok = {}

    def emit_qdq_v(slab):
        x1s = x1slab[:, D * slab:D * (slab + 1)]
        if slab > 0:
            wait("scalar", "sV", q_mark[slab - 1])
        vn = op("scalar", lambda e, y=x1s, s=stln[:, 2 * slab:2 * slab + 1],
                b=stln[:, 2 * slab + 1:2 * slab + 2]: e.activation(
            x2nb[:], y, AF.Identity, bias=b, scale=s), "sA")
        wait("vector", "sA", vn)
        op("vector", lambda e: e.tensor_scalar(tqb[:], x2nb[:], invsb[:, 0:1], 448.0, OP.mult, OP.min))
        if slab > 0:
            wait("vector", "sT", vtt_last[slab - 1])
        op("vector", lambda e: e.tensor_scalar(q8b[:].bitcast(F8), tqb[:], -448.0, None, OP.max))
        q_mark[slab] = op("vector", lambda e: e.tensor_copy(qbb[:], q8b[:].bitcast(F8)), "sV")

    emit_qdq_v(0)

    w8T_done = None
    prev_vup = None
    prev_vtt = None
    for oc in range(NCH):
        if prev_vup is not None:
            wait("sync", "sV", prev_vup)
        tw = dma("sync", w8row[:], w8[128 * oc:128 * (oc + 1), :])
        wait_dma("vector", tw)
        if prev_vtt is not None:
            wait("vector", "sT", prev_vtt)
        prev_vup = vup = op("vector", lambda e: e.tensor_copy(w8rb[:], w8row[:]), "sV")
        wait("tensor", "sV", vup)
        for icq in range(4):
            bank = (psT8, psT8b)[icq % 2]
            if t8_ev[icq % 2] is not None:
                wait("tensor", "sV", t8_ev[icq % 2])
            vtt = None
            for j in range(4):
                ic = 4 * icq + j
                vtt = op("tensor", lambda e, ic=ic, j=j, b=bank: e.transpose(
                    b[:, 128 * j:128 * (j + 1)], w8rb[:, 128 * ic:128 * (ic + 1)], c_identb[:]),
                    "sT" if j == 3 else None)
            wait("vector", "sT", vtt)
            t8_ev[icq % 2] = w8T_done = op("vector", lambda e, icq=icq, oc=oc, b=bank: e.tensor_copy(
                w8T[:].rearrange("p (c f) -> p c f", c=NCH)[:, 4 * icq:4 * (icq + 1),
                                                           128 * oc:128 * (oc + 1)],
                b[:, 0:512].rearrange("p (c f) -> p c f", c=4)), "sV")
        prev_vtt = vtt

    def emit_xT(slab):
        par = slab % 2
        if slab >= 2:
            wait("vector", "sT", mm_mark2[(slab - 2, 3)])
        wait("tensor", "sV", q_mark[slab])
        for icq in range(4):
            bank = (psT8, psT8b)[icq % 2]
            if t8_ev[icq % 2] is not None:
                wait("tensor", "sV", t8_ev[icq % 2])
            vtt = None
            for j in range(4):
                ic = 4 * icq + j
                vtt = op("tensor", lambda e, ic=ic, j=j, b=bank: e.transpose(
                    b[:, 128 * j:128 * (j + 1)], qbb[:, 128 * ic:128 * (ic + 1)], c_identb[:]),
                    "sT" if j == 3 else None)
            wait("vector", "sT", vtt)
            t8_ev[icq % 2] = xq_ev[slab] = op("vector", lambda e, icq=icq, b=bank, par=par: e.tensor_copy(
                xq_bufs[par][:, 512 * icq:512 * (icq + 1)], b[:, 0:512]), "sV")
        vtt_last[slab] = vtt

    def emit_mm(slab, og):
        par = slab % 2
        psf = (ps0, ps2)[og % 2]
        if slab == 0 and og == 0:
            wait("tensor", "sV", w8T_done)
        if og == 0:
            wait("tensor", "sV", xq_ev[slab])
        if ffn_evac[og % 2] is not None:
            wait("tensor", "sA", ffn_evac[og % 2])
        vmf = None
        for ic in range(NCH):
            if ic < NCH - 1:
                op("tensor", lambda e, p=psf, ic=ic, og=og, par=par, st=(ic == 0): e.matmul(
                    p[:], xq_bufs[par][:, 128 * ic:128 * (ic + 1)],
                    w8T[:, D * ic + 512 * og:D * ic + 512 * (og + 1)], start=st, stop=False))
            else:
                vmf = op("tensor", lambda e, p=psf, ic=ic, og=og, par=par: e.matmul(
                    p[:], xq_bufs[par][:, 128 * ic:128 * (ic + 1)],
                    w8T[:, D * ic + 512 * og:D * ic + 512 * (og + 1)], start=False, stop=True), "sT")
        mm_mark2[(slab, og)] = vmf
        wait("scalar", "sT", vmf)
        if og == 0 and slab >= 1:
            wait("scalar", "sV", tail_vb[slab - 1])
        ffn_evac[og % 2] = op("scalar", lambda e, p=psf, og=og: e.activation(
            ysb2[:, 512 * og:512 * (og + 1)], p[:], AF.Identity, scale=yscb[:, 0:1]), "sA")

    def emit_tail2(slab):
        bb, tb = slab // 4, slab % 4
        x1s = x1slab[:, D * slab:D * (slab + 1)]
        wait("vector", "sA", ffn_evac[1])
        st_, b1_, vb = ln_stats(ysb2[:], 8, double=False)
        tail_vb[slab] = vb
        wait("scalar", "sV", vb)
        if slab >= 1:
            wait_dma("scalar", tod_tok[slab - 1])
        vn2 = op("scalar", lambda e, s=st_, b=b1_: e.activation(
            outb[:], ysb2[:], AF.Identity, bias=b, scale=s), "sA")
        wait("vector", "sA", vn2)
        vout = op("vector", lambda e, y=x1s: e.tensor_tensor(outb[:], outb[:], y, OP.add), "sV")
        wait("sync", "sV", vout)
        tod_tok[slab] = dma("sync", out[bb][128 * tb:128 * (tb + 1), :], outb[:])

    for it in range(9):
        if 0 < it < 8:
            emit_qdq_v(it)
        if it > 0:
            emit_mm(it - 1, 0)
            emit_mm(it - 1, 1)
        if it < 8:
            emit_xT(it)
        if it > 0:
            emit_mm(it - 1, 2)
            emit_mm(it - 1, 3)
            emit_tail2(it - 1)

    wait_all_dma("gpsimd")
    flush(block)

    ctxB2.close()
    ctxB0.close()
    ctx.close()
    return nc


_NC_CACHE = None


def kernel(L0_attn_X, L0_attn_ln_g, L0_attn_ln_b, L0_attn_Wq, L0_attn_bq,
           L0_attn_Wk, L0_attn_bk, L0_attn_Wv, L0_attn_bv, L0_attn_Wo,
           L0_attn_bo, L0_ffn_weight, L0_ffn_weight_scale, L0_ffn_input_scale):
    global _NC_CACHE
    X = np.ascontiguousarray(L0_attn_X, dtype=np.float32)
    Wq = np.ascontiguousarray(L0_attn_Wq, dtype=np.float32)
    Wk = np.ascontiguousarray(L0_attn_Wk, dtype=np.float32)
    Wv = np.ascontiguousarray(L0_attn_Wv, dtype=np.float32)
    Wo = np.ascontiguousarray(L0_attn_Wo, dtype=np.float32)
    w8v = np.ascontiguousarray(L0_ffn_weight).view(ml_dtypes.float8_e4m3)
    c32 = np.float32(np.sqrt(np.float32(128.0), dtype=np.float32))

    identf = np.eye(128, dtype=np.float32)
    identb = np.eye(128, dtype=ml_dtypes.bfloat16)
    ident8 = np.eye(128).astype(ml_dtypes.float8_e4m3)
    pp, ff = np.arange(128)[:, None], np.arange(128)[None, :]
    dmaskv = np.where(pp <= ff, np.float32(0.0), np.float32(NEG)).astype(np.float32)
    onesfv = np.ones((1, 128), dtype=np.float32)

    in_maps = []
    for c in range(NCORES):
        b, hg = c // 4, c % 4
        in_maps.append(dict(
            xb=X[b],
            xres=np.ascontiguousarray(X[:, 512 * c:512 * (c + 1), :]),
            wq=np.ascontiguousarray(Wq[:, 512 * hg:512 * (hg + 1)]),
            wk=np.ascontiguousarray(Wk[:, 512 * hg:512 * (hg + 1)]),
            wv=np.ascontiguousarray(Wv[:, 512 * hg:512 * (hg + 1)]),
            bq=np.ascontiguousarray(L0_attn_bq[512 * hg:512 * (hg + 1)]).astype(np.float32),
            bk=np.ascontiguousarray(L0_attn_bk[512 * hg:512 * (hg + 1)]).astype(np.float32),
            bv=np.ascontiguousarray(L0_attn_bv[512 * hg:512 * (hg + 1)]).astype(np.float32),
            wo=Wo,
            bo=np.ascontiguousarray(L0_attn_bo, dtype=np.float32),
            lng=np.ascontiguousarray(L0_attn_ln_g, dtype=np.float32),
            lnb=np.ascontiguousarray(L0_attn_ln_b, dtype=np.float32),
            w8=w8v,
            wsc=np.asarray(L0_ffn_weight_scale, dtype=np.float32).reshape(1, 1),
            isc=np.asarray(L0_ffn_input_scale).reshape(1, 1),
            c32t=np.asarray(c32, dtype=np.float32).reshape(1, 1),
            identf=identf, identb=identb, ident8=ident8,
            dmask=dmaskv, onesf=onesfv,
        ))

    if _NC_CACHE is None:
        _NC_CACHE = build_program()
    res = run_bass_kernel_spmd(_NC_CACHE, in_maps, core_ids=list(range(NCORES)))

    outf = np.empty((B, S, D), dtype=np.float32)
    for c in range(NCORES):
        outf[:, 512 * c:512 * (c + 1), :] = res.results[c]["out"]
    return outf


# version 22
# speedup vs baseline: 1.1000x; 1.0427x over previous
import numpy as np
import ml_dtypes
import concourse.bass as bass
import concourse.mybir as mybir
from concourse.bass_utils import run_bass_kernel_spmd
from concourse.aot_env import allow_ici_all_to_all
from contextlib import ExitStack

F32 = mybir.dt.float32
BF16 = mybir.dt.bfloat16
F8 = mybir.dt.float8e4
U8 = mybir.dt.uint8
AF = mybir.ActivationFunctionType
OP = mybir.AluOpType
AX = mybir.AxisListType

D = 2048
NH = 16
DH = 128
S = 4096
B = 2
NCORES = 8
NHL = 4
NCH = D // 128
NTB = S // 128
NG = S // 512
EPS = 1e-5
NEG = -1.0e30
NDMA = 6

_DEBUG_TAPS = False
_PHASE_LIMIT = 99
_SKIP_BOUND = False
_SKIP_VT = False
_SKIP_25 = False
_NG_LIMIT = NG
_SKIP_PROJ = False
_PROJ_N = 3
_SKIP_EVAC = False
_SKIP_SPLIT = False
_P1_BARRIER = False


def build_program():
    nc = bass.Bass("TRN2", target_bir_lowering=False, detect_race_conditions=False)

    xb = nc.dram_tensor("xb", [S, D], F32, kind="ExternalInput").ap()
    xres = nc.dram_tensor("xres", [B, 512, D], F32, kind="ExternalInput").ap()
    wq = nc.dram_tensor("wq", [D, 512], F32, kind="ExternalInput").ap()
    wk = nc.dram_tensor("wk", [D, 512], F32, kind="ExternalInput").ap()
    wv = nc.dram_tensor("wv", [D, 512], F32, kind="ExternalInput").ap()
    bq = nc.dram_tensor("bq", [512], F32, kind="ExternalInput").ap()
    bk = nc.dram_tensor("bk", [512], F32, kind="ExternalInput").ap()
    bv = nc.dram_tensor("bv", [512], F32, kind="ExternalInput").ap()
    wo = nc.dram_tensor("wo", [D, D], F32, kind="ExternalInput").ap()
    bo = nc.dram_tensor("bo", [D], F32, kind="ExternalInput").ap()
    lng = nc.dram_tensor("lng", [D], F32, kind="ExternalInput").ap()
    lnb = nc.dram_tensor("lnb", [D], F32, kind="ExternalInput").ap()
    w8 = nc.dram_tensor("w8", [D, D], F8, kind="ExternalInput").ap()
    wsc = nc.dram_tensor("wsc", [1, 1], F32, kind="ExternalInput").ap()
    isc = nc.dram_tensor("isc", [1, 1], BF16, kind="ExternalInput").ap()
    c32t = nc.dram_tensor("c32t", [1, 1], F32, kind="ExternalInput").ap()
    identf = nc.dram_tensor("identf", [128, 128], F32, kind="ExternalInput").ap()
    identb = nc.dram_tensor("identb", [128, 128], BF16, kind="ExternalInput").ap()
    ident8 = nc.dram_tensor("ident8", [128, 128], F8, kind="ExternalInput").ap()
    dmask = nc.dram_tensor("dmask", [128, 128], F32, kind="ExternalInput").ap()
    onesf = nc.dram_tensor("onesf", [1, 128], F32, kind="ExternalInput").ap()

    out = nc.dram_tensor("out", [B, 512, D], F32, kind="ExternalOutput").ap()

    xnf_hi = nc.dram_tensor("xnf_hi", [NG, 128, NCH * 512], BF16, kind="Internal").ap()
    xnf_lo = nc.dram_tensor("xnf_lo", [NG, 128, NCH * 512], BF16, kind="Internal").ap()
    a2a_in = [nc.dram_tensor(f"a2a_in{l}", [8, 128, 1024], BF16, kind="Internal")
              for l in range(NHL)]
    a2a_out = [nc.dram_tensor(f"a2a_out{l}", [8, 128, 1024], BF16, kind="Internal")
               for l in range(NHL)]

    dbg = {}
    if _DEBUG_TAPS:
        dbg["xnf_hi"] = nc.dram_tensor("dbg_xnf_hi", [NG, 128, NCH * 512], BF16, kind="ExternalOutput").ap()
        dbg["xnf_lo"] = nc.dram_tensor("dbg_xnf_lo", [NG, 128, NCH * 512], BF16, kind="ExternalOutput").ap()
        dbg["qk"] = nc.dram_tensor("dbg_qk", [4, 128, S], BF16, kind="ExternalOutput").ap()
        dbg["av0"] = nc.dram_tensor("dbg_av0", [8, 128, 1024], BF16, kind="ExternalOutput").ap()
        dbg["x1"] = nc.dram_tensor("dbg_x1", [8, 128, D], F32, kind="ExternalOutput").ap()
        dbg["ysb"] = nc.dram_tensor("dbg_ysb", [B, 128, 4 * D], F32, kind="ExternalOutput").ap()
        dbg["rcv"] = nc.dram_tensor("dbg_rcv", [16, 128, 512], F32, kind="ExternalOutput").ap()
        dbg["st1"] = nc.dram_tensor("dbg_st1", [NTB, 128, 8], F32, kind="ExternalOutput").ap()

    ctx = ExitStack()
    qD = [ctx.enter_context(nc.semaphore(f"qD{i}")) for i in range(NDMA)]
    qG = ctx.enter_context(nc.semaphore("qG"))
    sT = ctx.enter_context(nc.semaphore("sT"))
    sV = ctx.enter_context(nc.semaphore("sV"))
    sA = ctx.enter_context(nc.semaphore("sA"))
    sC = ctx.enter_context(nc.semaphore("sC"))
    esem = dict(sT=sT, sV=sV, sA=sA, sC=sC)
    cnt = dict(sT=0, sV=0, sA=0, sC=0)
    dcnt = [0] * NDMA
    dturn = [0]
    gcnt = [0]

    ps0 = ctx.enter_context(nc.psum_tensor("ps0", [128, 512], F32))
    ps1 = ctx.enter_context(nc.psum_tensor("ps1", [128, 512], F32))
    ps2 = ctx.enter_context(nc.psum_tensor("ps2", [128, 512], F32))
    psT32 = ctx.enter_context(nc.psum_tensor("psT32", [128, 512], F32))

    sb = lambda name, shape, dt: ctx.enter_context(nc.sbuf_tensor(name, shape, dt))
    c_identf = sb("c_identf", [128, 128], F32)
    c_identb = sb("c_identb", [128, 128], BF16)
    c_ident8 = sb("c_ident8", [128, 128], F8)
    c_dmask = sb("c_dmask", [128, 128], F32)
    c_onesf = sb("c_onesf", [1, 128], F32)
    c_onesc = sb("c_onesc", [128, 1], BF16)
    c_zerob = sb("c_zerob", [128, 128], BF16)
    c_lng = sb("c_lng", [128, NCH], F32)
    c_lnb = sb("c_lnb", [128, NCH], F32)
    c_bq = sb("c_bq", [128, NHL], F32)
    c_bk = sb("c_bk", [128, NHL], F32)
    c_bv = sb("c_bv", [128, NHL], F32)
    c_sc = sb("c_sc", [1, 8], F32)
    c_iscb = sb("c_iscb", [1, 1], BF16)
    rcb = sb("rcb", [128, 1], F32)
    invsb = sb("invsb", [128, 1], F32)
    yscb = sb("yscb", [128, 1], F32)
    negm11 = sb("negm11", [1, 1], F32)
    negmb = sb("negmb", [128, 1], F32)
    bnd = sb("bnd", [1, 8], F32)
    stat = sb("stat", [128, 16], F32)
    bnbuf = sb("bnbuf", [128, 48], F32)
    mvbuf = sb("mvbuf", [128, 4], F32)

    phase_ops = []

    def emit(eng, fn):
        phase_ops.append((eng, fn))

    def inc(sem_name, n=1):
        cnt[sem_name] += n
        return cnt[sem_name]

    def wait(eng, sem_name, val):
        assert val <= cnt[sem_name], f"wait {sem_name}>={val}, emitted {cnt[sem_name]}"
        s = esem[sem_name]
        emit(eng, lambda e, s=s, v=val: e.wait_ge(s, v))

    def dma(eng_name, outap, inap, **kw):
        if eng_name == "gpsimd":
            gcnt[0] += 16
            tok = ("G", gcnt[0])
            emit(eng_name, lambda e, o=outap, i=inap, kw=kw: e.dma_start(out=o, in_=i, **kw).then_inc(qG, 16))
            return tok
        i = dturn[0] % NDMA
        dturn[0] += 1
        dcnt[i] += 16
        tok = (i, dcnt[i])
        s = qD[i]
        emit(eng_name, lambda e, o=outap, ia=inap, s=s, kw=kw: e.dma_start(out=o, in_=ia, **kw).then_inc(s, 16))
        return tok

    def wait_dma(eng, tok):
        i, v = tok
        s = qG if i == "G" else qD[i]
        emit(eng, lambda e, s=s, v=v: e.wait_ge(s, v))

    def wait_all_dma(eng):
        for i in range(NDMA):
            if dcnt[i]:
                v = dcnt[i]
                emit(eng, lambda e, s=qD[i], v=v: e.wait_ge(s, v))
        if gcnt[0]:
            v = gcnt[0]
            emit(eng, lambda e, v=v: e.wait_ge(qG, v))

    def op(eng_name, build, sem_name=None, n=1):
        if eng_name in ("vector", "scalar"):
            chain = "sV" if eng_name == "vector" else "sA"
            pre = cnt[chain]
            cs = esem[chain]
            if pre:
                emit(eng_name, lambda e, s=cs, v=pre: e.wait_ge(s, v))
            v = inc(chain)
            emit(eng_name, lambda e, b=build, s=cs: b(e).then_inc(s, 1))
            return v
        if sem_name is None:
            emit(eng_name, lambda e, b=build: b(e))
            return None
        s = esem[sem_name]
        v = inc(sem_name, n)
        emit(eng_name, lambda e, b=build, s=s, n=n: b(e).then_inc(s, n))
        return v

    def flush(block):
        ops = list(phase_ops)
        phase_ops.clear()
        for eng_name in ("sync", "tensor", "vector", "scalar", "gpsimd"):
            mine = [fn for e, fn in ops if e == eng_name]
            if not mine:
                continue
            getattr(block, eng_name)(lambda eng, mine=mine: [fn(eng) for fn in mine])

    def barrier():
        for eng in ("sync", "tensor", "vector", "scalar", "gpsimd"):
            for sname in ("sT", "sV", "sA"):
                if cnt[sname]:
                    wait(eng, sname, cnt[sname])
            wait_all_dma(eng)

    def rsqrt_col(acol, dst, t0, t1):
        va = op("scalar", lambda e, a=acol, d=dst: e.activation(d, a, AF.Sqrt), "sA")
        wait("vector", "sA", va)
        op("vector", lambda e, d=dst: e.reciprocal(d, d))
        for _ in range(1):
            op("vector", lambda e, d=dst, t=t0: e.tensor_mul(t, d, d))
            op("vector", lambda e, a=acol, t=t0, u=t1: e.tensor_mul(u, a, t))
            op("vector", lambda e, u=t1: e.tensor_scalar(u, u, -0.5, 1.5, OP.mult, OP.add))
            op("vector", lambda e, d=dst, u=t1: e.tensor_mul(d, d, u))

    def ln_stats(tile_ap, sc0, double):
        bs = bnbuf[:, 24 * (sc0 // 8):24 * (sc0 // 8) + 24]
        mv = mvbuf[:, 2 * (sc0 // 8):2 * (sc0 // 8) + 2]
        for k in range(4):
            op("vector", lambda e, k=k, t=tile_ap, b=bs: e.bn_stats(
                b[:, 6 * k:6 * k + 6], t[:, 512 * k:512 * (k + 1)]))
        op("vector", lambda e, b=bs, m=mv: e.bn_aggr(m, b))
        a_ = stat[:, sc0 + 0:sc0 + 1]
        t0 = stat[:, sc0 + 1:sc0 + 2]
        t1 = stat[:, sc0 + 2:sc0 + 3]
        st_ = stat[:, sc0 + 3:sc0 + 4]
        b1_ = stat[:, sc0 + 4:sc0 + 5]
        var = mv[:, 1:2]
        if double:
            op("vector", lambda e, v=var, t=t0: e.tensor_scalar(t, v, EPS, EPS * EPS, OP.mult, OP.add))
            va = op("vector", lambda e, v=var, t=t0, a=a_: e.tensor_add(a, v, t), "sV")
        else:
            va = op("vector", lambda e, v=var, a=a_: e.tensor_scalar(a, v, 1.0, EPS, OP.mult, OP.add), "sV")
        wait("scalar", "sV", va)
        rsqrt_col(a_, st_, t0, t1)
        op("vector", lambda e, m=mv, s=st_, b=b1_: e.tensor_mul(b, m[:, 0:1], s))
        vb = op("vector", lambda e, b=b1_: e.tensor_scalar_mul(b, b, -1.0), "sV")
        return st_, b1_, vb

    block = ctx.enter_context(nc.Block())

    dma("sync", c_identf[:], identf)
    dma("sync", c_identb[:], identb)
    dma("sync", c_ident8[:], ident8)
    dma("sync", c_dmask[:], dmask)
    dma("sync", c_onesf[:], onesf)
    dma("sync", c_lng[:], lng.rearrange("(c p) -> p c", p=128), allow_slow_non_contiguous=True)
    dma("sync", c_lnb[:], lnb.rearrange("(c p) -> p c", p=128), allow_slow_non_contiguous=True)
    dma("sync", c_bq[:], bq.rearrange("(h p) -> p h", p=128), allow_slow_non_contiguous=True)
    dma("sync", c_bk[:], bk.rearrange("(h p) -> p h", p=128), allow_slow_non_contiguous=True)
    dma("sync", c_bv[:], bv.rearrange("(h p) -> p h", p=128), allow_slow_non_contiguous=True)
    t1_ = dma("sync", c_sc[:, 0:1], c32t)
    t2_ = dma("sync", c_sc[:, 2:3], wsc)
    t3_ = dma("sync", c_iscb[:], isc)
    for t in (t1_, t2_, t3_):
        wait_dma("vector", t)
    op("vector", lambda e: e.memset(c_onesc[:], 1.0))
    op("vector", lambda e: e.memset(c_zerob[:], 0.0))
    op("vector", lambda e: e.tensor_copy(c_sc[:, 3:4], c_iscb[:]))
    op("vector", lambda e: e.reciprocal(c_sc[:, 1:2], c_sc[:, 0:1]))
    op("vector", lambda e: e.reciprocal(c_sc[:, 4:5], c_sc[:, 3:4]))
    vs = op("vector", lambda e: e.tensor_mul(c_sc[:, 5:6], c_sc[:, 3:4], c_sc[:, 2:3]), "sV")
    wait("tensor", "sV", vs)
    op("tensor", lambda e: e.matmul(ps1[0:128, 0:1], c_onesf[:], c_sc[:, 1:2], start=True, stop=True))
    op("tensor", lambda e: e.matmul(ps1[0:128, 1:2], c_onesf[:], c_sc[:, 4:5], start=True, stop=True))
    vt = op("tensor", lambda e: e.matmul(ps1[0:128, 2:3], c_onesf[:], c_sc[:, 5:6], start=True, stop=True), "sT")
    wait("vector", "sT", vt)
    op("vector", lambda e: e.tensor_copy(rcb[:], ps1[0:128, 0:1]))
    op("vector", lambda e: e.tensor_copy(invsb[:], ps1[0:128, 1:2]))
    vp0 = op("vector", lambda e: e.tensor_copy(yscb[:], ps1[0:128, 2:3]), "sV")

    ctxA = ExitStack()
    sbA = lambda name, shape, dt: ctxA.enter_context(nc.sbuf_tensor(name, shape, dt))
    ps3 = ctxA.enter_context(nc.psum_tensor("ps3", [128, 512], F32))
    ps4 = ctxA.enter_context(nc.psum_tensor("ps4", [128, 512], F32))
    ps5 = ctxA.enter_context(nc.psum_tensor("ps5", [128, 512], F32))
    psTb = ctxA.enter_context(nc.psum_tensor("psTb", [128, 1024], BF16))

    xt = [sbA("xt0", [128, D], F32), sbA("xt1", [128, D], F32)]
    ynorm = sbA("ynorm", [128, D], F32)
    xnf32 = [sbA("xnf32a", [128, D], F32), sbA("xnf32b", [128, D], F32)]
    xnb_hi = [sbA("xnb_hi0", [128, NCH * 512], BF16), sbA("xnb_hi1", [128, NCH * 512], BF16)]
    xnb_lo = [sbA("xnb_lo0", [128, NCH * 512], BF16), sbA("xnb_lo1", [128, NCH * 512], BF16)]
    w32t = ynorm
    wqh = [sbA("wq_hi", [128, D], BF16), sbA("wq_lo", [128, D], BF16)]
    wkh = [sbA("wk_hi", [128, D], BF16), sbA("wk_lo", [128, D], BF16)]
    wvh = [sbA("wv_hi", [128, D], BF16), sbA("wv_lo", [128, D], BF16)]
    qf32 = {"q": sbA("qf32q", [128, 512], F32), "k": sbA("qf32k", [128, 512], F32),
            "v": sbA("qf32v", [128, 512], F32)}
    qsqb = sbA("qsqb", [128, 512], BF16)
    Qhi = sbA("Qhi", [128, S], BF16)
    Qlo = sbA("Qlo", [128, S], BF16)
    Khi = sbA("Khi", [128, S], BF16)
    Klo = sbA("Klo", [128, S], BF16)
    vfs = [sbA("vfs_hi", [128, 512], BF16), sbA("vfs_lo", [128, 512], BF16)]
    vaug = sbA("vaug", [128, NTB * 257], BF16)
    p32b = [sbA("p32b0", [128, 512], F32), sbA("p32b1", [128, 512], F32)]
    phib = [sbA("phib0", [128, 512], BF16), sbA("phib1", [128, 512], BF16)]
    plob = [sbA("plob0", [128, 512], BF16), sbA("plob1", [128, 512], BF16)]
    avt32 = [sbA(f"avt32{q}", [128, 128], F32) for q in range(4)]
    avbh = [sbA(f"avbh{q}", [128, 128], BF16) for q in range(4)]
    avbl = [sbA(f"avbl{q}", [128, 128], BF16) for q in range(4)]
    rsumc = [sbA("rsumc0", [128, 1], F32), sbA("rsumc1", [128, 1], F32)]
    avfst = sbA("avfst", [128, 1024], BF16)

    qkvbias = ctxA.enter_context(nc.sbuf_tensor("qkvbias", [128, 12], F32))
    psav_free_w = [None]
    bias_read = [None]
    qkv_evac = {}
    split_prev = {}
    vtev_done = {}
    xnb_consumed = [None, None]
    stg_tok = [None]
    pend_epi = [None]
    epi_ev_prev = [None]
    epi_wait_tr = [None] * 4
    qkv_mm_end = [None]
    attn_end = [None]
    bound_state = {"read": None, "mm": None}
    qkv_tail = []

    def emit_w_section(h):
        wsplit_last = None
        for wi, (wdram, wsb) in enumerate(((wq, wqh), (wk, wkh), (wv, wvh))):
            tw = dma("sync", w32t[:].rearrange("p (c f) -> p c f", c=NCH),
                     wdram.rearrange("(c p) n -> p c n", p=128)[:, :, 128 * h:128 * (h + 1)])
            wait_dma("tensor", tw)
            if psav_free_w[0] is not None:
                wait("tensor", "sV", psav_free_w[0])
                psav_free_w[0] = None
            if bias_read[0] is not None:
                wait("tensor", "sV", bias_read[0])
            vbm_ = None
            for ch2 in range(NCH):
                vbm_ = op("tensor", lambda e, c=ch2, st=(ch2 == 0), sp=(ch2 == NCH - 1): e.matmul(
                    ps3[0:128, 0:1], w32t[:, 128 * c:128 * (c + 1)], c_lnb[:, c:c + 1],
                    start=st, stop=sp), "sT" if ch2 == NCH - 1 else None)
            wait("vector", "sT", vbm_)
            bcol_d = qkvbias[:, 3 * h + wi:3 * h + wi + 1]
            bsrc = {0: c_bq, 1: c_bk, 2: c_bv}[wi][:, h:h + 1]
            bias_read[0] = op("vector", lambda e, d=bcol_d, b=bsrc: e.tensor_tensor(
                d, ps3[0:128, 0:1], b, OP.add), "sV")
            wait_dma("vector", tw)
            for ch2 in range(NCH):
                op("vector", lambda e, c=ch2: e.tensor_scalar(
                    w32t[:, 128 * c:128 * (c + 1)], w32t[:, 128 * c:128 * (c + 1)],
                    c_lng[:, c:c + 1], None, OP.mult))
            op("vector", lambda e, w=wsb: e.tensor_copy(w[0][:], w32t[:]))
            wsplit_last = op("vector", lambda e, w=wsb: e.tensor_tensor(
                w[1][:], w32t[:], w[0][:], OP.subtract), "sV")
            wait("sync", "sV", wsplit_last)
        op("vector", lambda e: e.memset(bnd[:, 0:2], 0.0))
        for kt in range(NTB - 1):
            op("vector", lambda e, kt=kt: e.memset(vaug[:, 257 * kt:257 * kt + 1], 1.0))
        vones = op("vector", lambda e, kt=NTB - 1: e.memset(vaug[:, 257 * kt:257 * kt + 1], 1.0), "sV")
        wait("tensor", "sV", wsplit_last)
        wait("tensor", "sV", vones)
        bound_state["read"] = None
        bound_state["mm"] = None
        return wsplit_last

    def emit_qkv_group(h, tg, xh_b, xl_b):
        for wi in range(3):
            emit_qkv_proj(h, tg, xh_b, xl_b, wi)

    def emit_qkv_proj(h, tg, xh_b, xl_b, wi, split_gate=None):
        for psx, wsb, name in (((ps0, wqh, "q"), (ps1, wkh, "k"), (ps2, wvh, "v"))[wi],):
            if split_gate is not None:
                wait("tensor", "sV", split_gate)
            if name in qkv_evac:
                wait("tensor", "sA", qkv_evac[name])
            for ch in range(NCH):
                wh = wsb[0][:, 128 * ch:128 * (ch + 1)]
                wl = wsb[1][:, 128 * ch:128 * (ch + 1)]
                xh = xh_b[:, 512 * ch:512 * (ch + 1)]
                xl = xl_b[:, 512 * ch:512 * (ch + 1)]
                op("tensor", lambda e, p=psx, w=wh, x=xh, st=(ch == 0): e.matmul(p[:], w, x, start=st, stop=False))
                op("tensor", lambda e, p=psx, w=wh, x=xl: e.matmul(p[:], w, x, start=False, stop=False))
                if ch < NCH - 1:
                    op("tensor", lambda e, p=psx, w=wl, x=xh: e.matmul(p[:], w, x, start=False, stop=False))
            vmm = op("tensor", lambda e, p=psx, w=wsb[1][:, 128 * (NCH - 1):], x=xh_b[:, 512 * (NCH - 1):]: e.matmul(
                p[:], w, x, start=False, stop=True), "sT")
            xnb_consumed[tg % 2] = vmm

            if qkv_tail:
                qkv_tail.pop(0)()

            wait("scalar", "sT", vmm)
            if name in split_prev:
                wait("scalar", "sV", split_prev[name])
            bcol = qkvbias[:, 3 * h + {"q": 0, "k": 1, "v": 2}[name]:
                           3 * h + {"q": 0, "k": 1, "v": 2}[name] + 1]
            fb = qf32[name]
            ve = op("scalar", lambda e, p=psx, b=bcol, f=fb: e.activation(
                f[:], p[:], AF.Identity, bias=b), "sA")
            qkv_evac[name] = ve
            wait("vector", "sA", ve)
            if name == "q":
                hi_d = Qhi[:, 512 * tg:512 * (tg + 1)]
                lo_d = Qlo[:, 512 * tg:512 * (tg + 1)]
                op("vector", lambda e, d=hi_d, f=fb: e.tensor_scalar(d, f[:], rcb[:, 0:1], None, OP.mult))
                vspl = op("vector", lambda e, d=lo_d, s=hi_d, f=fb: e.scalar_tensor_tensor(
                    d, f[:], rcb[:, 0:1], s, OP.mult, OP.subtract), "sV")
            elif name == "k":
                hi_d = Khi[:, 512 * tg:512 * (tg + 1)]
                lo_d = Klo[:, 512 * tg:512 * (tg + 1)]
                op("vector", lambda e, d=hi_d, f=fb: e.tensor_copy(d, f[:]))
                vspl = op("vector", lambda e, d=lo_d, s=hi_d, f=fb: e.tensor_tensor(
                    d, f[:], s, OP.subtract), "sV")
            else:
                op("vector", lambda e, f=fb: e.tensor_copy(vfs[0][:], f[:]))
                vspl = op("vector", lambda e, f=fb: e.tensor_tensor(
                    vfs[1][:], f[:], vfs[0][:], OP.subtract), "sV")
            split_prev[name] = vspl

            if name in ("q", "k"):
                if bound_state["mm"] is not None:
                    wait("vector", "sT", bound_state["mm"])
                vq = op("vector", lambda e, s=hi_d: e.tensor_mul(qsqb[:], s, s), "sV")
                def btail(vq=vq, name=name):
                    wait("tensor", "sV", vq)
                    if bound_state["read"] is not None:
                        wait("tensor", "sV", bound_state["read"])
                    vmm2 = op("tensor", lambda e: e.matmul(ps4[0:1, 0:512], c_onesc[:], qsqb[:],
                                                           start=True, stop=True), "sT")
                    bound_state["mm"] = vmm2
                    wait("vector", "sT", vmm2)
                    bslot = bnd[0:1, 0:1] if name == "q" else bnd[0:1, 1:2]
                    op("vector", lambda e: e.reduce_max(bnd[0:1, 4:5], ps4[0:1, 0:512], AX.X))
                    bound_state["read"] = op("vector", lambda e, b=bslot: e.tensor_max(b, b, bnd[0:1, 4:5]), "sV")
                qkv_tail.append(btail)
            else:
                def vtail(tg=tg, vspl=vspl):
                    wait("tensor", "sV", vspl)
                    for j in range(4):
                        kt = 4 * tg + j
                        prev = vtev_done.get("last")
                        if prev is not None:
                            wait("tensor", "sV", prev)
                        vt0 = op("tensor", lambda e, j=j: e.transpose(
                            psTb[:, 0:128], vfs[0][:, 128 * j:128 * (j + 1)], c_identb[:]), "sT")
                        vt1 = op("tensor", lambda e, j=j: e.transpose(
                            psTb[:, 128:256], vfs[1][:, 128 * j:128 * (j + 1)], c_identb[:]), "sT")
                        wait("vector", "sT", vt1)
                        op("vector", lambda e, kt=kt: e.tensor_copy(
                            vaug[:, 257 * kt + 1:257 * kt + 129], psTb[:, 0:128]))
                        vtev_done["last"] = op("vector", lambda e, kt=kt: e.tensor_copy(
                            vaug[:, 257 * kt + 129:257 * kt + 257], psTb[:, 128:256]), "sV")
                qkv_tail.append(vtail)

    def flush_qkv_tails():
        while qkv_tail:
            qkv_tail.pop(0)()

    a_done = {}
    split_done = {}
    stage_tok = {}
    evac_done = {}
    pend_qkv = []
    emit_w_section(0)
    for i in range(NTB):
        par = i % 2
        g = i // 4
        if _P1_BARRIER:
            barrier()
        if i >= 2:
            wait("sync", "sA", a_done[i - 2])
        tx = dma("sync", xt[par][:], xb[128 * i:128 * (i + 1), :])
        wait_dma("vector", tx)
        st_, b1_, vb = ln_stats(xt[par][:], 8 * par, double=True)
        wait("scalar", "sV", vb)
        if _DEBUG_TAPS:
            wait("sync", "sV", vb)
            dma("sync", dbg["st1"][i][:, 0:2], mvbuf[:, 2 * par:2 * par + 2])
            dma("sync", dbg["st1"][i][:, 2:7], stat[:, 8 * par:8 * par + 5], allow_slow_non_contiguous=True)
        a_done[i] = op("scalar", lambda e, p=par, s=st_, b=b1_: e.activation(
            ynorm[:], xt[p][:], AF.Identity, bias=b, scale=s), "sA")
        if pend_qkv:
            g_, xp_, wi_, sg_ = pend_qkv.pop(0)
            emit_qkv_proj(0, g_, xnb_hi[xp_], xnb_lo[xp_], wi_, split_gate=sg_)
        tpar = i % 2
        if i >= 2:
            wait("scalar", "sV", split_done[i - 2])
        if i % 4 == 0 and g >= 2:
            wait("vector", "sT", xnb_consumed[g % 2])
        for grp in range(4):
            bank = psT32 if grp % 2 == 0 else ps5
            if grp == 0:
                wait("tensor", "sA", a_done[i])
            if grp < 2 and i > 0:
                wait("tensor", "sA", evac_done[(i - 1, 2 + grp)])
            if grp >= 2:
                wait("tensor", "sA", evac_done[(i, grp - 2)])
            vtr = None
            for j in range(4):
                ch = 4 * grp + j
                vtr = op("tensor", lambda e, c=ch, j=j, b=bank: e.transpose(
                    b[:, 128 * j:128 * (j + 1)], ynorm[:, 128 * c:128 * (c + 1)], c_identf[:]),
                    "sT" if j == 3 else None)
            wait("scalar", "sT", vtr)
            evac_done[(i, grp)] = op("scalar", lambda e, g2=grp, b=bank, p=tpar: e.activation(
                xnf32[p][:, 512 * g2:512 * (g2 + 1)], b[:, 0:512], AF.Identity), "sA")
        wait("vector", "sA", evac_done[(i, 3)])
        if g > 1 and i % 4 == 0:
            wait_dma("vector", stage_tok[g - 2][0])
            wait_dma("vector", stage_tok[g - 2][1])
        dh = xnb_hi[g % 2][:].rearrange("p (c f) -> p c f", c=NCH)[:, :, 128 * (i % 4):128 * (i % 4) + 128]
        dl = xnb_lo[g % 2][:].rearrange("p (c f) -> p c f", c=NCH)[:, :, 128 * (i % 4):128 * (i % 4) + 128]
        op("vector", lambda e, d=dh, p=tpar: e.tensor_copy(d, xnf32[p][:]))
        split_done[i] = op("vector", lambda e, d=dl, sh=dh, p=tpar: e.tensor_tensor(
            d, xnf32[p][:], sh, OP.subtract), "sV")
        if i % 4 == 3:
            wait("sync", "sV", split_done[i])
            th = dma("sync", xnf_hi[g], xnb_hi[g % 2][:])
            tl = dma("sync", xnf_lo[g], xnb_lo[g % 2][:])
            stage_tok[g] = (th, tl)
            if _DEBUG_TAPS:
                dma("sync", dbg["xnf_hi"][g], xnb_hi[g % 2][:])
                dma("sync", dbg["xnf_lo"][g], xnb_lo[g % 2][:])
            for wi_ in range(3):
                pend_qkv.append((g, g % 2, wi_, split_done[i] if wi_ == 0 else None))
    while pend_qkv:
        g_, xp_, wi_, sg_ = pend_qkv.pop(0)
        emit_qkv_proj(0, g_, xnb_hi[xp_], xnb_lo[xp_], wi_, split_gate=sg_)
    flush_qkv_tails()

    flush(block)
    barrier()
    flush(block)

    def early_out(*stacks):
        ob = stacks[0].enter_context(nc.sbuf_tensor("early_outb", [128, D], F32))
        vz = op("vector", lambda e: e.memset(ob[:], 0.0), "sV")
        wait("sync", "sV", vz)
        toks = []
        for bb in range(B):
            for tb in range(4):
                toks.append(dma("sync", out[bb][128 * tb:128 * (tb + 1), :], ob[:]))
        wait_all_dma("gpsimd")
        flush(block)
        for st_ in stacks:
            st_.close()
        ctx.close()
        return nc

    if _PHASE_LIMIT == 1:
        return early_out(ctxA)

    for h in range(NHL):
        if h >= 1:
            if qkv_mm_end[0] is not None:
                wait("vector", "sT", qkv_mm_end[0])
            if attn_end[0] is not None:
                wait("vector", "sT", attn_end[0])
            emit_w_section(h)
            bound_state["read"] = psav_free[0] if h >= 2 else None
            for tg in range(_NG_LIMIT):
                xpar = tg % 2
                xh_b, xl_b = xnb_hi[xpar], xnb_lo[xpar]
                if xnb_consumed[xpar] is not None:
                    wait("sync", "sT", xnb_consumed[xpar])
                if h == 1:
                    wait_dma("sync", stage_tok[tg][0])
                    wait_dma("sync", stage_tok[tg][1])
                th = dma("sync", xh_b[:], xnf_hi[tg])
                tl = dma("sync", xl_b[:], xnf_lo[tg])
                wait_dma("tensor", th)
                wait_dma("tensor", tl)
                emit_qkv_group(h, tg, xh_b, xl_b)
            flush_qkv_tails()
        qkv_mm_end[0] = cnt["sT"]
        wait("vector", "sA", cnt["sA"])
        if _SKIP_25 or _SKIP_BOUND:
            vneg = op("vector", lambda e: e.memset(negmb[:], -1.0), "sV")
        if not (_SKIP_25 or _SKIP_BOUND):
            vbn = op("vector", lambda e: e.tensor_mul(bnd[0:1, 2:3], bnd[0:1, 0:1], bnd[0:1, 1:2]), "sV")
            wait("scalar", "sV", vbn)
            vsq = op("scalar", lambda e: e.activation(bnd[0:1, 3:4], bnd[0:1, 2:3], AF.Sqrt), "sA")
            wait("vector", "sA", vsq)
            vnm = op("vector", lambda e: e.tensor_scalar(negm11[:], bnd[0:1, 3:4], -1.03, -0.01, OP.mult, OP.add), "sV")
            wait("tensor", "sV", vnm)
            vbm = op("tensor", lambda e: e.matmul(ps3[0:128, 0:1], c_onesf[:], negm11[:], start=True, stop=True), "sT")
            wait("vector", "sT", vbm)
            vneg = op("vector", lambda e: e.tensor_copy(negmb[:], ps3[0:128, 0:1]), "sV")

        if _PHASE_LIMIT == 1.3:
            flush(block); barrier(); flush(block)
            return early_out(ctxA)

        psAV = [ps2, ps3, ps4, ps5]
        wait("tensor", "sA", cnt["sA"])
        wait("tensor", "sV", cnt["sV"])
        wait("scalar", "sV", vneg)

        exp_done = {}
        split3 = {}
        avmm_done = {}
        psav_free = [None]
        kseq = 0

        def emit_epi_tr(qb):
            eh, eqg, vvq = pend_epi[0]
            wait("tensor", "sV", vvq[qb])
            if epi_ev_prev[0] is not None:
                wait("tensor", "sV", epi_ev_prev[0])
            op("tensor", lambda e, t=avbh[qb]: e.transpose(psTb[:, 0:128], t[:], c_identb[:]))
            vt1 = op("tensor", lambda e, t=avbl[qb]: e.transpose(psTb[:, 128:256], t[:], c_identb[:]), "sT")
            epi_wait_tr[qb] = vt1
            wait("vector", "sT", vt1)
            if qb == 0 and stg_tok[0] is not None:
                wait_dma("vector", stg_tok[0])
            op("vector", lambda e, qb=qb: e.tensor_copy(
                avfst[:, 128 * qb:128 * (qb + 1)], psTb[:, 0:128]))
            epi_ev_prev[0] = op("vector", lambda e, qb=qb: e.tensor_copy(
                avfst[:, 512 + 128 * qb:512 + 128 * (qb + 1)], psTb[:, 128:256]), "sV")
            if qb == 3:
                wait("sync", "sV", epi_ev_prev[0])
                stg_tok[0] = dma("sync", a2a_in[eh].ap()[eqg], avfst[:])
                if _DEBUG_TAPS and eh == 0:
                    dma("sync", dbg["av0"][eqg], avfst[:])
                pend_epi[0] = None

        sc_done = {}

        def emit_scores_T(qg, kt, kseq):
            psS = (ps0, ps1, psT32)[kseq % 3]
            diag_j = kt - 4 * qg if kt >= 4 * qg else -1
            off = 128 * diag_j if diag_j > 0 else 0
            if kseq >= 3:
                wait("tensor", "sA", exp_done[kseq - 3])
            kh = Khi[:, 128 * kt:128 * (kt + 1)]
            kl = Klo[:, 128 * kt:128 * (kt + 1)]
            qh_ = Qhi[:, 512 * qg + off:512 * (qg + 1)]
            ql_ = Qlo[:, 512 * qg + off:512 * (qg + 1)]
            pslc = psS[:, off:512]
            op("tensor", lambda e, p=pslc, a=kh, b=qh_: e.matmul(p, a, b, start=True, stop=False))
            op("tensor", lambda e, p=pslc, a=kh, b=ql_: e.matmul(p, a, b, start=False, stop=False))
            sc_done[kseq] = op("tensor", lambda e, p=pslc, a=kl, b=qh_: e.matmul(p, a, b, start=False, stop=True), "sT")

        def emit_softmax_VS(qg, kt, kseq):
            par = kseq % 2
            psS = (ps0, ps1, psT32)[kseq % 3]
            diag_j = kt - 4 * qg if kt >= 4 * qg else -1
            off = 128 * diag_j if diag_j > 0 else 0
            vsc_ = sc_done[kseq]
            if diag_j >= 0:
                wait("vector", "sT", vsc_)
                dslc = psS[:, 128 * diag_j:128 * (diag_j + 1)]
                vmk = op("vector", lambda e, d=dslc: e.tensor_tensor(d, d, c_dmask[:], OP.add), "sV")
                wait("scalar", "sV", vmk)
            else:
                wait("scalar", "sT", vsc_)
            if kseq >= 2:
                wait("scalar", "sV", split3[kseq - 2])
                wait("scalar", "sT", avmm_done[kseq - 2])
            ncols = 512 - off
            op("scalar", lambda e, p=psS[:, off:512], b=p32b[par], o=off, n=ncols: e.activation(
                b[:, o:o + n], p, AF.Exp, bias=negmb[:, 0:1]))
            exp_done[kseq] = op("scalar", lambda e, p=psS[:, off:512], b=phib[par], o=off, n=ncols: e.activation(
                b[:, o:o + n], p, AF.Exp, bias=negmb[:, 0:1]), "sA")
            wait("vector", "sA", exp_done[kseq])
            split3[kseq] = op("vector", lambda e, b=p32b[par], s=phib[par], d=plob[par], o=off, n=ncols: e.tensor_tensor(
                d[:, o:o + n], b[:, o:o + n], s[:, o:o + n], OP.subtract), "sV")

        def emit_av(qg, kt, kseq, nkt):
            par = kseq % 2
            diag_j = kt - 4 * qg if kt >= 4 * qg else -1
            if kt == 0 and psav_free[0] is not None:
                wait("tensor", "sV", psav_free[0])
            wait("tensor", "sV", split3[kseq])
            jmin = diag_j if diag_j >= 0 else 0
            for qb in range(jmin, 4):
                qb_last = (kt == 4 * qg + qb)
                phslc = phib[par][:, 128 * qb:128 * (qb + 1)]
                plslc = plob[par][:, 128 * qb:128 * (qb + 1)]
                vslc = vaug[:, 257 * kt:257 * kt + 257]
                vslc129 = vaug[:, 257 * kt:257 * kt + 129]
                if qb_last and kt == 0:
                    op("tensor", lambda e, p=psAV[qb], a=phslc, v=vslc: e.matmul(
                        p[:, 0:257], a, v, start=True, stop=False))
                    op("tensor", lambda e, p=psAV[qb], a=plslc, v=vslc129: e.matmul(
                        p[:, 0:129], a, v, start=False, stop=False))
                    if qb == 3:
                        avmm_done[kseq] = op("tensor", lambda e, p=psAV[qb], v=vslc: e.matmul(
                            p[:, 0:257], c_zerob[:], v, start=False, stop=True), "sT")
                    else:
                        op("tensor", lambda e, p=psAV[qb], v=vslc: e.matmul(
                            p[:, 0:257], c_zerob[:], v, start=False, stop=True))
                elif qb_last:
                    op("tensor", lambda e, p=psAV[qb], a=plslc, v=vslc129: e.matmul(
                        p[:, 0:129], a, v, start=False, stop=False))
                    if qb == 3:
                        avmm_done[kseq] = op("tensor", lambda e, p=psAV[qb], a=phslc, v=vslc: e.matmul(
                            p[:, 0:257], a, v, start=False, stop=True), "sT")
                    else:
                        op("tensor", lambda e, p=psAV[qb], a=phslc, v=vslc: e.matmul(
                            p[:, 0:257], a, v, start=False, stop=True))
                else:
                    op("tensor", lambda e, p=psAV[qb], a=phslc, v=vslc, st=(kt == 0): e.matmul(
                        p[:, 0:257], a, v, start=st, stop=False))
                    if qb == 3:
                        avmm_done[kseq] = op("tensor", lambda e, p=psAV[qb], a=plslc, v=vslc129: e.matmul(
                            p[:, 0:129], a, v, start=False, stop=False), "sT")
                    else:
                        op("tensor", lambda e, p=psAV[qb], a=plslc, v=vslc129: e.matmul(
                            p[:, 0:129], a, v, start=False, stop=False))

        for qg in range(NG):
            nkt = 4 * qg + 4
            base = kseq
            for kt in range(nkt):
                emit_scores_T(qg, kt, base + kt)
                if pend_epi[0] is not None and 1 <= kt <= 4:
                    emit_epi_tr(kt - 1)
                if kt >= 2:
                    emit_av(qg, kt - 2, base + kt - 2, nkt)
                emit_softmax_VS(qg, kt, base + kt)
            emit_av(qg, nkt - 2, base + nkt - 2, nkt)
            emit_av(qg, nkt - 1, base + nkt - 1, nkt)
            kseq = base + nkt

            wait("vector", "sT", avmm_done[kseq - 1])
            epi_psav = None
            vvq = [None] * 4
            for qb in range(4):
                if epi_wait_tr[qb] is not None:
                    wait("vector", "sT", epi_wait_tr[qb])
                op("vector", lambda e, p=psAV[qb], r=rsumc[qb % 2]: e.reciprocal(r[:], p[:, 0:1]))
                op("vector", lambda e, p=psAV[qb], t=avt32[qb]: e.tensor_copy(t[:], p[:, 1:129]))
                epi_psav = op("vector", lambda e, p=psAV[qb], t=avt32[qb]: e.tensor_tensor(
                    t[:], t[:], p[:, 129:257], OP.add), "sV")
                op("vector", lambda e, t=avt32[qb], r=rsumc[qb % 2]: e.tensor_scalar(
                    t[:], t[:], r[:][:, 0:1], None, OP.mult))
                op("vector", lambda e, t=avt32[qb], d=avbh[qb]: e.tensor_copy(d[:], t[:]))
                vvq[qb] = op("vector", lambda e, t=avt32[qb], hh=avbh[qb], d=avbl[qb]: e.tensor_tensor(
                    d[:], t[:], hh[:], OP.subtract), "sV")
            psav_free[0] = epi_psav
            pend_epi[0] = (h, qg, vvq)

        if _PHASE_LIMIT == 1.6:
            flush(block); barrier(); flush(block)
            return early_out(ctxA)

        for qb in range(4):
            emit_epi_tr(qb)

        wait_all_dma("gpsimd")
        def mk_cc(l):
            def go(e):
                with allow_ici_all_to_all():
                    return e.collective_compute(
                        "AllToAll", OP.bypass,
                        replica_groups=[list(range(NCORES))],
                        ins=[a2a_in[l].ap().opt()],
                        outs=[a2a_out[l].ap().opt()])
            return go
        op("gpsimd", mk_cc(h), "sC")
        attn_end[0] = cnt["sT"]
        psav_free_w[0] = psav_free[0]
        vtev_done["last"] = epi_ev_prev[0]

        flush(block)
        if _PHASE_LIMIT == 2:
            break

    if _PHASE_LIMIT == 2:
        return early_out(ctxA)

    ctxA.close()
    ctxB0 = ExitStack()
    x1slab = ctxB0.enter_context(nc.sbuf_tensor("x1slab", [128, 8 * D], F32))
    stln = ctxB0.enter_context(nc.sbuf_tensor("stln", [128, 16], F32))
    ctxB1 = ExitStack()
    sbB = lambda name, shape, dt: ctxB1.enter_context(nc.sbuf_tensor(name, shape, dt))
    avrhi = sbB("avrhi", [128, NCH * 512], BF16)
    avrlo = sbB("avrlo", [128, NCH * 512], BF16)
    wo32 = [sbB("wo32a", [128, 4 * 512], F32)]
    wohi = sbB("wohi", [128, NCH * 512], BF16)
    wolo = sbB("wolo", [128, NCH * 512], BF16)
    ysball = sbB("ysball", [128, 4 * D], F32)
    bo_bc = sbB("bo_bc", [128, D], F32)
    bosb = sbB("bosb", [1, D], F32)
    ynorm2 = sbB("ynorm2", [128, D], F32)

    barrier()

    tbo = dma("sync", bosb[:], bo.unsqueeze(0))
    wait_dma("tensor", tbo)
    vbcp = None
    for og in range(4):
        if vbcp is not None:
            wait("tensor", "sV", vbcp)
        vbm = op("tensor", lambda e, og=og: e.matmul(
            ps1[:, 0:512], c_onesf[:], bosb[0:1, 512 * og:512 * (og + 1)], start=True, stop=True), "sT")
        wait("vector", "sT", vbm)
        vbcp = op("vector", lambda e, og=og: e.tensor_copy(bo_bc[:, 512 * og:512 * (og + 1)], ps1[:, 0:512]), "sV")

    x1_toks = [None] * 8
    for bb in range(B):
        for tb in range(4):
            slab = 4 * bb + tb
            x1_toks[slab] = dma("gpsimd", x1slab[:, D * slab:D * (slab + 1)],
                                xres[bb][128 * tb:128 * (tb + 1), :])

    def emit_avr(eng_name, bb):
        toks = []
        for l in range(NHL):
            dsth = avrhi[:].rearrange("p (c f) -> p c f", c=NCH)[:, l::4, :]
            dstl = avrlo[:].rearrange("p (c f) -> p c f", c=NCH)[:, l::4, :]
            srcs = a2a_out[l].ap()[4 * bb:4 * bb + 4].transpose([1, 0, 2])
            toks.append(dma(eng_name, dsth, srcs[:, :, 0:512]))
            toks.append(dma(eng_name, dstl, srcs[:, :, 512:1024]))
        return toks

    banks = [ps0, ps1, ps2, psT32]
    mm_mark = {}
    split_mark = {}
    evac_mark = [None]
    avr_toks = [None, None]

    def emit_tail(bb, tb):
        yslc = ysball[:, D * tb:D * (tb + 1)]
        st_, b1_, vb = ln_stats(yslc, 0, double=False)
        wait("scalar", "sV", vb)
        vn = op("scalar", lambda e, y=yslc, s=st_, b=b1_: e.activation(
            ynorm2[:], y, AF.Identity, bias=b, scale=s), "sA")
        slab = 4 * bb + tb
        wait("vector", "sA", vn)
        wait_dma("vector", x1_toks[slab])
        op("vector", lambda e, s2=slab: e.tensor_tensor(
            x1slab[:, D * s2:D * (s2 + 1)], x1slab[:, D * s2:D * (s2 + 1)], ynorm2[:], OP.add), "sV")
        st2_, b2_, _vb2 = ln_stats(x1slab[:, D * slab:D * (slab + 1)], 0, double=False)
        op("vector", lambda e, s2=slab, s=st2_: e.tensor_copy(
            stln[:, 2 * s2:2 * s2 + 2], stat[:, 3:5]), "sV")
        if _DEBUG_TAPS:
            dma("sync", dbg["x1"][slab], x1slab[:, D * slab:D * (slab + 1)])

    for s in range(32):
        bb, og, qtr = s // 16, (s % 16) // 4, s % 4
        if s > 0:
            wait("sync", "sV", split_mark[s - 1])
        if s == 2:
            wait("sync", "sC", cnt["sC"])
            avr_toks[0] = emit_avr("sync", 0)
        tl_ = dma("sync", wo32[0][:].rearrange("p (c f) -> p c f", c=4),
                  wo.rearrange("(c p) o -> p c o", p=128)[:, qtr::4,
                                                         512 * og:512 * (og + 1)])
        if s >= 4:
            wait("vector", "sT", mm_mark[s - 4])
        wait_dma("vector", tl_)
        op("vector", lambda e, q=qtr: e.tensor_copy(
            wohi[:, 2048 * q:2048 * (q + 1)], wo32[0][:]))
        split_mark[s] = op("vector", lambda e, q=qtr: e.tensor_tensor(
            wolo[:, 2048 * q:2048 * (q + 1)], wo32[0][:], wohi[:, 2048 * q:2048 * (q + 1)], OP.subtract), "sV")
        wait("tensor", "sV", split_mark[s])
        if s == 0:
            wait("tensor", "sV", vbcp)
        if og == 0:
            if bb == 0:
                for j in (2 * qtr, 2 * qtr + 1):
                    emit("tensor", lambda e, j=j: e.wait_ge(
                        qG if avr_toks[0][j][0] == "G" else qD[avr_toks[0][j][0]],
                        avr_toks[0][j][1]))
            else:
                for j in (2 * qtr, 2 * qtr + 1):
                    wait_dma("tensor", avr_toks[1][j])
        if qtr == 0 and evac_mark[0] is not None:
            wait("tensor", "sV", evac_mark[0])
        for tb in range(4):
            pso = banks[tb]
            for k in range(4):
                chp = 4 * k + qtr
                slot = 4 * qtr + k
                ah = avrhi[:, 512 * chp + 128 * tb:512 * chp + 128 * (tb + 1)]
                al = avrlo[:, 512 * chp + 128 * tb:512 * chp + 128 * (tb + 1)]
                wh = wohi[:, 512 * slot:512 * (slot + 1)]
                wl = wolo[:, 512 * slot:512 * (slot + 1)]
                op("tensor", lambda e, p=pso, a=ah, w=wh, st=(qtr == 0 and k == 0): e.matmul(
                    p[:], a, w, start=st, stop=False))
                op("tensor", lambda e, p=pso, a=ah, w=wl: e.matmul(p[:], a, w, start=False, stop=False))
                stp = (qtr == 3 and k == 3)
                if k == 3 and tb == 3:
                    mm_mark[s] = op("tensor", lambda e, p=pso, a=al, w=wh, stp=stp: e.matmul(
                        p[:], a, w, start=False, stop=stp), "sT")
                else:
                    op("tensor", lambda e, p=pso, a=al, w=wh, stp=stp: e.matmul(
                        p[:], a, w, start=False, stop=stp))
        if bb == 1 and og == 0:
            emit_tail(0, qtr)
        if qtr == 3:
            wait("vector", "sT", mm_mark[s])
            for tb in range(4):
                evac_mark[0] = op("vector", lambda e, p=banks[tb], tb=tb, og=og: e.scalar_tensor_tensor(
                    ysball[:, D * tb + 512 * og:D * tb + 512 * (og + 1)], p[:], 1.0,
                    bo_bc[:, 512 * og:512 * (og + 1)], OP.mult, OP.add), "sV")
            if og == 3:
                if _DEBUG_TAPS:
                    wait("sync", "sV", cnt["sV"])
                    dma("sync", dbg["ysb"][bb], ysball[:])
                    if bb == 0:
                        for l in range(NHL):
                            for si in range(4):
                                dma("sync", dbg["rcv"][4 * si + l], a2a_out[l].ap()[si])
                if bb == 0:
                    toks = []
                    for l in range(NHL):
                        wait("gpsimd", "sT", mm_mark[12 + l])
                        dsth = avrhi[:].rearrange("p (c f) -> p c f", c=NCH)[:, l::4, :]
                        dstl = avrlo[:].rearrange("p (c f) -> p c f", c=NCH)[:, l::4, :]
                        srcs = a2a_out[l].ap()[4:8].transpose([1, 0, 2])
                        toks.append(dma("gpsimd", dsth, srcs[:, :, 0:512]))
                        toks.append(dma("gpsimd", dstl, srcs[:, :, 512:1024]))
                    avr_toks[1] = toks
                else:
                    for tb in range(4):
                        emit_tail(1, tb)

    flush(block)
    barrier()
    flush(block)

    if _PHASE_LIMIT == 3:
        return early_out(ctxB1, ctxB0)

    ctxB1.close()
    ctxB2 = ExitStack()
    sb2 = lambda name, shape, dt: ctxB2.enter_context(nc.sbuf_tensor(name, shape, dt))
    w8row = sb2("w8row", [128, D], F8)
    w8rb = sb2("w8rb", [128, D], BF16)
    w8T = sb2("w8T", [128, NCH * D], BF16)
    x2nb = sb2("x2nb", [128, D], F32)
    tqb = sb2("tqb", [128, D], F32)
    q8b = sb2("q8b", [128, D], U8)
    qbb = sb2("qbb", [128, D], BF16)
    xqf = sb2("xqf", [128, NCH * 128], BF16)
    xqf2 = sb2("xqf2", [128, NCH * 128], BF16)
    ysb2 = sb2("ysb2", [128, D], F32)
    outb = sb2("outb", [128, D], F32)
    psT8 = ctxB2.enter_context(nc.psum_tensor("psT8", [128, 1024], BF16))
    psT8b = ctxB2.enter_context(nc.psum_tensor("psT8b", [128, 1024], BF16))

    xq_bufs = [xqf, xqf2]
    t8_ev = [None, None]
    q_mark = {}
    vtt_last = {}
    xq_ev = {}
    mm_mark2 = {}
    ffn_evac = [None, None]
    tail_vb = {}
    tod_tok = {}

    def emit_qdq_v(slab):
        x1s = x1slab[:, D * slab:D * (slab + 1)]
        if slab > 0:
            wait("scalar", "sV", q_mark[slab - 1])
        vn = op("scalar", lambda e, y=x1s, s=stln[:, 2 * slab:2 * slab + 1],
                b=stln[:, 2 * slab + 1:2 * slab + 2]: e.activation(
            x2nb[:], y, AF.Identity, bias=b, scale=s), "sA")
        wait("vector", "sA", vn)
        op("vector", lambda e: e.tensor_scalar(tqb[:], x2nb[:], invsb[:, 0:1], 448.0, OP.mult, OP.min))
        if slab > 0:
            wait("vector", "sT", vtt_last[slab - 1])
        op("vector", lambda e: e.tensor_scalar(q8b[:].bitcast(F8), tqb[:], -448.0, None, OP.max))
        q_mark[slab] = op("vector", lambda e: e.tensor_copy(qbb[:], q8b[:].bitcast(F8)), "sV")

    emit_qdq_v(0)

    w8T_done = None
    prev_vup = None
    prev_vtt = None
    for oc in range(NCH):
        if prev_vup is not None:
            wait("sync", "sV", prev_vup)
        tw = dma("sync", w8row[:], w8[128 * oc:128 * (oc + 1), :])
        wait_dma("vector", tw)
        if prev_vtt is not None:
            wait("vector", "sT", prev_vtt)
        prev_vup = vup = op("vector", lambda e: e.tensor_copy(w8rb[:], w8row[:]), "sV")
        wait("tensor", "sV", vup)
        for icq in range(4):
            bank = (psT8, psT8b)[icq % 2]
            if t8_ev[icq % 2] is not None:
                wait("tensor", "sV", t8_ev[icq % 2])
            vtt = None
            for j in range(4):
                ic = 4 * icq + j
                vtt = op("tensor", lambda e, ic=ic, j=j, b=bank: e.transpose(
                    b[:, 128 * j:128 * (j + 1)], w8rb[:, 128 * ic:128 * (ic + 1)], c_identb[:]),
                    "sT" if j == 3 else None)
            wait("vector", "sT", vtt)
            t8_ev[icq % 2] = w8T_done = op("vector", lambda e, icq=icq, oc=oc, b=bank: e.tensor_copy(
                w8T[:].rearrange("p (c f) -> p c f", c=NCH)[:, 4 * icq:4 * (icq + 1),
                                                           128 * oc:128 * (oc + 1)],
                b[:, 0:512].rearrange("p (c f) -> p c f", c=4)), "sV")
        prev_vtt = vtt

    def emit_xT(slab):
        par = slab % 2
        if slab >= 2:
            wait("vector", "sT", mm_mark2[(slab - 2, 3)])
        wait("tensor", "sV", q_mark[slab])
        for icq in range(4):
            bank = (psT8, psT8b)[icq % 2]
            if t8_ev[icq % 2] is not None:
                wait("tensor", "sV", t8_ev[icq % 2])
            vtt = None
            for j in range(4):
                ic = 4 * icq + j
                vtt = op("tensor", lambda e, ic=ic, j=j, b=bank: e.transpose(
                    b[:, 128 * j:128 * (j + 1)], qbb[:, 128 * ic:128 * (ic + 1)], c_identb[:]),
                    "sT" if j == 3 else None)
            wait("vector", "sT", vtt)
            t8_ev[icq % 2] = xq_ev[slab] = op("vector", lambda e, icq=icq, b=bank, par=par: e.tensor_copy(
                xq_bufs[par][:, 512 * icq:512 * (icq + 1)], b[:, 0:512]), "sV")
        vtt_last[slab] = vtt

    def emit_mm(slab, og):
        par = slab % 2
        psf = (ps0, ps2)[og % 2]
        if slab == 0 and og == 0:
            wait("tensor", "sV", w8T_done)
        if og == 0:
            wait("tensor", "sV", xq_ev[slab])
        if ffn_evac[og % 2] is not None:
            wait("tensor", "sA", ffn_evac[og % 2])
        vmf = None
        for ic in range(NCH):
            if ic < NCH - 1:
                op("tensor", lambda e, p=psf, ic=ic, og=og, par=par, st=(ic == 0): e.matmul(
                    p[:], xq_bufs[par][:, 128 * ic:128 * (ic + 1)],
                    w8T[:, D * ic + 512 * og:D * ic + 512 * (og + 1)], start=st, stop=False))
            else:
                vmf = op("tensor", lambda e, p=psf, ic=ic, og=og, par=par: e.matmul(
                    p[:], xq_bufs[par][:, 128 * ic:128 * (ic + 1)],
                    w8T[:, D * ic + 512 * og:D * ic + 512 * (og + 1)], start=False, stop=True), "sT")
        mm_mark2[(slab, og)] = vmf
        wait("scalar", "sT", vmf)
        if og == 0 and slab >= 1:
            wait("scalar", "sV", tail_vb[slab - 1])
        ffn_evac[og % 2] = op("scalar", lambda e, p=psf, og=og: e.activation(
            ysb2[:, 512 * og:512 * (og + 1)], p[:], AF.Identity, scale=yscb[:, 0:1]), "sA")

    def emit_tail2(slab):
        bb, tb = slab // 4, slab % 4
        x1s = x1slab[:, D * slab:D * (slab + 1)]
        wait("vector", "sA", ffn_evac[1])
        st_, b1_, vb = ln_stats(ysb2[:], 8, double=False)
        tail_vb[slab] = vb
        wait("scalar", "sV", vb)
        if slab >= 1:
            wait_dma("scalar", tod_tok[slab - 1])
        vn2 = op("scalar", lambda e, s=st_, b=b1_: e.activation(
            outb[:], ysb2[:], AF.Identity, bias=b, scale=s), "sA")
        wait("vector", "sA", vn2)
        vout = op("vector", lambda e, y=x1s: e.tensor_tensor(outb[:], outb[:], y, OP.add), "sV")
        wait("sync", "sV", vout)
        tod_tok[slab] = dma("sync", out[bb][128 * tb:128 * (tb + 1), :], outb[:])

    for it in range(9):
        if 0 < it < 8:
            emit_qdq_v(it)
        if it > 0:
            emit_mm(it - 1, 0)
            emit_mm(it - 1, 1)
        if it < 8:
            emit_xT(it)
        if it > 0:
            emit_mm(it - 1, 2)
            emit_mm(it - 1, 3)
            emit_tail2(it - 1)

    wait_all_dma("gpsimd")
    flush(block)

    ctxB2.close()
    ctxB0.close()
    ctx.close()
    return nc


_NC_CACHE = None


def kernel(L0_attn_X, L0_attn_ln_g, L0_attn_ln_b, L0_attn_Wq, L0_attn_bq,
           L0_attn_Wk, L0_attn_bk, L0_attn_Wv, L0_attn_bv, L0_attn_Wo,
           L0_attn_bo, L0_ffn_weight, L0_ffn_weight_scale, L0_ffn_input_scale):
    global _NC_CACHE
    X = np.ascontiguousarray(L0_attn_X, dtype=np.float32)
    Wq = np.ascontiguousarray(L0_attn_Wq, dtype=np.float32)
    Wk = np.ascontiguousarray(L0_attn_Wk, dtype=np.float32)
    Wv = np.ascontiguousarray(L0_attn_Wv, dtype=np.float32)
    Wo = np.ascontiguousarray(L0_attn_Wo, dtype=np.float32)
    w8v = np.ascontiguousarray(L0_ffn_weight).view(ml_dtypes.float8_e4m3)
    c32 = np.float32(np.sqrt(np.float32(128.0), dtype=np.float32))

    identf = np.eye(128, dtype=np.float32)
    identb = np.eye(128, dtype=ml_dtypes.bfloat16)
    ident8 = np.eye(128).astype(ml_dtypes.float8_e4m3)
    pp, ff = np.arange(128)[:, None], np.arange(128)[None, :]
    dmaskv = np.where(pp <= ff, np.float32(0.0), np.float32(NEG)).astype(np.float32)
    onesfv = np.ones((1, 128), dtype=np.float32)

    in_maps = []
    for c in range(NCORES):
        b, hg = c // 4, c % 4
        in_maps.append(dict(
            xb=X[b],
            xres=np.ascontiguousarray(X[:, 512 * c:512 * (c + 1), :]),
            wq=np.ascontiguousarray(Wq[:, 512 * hg:512 * (hg + 1)]),
            wk=np.ascontiguousarray(Wk[:, 512 * hg:512 * (hg + 1)]),
            wv=np.ascontiguousarray(Wv[:, 512 * hg:512 * (hg + 1)]),
            bq=np.ascontiguousarray(L0_attn_bq[512 * hg:512 * (hg + 1)]).astype(np.float32),
            bk=np.ascontiguousarray(L0_attn_bk[512 * hg:512 * (hg + 1)]).astype(np.float32),
            bv=np.ascontiguousarray(L0_attn_bv[512 * hg:512 * (hg + 1)]).astype(np.float32),
            wo=Wo,
            bo=np.ascontiguousarray(L0_attn_bo, dtype=np.float32),
            lng=np.ascontiguousarray(L0_attn_ln_g, dtype=np.float32),
            lnb=np.ascontiguousarray(L0_attn_ln_b, dtype=np.float32),
            w8=w8v,
            wsc=np.asarray(L0_ffn_weight_scale, dtype=np.float32).reshape(1, 1),
            isc=np.asarray(L0_ffn_input_scale).reshape(1, 1),
            c32t=np.asarray(c32, dtype=np.float32).reshape(1, 1),
            identf=identf, identb=identb, ident8=ident8,
            dmask=dmaskv, onesf=onesfv,
        ))

    if _NC_CACHE is None:
        _NC_CACHE = build_program()
    res = run_bass_kernel_spmd(_NC_CACHE, in_maps, core_ids=list(range(NCORES)))

    outf = np.empty((B, S, D), dtype=np.float32)
    for c in range(NCORES):
        outf[:, 512 * c:512 * (c + 1), :] = res.results[c]["out"]
    return outf


# version 23 (final)
# speedup vs baseline: 1.1016x; 1.0014x over previous
import numpy as np
import ml_dtypes
import concourse.bass as bass
import concourse.mybir as mybir
from concourse.bass_utils import run_bass_kernel_spmd
from concourse.aot_env import allow_ici_all_to_all
from contextlib import ExitStack

F32 = mybir.dt.float32
BF16 = mybir.dt.bfloat16
F8 = mybir.dt.float8e4
U8 = mybir.dt.uint8
AF = mybir.ActivationFunctionType
OP = mybir.AluOpType
AX = mybir.AxisListType

D = 2048
NH = 16
DH = 128
S = 4096
B = 2
NCORES = 8
NHL = 4
NCH = D // 128
NTB = S // 128
NG = S // 512
EPS = 1e-5
NEG = -1.0e30
NDMA = 6

_DEBUG_TAPS = False
_PHASE_LIMIT = 99
_SKIP_BOUND = False
_SKIP_VT = False
_SKIP_25 = False
_NG_LIMIT = NG
_SKIP_PROJ = False
_PROJ_N = 3
_SKIP_EVAC = False
_SKIP_SPLIT = False
_P1_BARRIER = False


def build_program():
    nc = bass.Bass("TRN2", target_bir_lowering=False, detect_race_conditions=False)

    xb = nc.dram_tensor("xb", [S, D], F32, kind="ExternalInput").ap()
    xres = nc.dram_tensor("xres", [B, 512, D], F32, kind="ExternalInput").ap()
    wq = nc.dram_tensor("wq", [D, 512], F32, kind="ExternalInput").ap()
    wk = nc.dram_tensor("wk", [D, 512], F32, kind="ExternalInput").ap()
    wv = nc.dram_tensor("wv", [D, 512], F32, kind="ExternalInput").ap()
    bq = nc.dram_tensor("bq", [512], F32, kind="ExternalInput").ap()
    bk = nc.dram_tensor("bk", [512], F32, kind="ExternalInput").ap()
    bv = nc.dram_tensor("bv", [512], F32, kind="ExternalInput").ap()
    wo = nc.dram_tensor("wo", [D, D], F32, kind="ExternalInput").ap()
    bo = nc.dram_tensor("bo", [D], F32, kind="ExternalInput").ap()
    lng = nc.dram_tensor("lng", [D], F32, kind="ExternalInput").ap()
    lnb = nc.dram_tensor("lnb", [D], F32, kind="ExternalInput").ap()
    w8 = nc.dram_tensor("w8", [D, D], F8, kind="ExternalInput").ap()
    wsc = nc.dram_tensor("wsc", [1, 1], F32, kind="ExternalInput").ap()
    isc = nc.dram_tensor("isc", [1, 1], BF16, kind="ExternalInput").ap()
    c32t = nc.dram_tensor("c32t", [1, 1], F32, kind="ExternalInput").ap()
    identf = nc.dram_tensor("identf", [128, 128], F32, kind="ExternalInput").ap()
    identb = nc.dram_tensor("identb", [128, 128], BF16, kind="ExternalInput").ap()
    ident8 = nc.dram_tensor("ident8", [128, 128], F8, kind="ExternalInput").ap()
    dmask = nc.dram_tensor("dmask", [128, 128], F32, kind="ExternalInput").ap()
    onesf = nc.dram_tensor("onesf", [1, 128], F32, kind="ExternalInput").ap()

    out = nc.dram_tensor("out", [B, 512, D], F32, kind="ExternalOutput").ap()

    xnf_hi = nc.dram_tensor("xnf_hi", [NG, 128, NCH * 512], BF16, kind="Internal").ap()
    xnf_lo = nc.dram_tensor("xnf_lo", [NG, 128, NCH * 512], BF16, kind="Internal").ap()
    a2a_in = [nc.dram_tensor(f"a2a_in{l}", [8, 128, 1024], BF16, kind="Internal")
              for l in range(NHL)]
    a2a_out = [nc.dram_tensor(f"a2a_out{l}", [8, 128, 1024], BF16, kind="Internal")
               for l in range(NHL)]

    dbg = {}
    if _DEBUG_TAPS:
        dbg["xnf_hi"] = nc.dram_tensor("dbg_xnf_hi", [NG, 128, NCH * 512], BF16, kind="ExternalOutput").ap()
        dbg["xnf_lo"] = nc.dram_tensor("dbg_xnf_lo", [NG, 128, NCH * 512], BF16, kind="ExternalOutput").ap()
        dbg["qk"] = nc.dram_tensor("dbg_qk", [4, 128, S], BF16, kind="ExternalOutput").ap()
        dbg["av0"] = nc.dram_tensor("dbg_av0", [8, 128, 1024], BF16, kind="ExternalOutput").ap()
        dbg["x1"] = nc.dram_tensor("dbg_x1", [8, 128, D], F32, kind="ExternalOutput").ap()
        dbg["ysb"] = nc.dram_tensor("dbg_ysb", [B, 128, 4 * D], F32, kind="ExternalOutput").ap()
        dbg["rcv"] = nc.dram_tensor("dbg_rcv", [16, 128, 512], F32, kind="ExternalOutput").ap()
        dbg["st1"] = nc.dram_tensor("dbg_st1", [NTB, 128, 8], F32, kind="ExternalOutput").ap()

    ctx = ExitStack()
    qD = [ctx.enter_context(nc.semaphore(f"qD{i}")) for i in range(NDMA)]
    qG = ctx.enter_context(nc.semaphore("qG"))
    sT = ctx.enter_context(nc.semaphore("sT"))
    sV = ctx.enter_context(nc.semaphore("sV"))
    sA = ctx.enter_context(nc.semaphore("sA"))
    sC = ctx.enter_context(nc.semaphore("sC"))
    esem = dict(sT=sT, sV=sV, sA=sA, sC=sC)
    cnt = dict(sT=0, sV=0, sA=0, sC=0)
    dcnt = [0] * NDMA
    dturn = [0]
    gcnt = [0]

    ps0 = ctx.enter_context(nc.psum_tensor("ps0", [128, 512], F32))
    ps1 = ctx.enter_context(nc.psum_tensor("ps1", [128, 512], F32))
    ps2 = ctx.enter_context(nc.psum_tensor("ps2", [128, 512], F32))
    psT32 = ctx.enter_context(nc.psum_tensor("psT32", [128, 512], F32))

    sb = lambda name, shape, dt: ctx.enter_context(nc.sbuf_tensor(name, shape, dt))
    c_identf = sb("c_identf", [128, 128], F32)
    c_identb = sb("c_identb", [128, 128], BF16)
    c_ident8 = sb("c_ident8", [128, 128], F8)
    c_dmask = sb("c_dmask", [128, 128], F32)
    c_onesf = sb("c_onesf", [1, 128], F32)
    c_onesc = sb("c_onesc", [128, 1], BF16)
    c_zerob = sb("c_zerob", [128, 128], BF16)
    c_lng = sb("c_lng", [128, NCH], F32)
    c_lnb = sb("c_lnb", [128, NCH], F32)
    c_bq = sb("c_bq", [128, NHL], F32)
    c_bk = sb("c_bk", [128, NHL], F32)
    c_bv = sb("c_bv", [128, NHL], F32)
    c_sc = sb("c_sc", [1, 8], F32)
    c_iscb = sb("c_iscb", [1, 1], BF16)
    rcb = sb("rcb", [128, 1], F32)
    invsb = sb("invsb", [128, 1], F32)
    yscb = sb("yscb", [128, 1], F32)
    negm11 = sb("negm11", [1, 1], F32)
    negmb = sb("negmb", [128, 1], F32)
    bnd = sb("bnd", [1, 8], F32)
    stat = sb("stat", [128, 16], F32)
    bnbuf = sb("bnbuf", [128, 48], F32)
    mvbuf = sb("mvbuf", [128, 4], F32)

    phase_ops = []

    def emit(eng, fn):
        phase_ops.append((eng, fn))

    def inc(sem_name, n=1):
        cnt[sem_name] += n
        return cnt[sem_name]

    def wait(eng, sem_name, val):
        assert val <= cnt[sem_name], f"wait {sem_name}>={val}, emitted {cnt[sem_name]}"
        s = esem[sem_name]
        emit(eng, lambda e, s=s, v=val: e.wait_ge(s, v))

    def dma(eng_name, outap, inap, **kw):
        if eng_name == "gpsimd":
            gcnt[0] += 16
            tok = ("G", gcnt[0])
            emit(eng_name, lambda e, o=outap, i=inap, kw=kw: e.dma_start(out=o, in_=i, **kw).then_inc(qG, 16))
            return tok
        i = dturn[0] % NDMA
        dturn[0] += 1
        dcnt[i] += 16
        tok = (i, dcnt[i])
        s = qD[i]
        emit(eng_name, lambda e, o=outap, ia=inap, s=s, kw=kw: e.dma_start(out=o, in_=ia, **kw).then_inc(s, 16))
        return tok

    def wait_dma(eng, tok):
        i, v = tok
        s = qG if i == "G" else qD[i]
        emit(eng, lambda e, s=s, v=v: e.wait_ge(s, v))

    def wait_all_dma(eng):
        for i in range(NDMA):
            if dcnt[i]:
                v = dcnt[i]
                emit(eng, lambda e, s=qD[i], v=v: e.wait_ge(s, v))
        if gcnt[0]:
            v = gcnt[0]
            emit(eng, lambda e, v=v: e.wait_ge(qG, v))

    def op(eng_name, build, sem_name=None, n=1):
        if eng_name in ("vector", "scalar"):
            chain = "sV" if eng_name == "vector" else "sA"
            pre = cnt[chain]
            cs = esem[chain]
            if pre:
                emit(eng_name, lambda e, s=cs, v=pre: e.wait_ge(s, v))
            v = inc(chain)
            emit(eng_name, lambda e, b=build, s=cs: b(e).then_inc(s, 1))
            return v
        if sem_name is None:
            emit(eng_name, lambda e, b=build: b(e))
            return None
        s = esem[sem_name]
        v = inc(sem_name, n)
        emit(eng_name, lambda e, b=build, s=s, n=n: b(e).then_inc(s, n))
        return v

    def flush(block):
        ops = list(phase_ops)
        phase_ops.clear()
        for eng_name in ("sync", "tensor", "vector", "scalar", "gpsimd"):
            mine = [fn for e, fn in ops if e == eng_name]
            if not mine:
                continue
            getattr(block, eng_name)(lambda eng, mine=mine: [fn(eng) for fn in mine])

    def barrier():
        for eng in ("sync", "tensor", "vector", "scalar", "gpsimd"):
            for sname in ("sT", "sV", "sA"):
                if cnt[sname]:
                    wait(eng, sname, cnt[sname])
            wait_all_dma(eng)

    def rsqrt_col(acol, dst, t0, t1):
        va = op("scalar", lambda e, a=acol, d=dst: e.activation(d, a, AF.Sqrt), "sA")
        wait("vector", "sA", va)
        op("vector", lambda e, d=dst: e.reciprocal(d, d))
        for _ in range(1):
            op("vector", lambda e, d=dst, t=t0: e.tensor_mul(t, d, d))
            op("vector", lambda e, a=acol, t=t0, u=t1: e.tensor_mul(u, a, t))
            op("vector", lambda e, u=t1: e.tensor_scalar(u, u, -0.5, 1.5, OP.mult, OP.add))
            op("vector", lambda e, d=dst, u=t1: e.tensor_mul(d, d, u))

    def ln_stats(tile_ap, sc0, double):
        bs = bnbuf[:, 24 * (sc0 // 8):24 * (sc0 // 8) + 24]
        mv = mvbuf[:, 2 * (sc0 // 8):2 * (sc0 // 8) + 2]
        for k in range(4):
            op("vector", lambda e, k=k, t=tile_ap, b=bs: e.bn_stats(
                b[:, 6 * k:6 * k + 6], t[:, 512 * k:512 * (k + 1)]))
        op("vector", lambda e, b=bs, m=mv: e.bn_aggr(m, b))
        a_ = stat[:, sc0 + 0:sc0 + 1]
        t0 = stat[:, sc0 + 1:sc0 + 2]
        t1 = stat[:, sc0 + 2:sc0 + 3]
        st_ = stat[:, sc0 + 3:sc0 + 4]
        b1_ = stat[:, sc0 + 4:sc0 + 5]
        var = mv[:, 1:2]
        if double:
            op("vector", lambda e, v=var, t=t0: e.tensor_scalar(t, v, EPS, EPS * EPS, OP.mult, OP.add))
            va = op("vector", lambda e, v=var, t=t0, a=a_: e.tensor_add(a, v, t), "sV")
        else:
            va = op("vector", lambda e, v=var, a=a_: e.tensor_scalar(a, v, 1.0, EPS, OP.mult, OP.add), "sV")
        wait("scalar", "sV", va)
        rsqrt_col(a_, st_, t0, t1)
        op("vector", lambda e, m=mv, s=st_, b=b1_: e.tensor_mul(b, m[:, 0:1], s))
        vb = op("vector", lambda e, b=b1_: e.tensor_scalar_mul(b, b, -1.0), "sV")
        return st_, b1_, vb

    block = ctx.enter_context(nc.Block())

    dma("sync", c_identf[:], identf)
    dma("sync", c_identb[:], identb)
    dma("sync", c_ident8[:], ident8)
    dma("sync", c_dmask[:], dmask)
    dma("sync", c_onesf[:], onesf)
    dma("sync", c_lng[:], lng.rearrange("(c p) -> p c", p=128), allow_slow_non_contiguous=True)
    dma("sync", c_lnb[:], lnb.rearrange("(c p) -> p c", p=128), allow_slow_non_contiguous=True)
    dma("sync", c_bq[:], bq.rearrange("(h p) -> p h", p=128), allow_slow_non_contiguous=True)
    dma("sync", c_bk[:], bk.rearrange("(h p) -> p h", p=128), allow_slow_non_contiguous=True)
    dma("sync", c_bv[:], bv.rearrange("(h p) -> p h", p=128), allow_slow_non_contiguous=True)
    t1_ = dma("sync", c_sc[:, 0:1], c32t)
    t2_ = dma("sync", c_sc[:, 2:3], wsc)
    t3_ = dma("sync", c_iscb[:], isc)
    for t in (t1_, t2_, t3_):
        wait_dma("vector", t)
    op("vector", lambda e: e.memset(c_onesc[:], 1.0))
    op("vector", lambda e: e.memset(c_zerob[:], 0.0))
    op("vector", lambda e: e.tensor_copy(c_sc[:, 3:4], c_iscb[:]))
    op("vector", lambda e: e.reciprocal(c_sc[:, 1:2], c_sc[:, 0:1]))
    op("vector", lambda e: e.reciprocal(c_sc[:, 4:5], c_sc[:, 3:4]))
    vs = op("vector", lambda e: e.tensor_mul(c_sc[:, 5:6], c_sc[:, 3:4], c_sc[:, 2:3]), "sV")
    wait("tensor", "sV", vs)
    op("tensor", lambda e: e.matmul(ps1[0:128, 0:1], c_onesf[:], c_sc[:, 1:2], start=True, stop=True))
    op("tensor", lambda e: e.matmul(ps1[0:128, 1:2], c_onesf[:], c_sc[:, 4:5], start=True, stop=True))
    vt = op("tensor", lambda e: e.matmul(ps1[0:128, 2:3], c_onesf[:], c_sc[:, 5:6], start=True, stop=True), "sT")
    wait("vector", "sT", vt)
    op("vector", lambda e: e.tensor_copy(rcb[:], ps1[0:128, 0:1]))
    op("vector", lambda e: e.tensor_copy(invsb[:], ps1[0:128, 1:2]))
    vp0 = op("vector", lambda e: e.tensor_copy(yscb[:], ps1[0:128, 2:3]), "sV")

    ctxA = ExitStack()
    sbA = lambda name, shape, dt: ctxA.enter_context(nc.sbuf_tensor(name, shape, dt))
    ps3 = ctxA.enter_context(nc.psum_tensor("ps3", [128, 512], F32))
    ps4 = ctxA.enter_context(nc.psum_tensor("ps4", [128, 512], F32))
    ps5 = ctxA.enter_context(nc.psum_tensor("ps5", [128, 512], F32))
    psTb = ctxA.enter_context(nc.psum_tensor("psTb", [128, 1024], BF16))

    xt = [sbA("xt0", [128, D], F32), sbA("xt1", [128, D], F32)]
    ynorm = sbA("ynorm", [128, D], F32)
    xnf32 = [sbA("xnf32a", [128, D], F32), sbA("xnf32b", [128, D], F32)]
    xnb_hi = [sbA("xnb_hi0", [128, NCH * 512], BF16), sbA("xnb_hi1", [128, NCH * 512], BF16)]
    xnb_lo = [sbA("xnb_lo0", [128, NCH * 512], BF16), sbA("xnb_lo1", [128, NCH * 512], BF16)]
    w32t = ynorm
    wqh = [sbA("wq_hi", [128, D], BF16), sbA("wq_lo", [128, D], BF16)]
    wkh = [sbA("wk_hi", [128, D], BF16), sbA("wk_lo", [128, D], BF16)]
    wvh = [sbA("wv_hi", [128, D], BF16), sbA("wv_lo", [128, D], BF16)]
    qf32 = {"q": sbA("qf32q", [128, 512], F32), "k": sbA("qf32k", [128, 512], F32),
            "v": sbA("qf32v", [128, 512], F32)}
    qsqb = sbA("qsqb", [128, 512], BF16)
    Qhi = sbA("Qhi", [128, S], BF16)
    Qlo = sbA("Qlo", [128, S], BF16)
    Khi = sbA("Khi", [128, S], BF16)
    Klo = sbA("Klo", [128, S], BF16)
    vfs = [sbA("vfs_hi", [128, 512], BF16), sbA("vfs_lo", [128, 512], BF16)]
    vaug = sbA("vaug", [128, NTB * 257], BF16)
    p32b = [sbA("p32b0", [128, 512], F32), sbA("p32b1", [128, 512], F32)]
    phib = [sbA("phib0", [128, 512], BF16), sbA("phib1", [128, 512], BF16)]
    plob = [sbA("plob0", [128, 512], BF16), sbA("plob1", [128, 512], BF16)]
    avt32 = [sbA(f"avt32{q}", [128, 128], F32) for q in range(4)]
    avbh = [sbA(f"avbh{q}", [128, 128], BF16) for q in range(4)]
    avbl = [sbA(f"avbl{q}", [128, 128], BF16) for q in range(4)]
    rsumc = [sbA("rsumc0", [128, 1], F32), sbA("rsumc1", [128, 1], F32)]
    avfst = sbA("avfst", [128, 1024], BF16)

    qkvbias = ctxA.enter_context(nc.sbuf_tensor("qkvbias", [128, 12], F32))
    psav_free_w = [None]
    bias_read = [None]
    qkv_evac = {}
    split_prev = {}
    vtev_done = {}
    xnb_consumed = [None, None]
    stg_tok = [None]
    pend_epi = [None]
    epi_ev_prev = [None]
    epi_wait_tr = [None] * 4
    qkv_mm_end = [None]
    attn_end = [None]
    bound_state = {"read": None, "mm": None}
    qkv_tail = []

    def emit_w_section(h):
        wsplit_last = None
        for wi, (wdram, wsb) in enumerate(((wq, wqh), (wk, wkh), (wv, wvh))):
            tw = dma("sync", w32t[:].rearrange("p (c f) -> p c f", c=NCH),
                     wdram.rearrange("(c p) n -> p c n", p=128)[:, :, 128 * h:128 * (h + 1)])
            wait_dma("tensor", tw)
            if psav_free_w[0] is not None:
                wait("tensor", "sV", psav_free_w[0])
                psav_free_w[0] = None
            if bias_read[0] is not None:
                wait("tensor", "sV", bias_read[0])
            vbm_ = None
            for ch2 in range(NCH):
                vbm_ = op("tensor", lambda e, c=ch2, st=(ch2 == 0), sp=(ch2 == NCH - 1): e.matmul(
                    ps3[0:128, 0:1], w32t[:, 128 * c:128 * (c + 1)], c_lnb[:, c:c + 1],
                    start=st, stop=sp), "sT" if ch2 == NCH - 1 else None)
            wait("vector", "sT", vbm_)
            bcol_d = qkvbias[:, 3 * h + wi:3 * h + wi + 1]
            bsrc = {0: c_bq, 1: c_bk, 2: c_bv}[wi][:, h:h + 1]
            bias_read[0] = op("vector", lambda e, d=bcol_d, b=bsrc: e.tensor_tensor(
                d, ps3[0:128, 0:1], b, OP.add), "sV")
            wait_dma("vector", tw)
            for ch2 in range(NCH):
                op("vector", lambda e, c=ch2: e.tensor_scalar(
                    w32t[:, 128 * c:128 * (c + 1)], w32t[:, 128 * c:128 * (c + 1)],
                    c_lng[:, c:c + 1], None, OP.mult))
            op("vector", lambda e, w=wsb: e.tensor_copy(w[0][:], w32t[:]))
            wsplit_last = op("vector", lambda e, w=wsb: e.tensor_tensor(
                w[1][:], w32t[:], w[0][:], OP.subtract), "sV")
            wait("sync", "sV", wsplit_last)
        op("vector", lambda e: e.memset(bnd[:, 0:2], 0.0))
        for kt in range(NTB - 1):
            op("vector", lambda e, kt=kt: e.memset(vaug[:, 257 * kt:257 * kt + 1], 1.0))
        vones = op("vector", lambda e, kt=NTB - 1: e.memset(vaug[:, 257 * kt:257 * kt + 1], 1.0), "sV")
        wait("tensor", "sV", wsplit_last)
        wait("tensor", "sV", vones)
        bound_state["read"] = None
        bound_state["mm"] = None
        return wsplit_last

    def emit_qkv_group(h, tg, xh_b, xl_b):
        for wi in range(3):
            emit_qkv_proj(h, tg, xh_b, xl_b, wi)

    def emit_qkv_proj(h, tg, xh_b, xl_b, wi, split_gate=None):
        for psx, wsb, name in (((ps0, wqh, "q"), (ps1, wkh, "k"), (ps2, wvh, "v"))[wi],):
            if split_gate is not None:
                wait("tensor", "sV", split_gate)
            if name in qkv_evac:
                wait("tensor", "sA", qkv_evac[name])
            for ch in range(NCH):
                wh = wsb[0][:, 128 * ch:128 * (ch + 1)]
                wl = wsb[1][:, 128 * ch:128 * (ch + 1)]
                xh = xh_b[:, 512 * ch:512 * (ch + 1)]
                xl = xl_b[:, 512 * ch:512 * (ch + 1)]
                op("tensor", lambda e, p=psx, w=wh, x=xh, st=(ch == 0): e.matmul(p[:], w, x, start=st, stop=False))
                op("tensor", lambda e, p=psx, w=wh, x=xl: e.matmul(p[:], w, x, start=False, stop=False))
                if ch < NCH - 1:
                    op("tensor", lambda e, p=psx, w=wl, x=xh: e.matmul(p[:], w, x, start=False, stop=False))
            vmm = op("tensor", lambda e, p=psx, w=wsb[1][:, 128 * (NCH - 1):], x=xh_b[:, 512 * (NCH - 1):]: e.matmul(
                p[:], w, x, start=False, stop=True), "sT")
            xnb_consumed[tg % 2] = vmm

            if qkv_tail:
                qkv_tail.pop(0)()

            wait("scalar", "sT", vmm)
            if name in split_prev:
                wait("scalar", "sV", split_prev[name])
            bcol = qkvbias[:, 3 * h + {"q": 0, "k": 1, "v": 2}[name]:
                           3 * h + {"q": 0, "k": 1, "v": 2}[name] + 1]
            fb = qf32[name]
            ve = op("scalar", lambda e, p=psx, b=bcol, f=fb: e.activation(
                f[:], p[:], AF.Identity, bias=b), "sA")
            qkv_evac[name] = ve
            wait("vector", "sA", ve)
            if name == "q":
                hi_d = Qhi[:, 512 * tg:512 * (tg + 1)]
                lo_d = Qlo[:, 512 * tg:512 * (tg + 1)]
                op("vector", lambda e, d=hi_d, f=fb: e.tensor_scalar(d, f[:], rcb[:, 0:1], None, OP.mult))
                vspl = op("vector", lambda e, d=lo_d, s=hi_d, f=fb: e.scalar_tensor_tensor(
                    d, f[:], rcb[:, 0:1], s, OP.mult, OP.subtract), "sV")
            elif name == "k":
                hi_d = Khi[:, 512 * tg:512 * (tg + 1)]
                lo_d = Klo[:, 512 * tg:512 * (tg + 1)]
                op("vector", lambda e, d=hi_d, f=fb: e.tensor_copy(d, f[:]))
                vspl = op("vector", lambda e, d=lo_d, s=hi_d, f=fb: e.tensor_tensor(
                    d, f[:], s, OP.subtract), "sV")
            else:
                op("vector", lambda e, f=fb: e.tensor_copy(vfs[0][:], f[:]))
                vspl = op("vector", lambda e, f=fb: e.tensor_tensor(
                    vfs[1][:], f[:], vfs[0][:], OP.subtract), "sV")
            split_prev[name] = vspl

            if name in ("q", "k"):
                if bound_state["mm"] is not None:
                    wait("vector", "sT", bound_state["mm"])
                vq = op("vector", lambda e, s=hi_d: e.tensor_mul(qsqb[:], s, s), "sV")
                def btail(vq=vq, name=name):
                    wait("tensor", "sV", vq)
                    if bound_state["read"] is not None:
                        wait("tensor", "sV", bound_state["read"])
                    vmm2 = op("tensor", lambda e: e.matmul(ps4[0:1, 0:512], c_onesc[:], qsqb[:],
                                                           start=True, stop=True), "sT")
                    bound_state["mm"] = vmm2
                    wait("vector", "sT", vmm2)
                    bslot = bnd[0:1, 0:1] if name == "q" else bnd[0:1, 1:2]
                    op("vector", lambda e: e.reduce_max(bnd[0:1, 4:5], ps4[0:1, 0:512], AX.X))
                    bound_state["read"] = op("vector", lambda e, b=bslot: e.tensor_max(b, b, bnd[0:1, 4:5]), "sV")
                qkv_tail.append(btail)
            else:
                def vtail(tg=tg, vspl=vspl):
                    wait("tensor", "sV", vspl)
                    for j in range(4):
                        kt = 4 * tg + j
                        prev = vtev_done.get("last")
                        if prev is not None:
                            wait("tensor", "sV", prev)
                        vt0 = op("tensor", lambda e, j=j: e.transpose(
                            psTb[:, 0:128], vfs[0][:, 128 * j:128 * (j + 1)], c_identb[:]), "sT")
                        vt1 = op("tensor", lambda e, j=j: e.transpose(
                            psTb[:, 128:256], vfs[1][:, 128 * j:128 * (j + 1)], c_identb[:]), "sT")
                        wait("vector", "sT", vt1)
                        op("vector", lambda e, kt=kt: e.tensor_copy(
                            vaug[:, 257 * kt + 1:257 * kt + 129], psTb[:, 0:128]))
                        vtev_done["last"] = op("vector", lambda e, kt=kt: e.tensor_copy(
                            vaug[:, 257 * kt + 129:257 * kt + 257], psTb[:, 128:256]), "sV")
                qkv_tail.append(vtail)

    def flush_qkv_tails():
        while qkv_tail:
            qkv_tail.pop(0)()

    a_done = {}
    split_done = {}
    stage_tok = {}
    evac_done = {}
    pend_qkv = []
    emit_w_section(0)
    for i in range(NTB):
        par = i % 2
        g = i // 4
        if _P1_BARRIER:
            barrier()
        if i >= 2:
            wait("sync", "sA", a_done[i - 2])
        tx = dma("sync", xt[par][:], xb[128 * i:128 * (i + 1), :])
        wait_dma("vector", tx)
        st_, b1_, vb = ln_stats(xt[par][:], 8 * par, double=True)
        wait("scalar", "sV", vb)
        if _DEBUG_TAPS:
            wait("sync", "sV", vb)
            dma("sync", dbg["st1"][i][:, 0:2], mvbuf[:, 2 * par:2 * par + 2])
            dma("sync", dbg["st1"][i][:, 2:7], stat[:, 8 * par:8 * par + 5], allow_slow_non_contiguous=True)
        a_done[i] = op("scalar", lambda e, p=par, s=st_, b=b1_: e.activation(
            ynorm[:], xt[p][:], AF.Identity, bias=b, scale=s), "sA")
        if pend_qkv:
            g_, xp_, wi_, sg_ = pend_qkv.pop(0)
            emit_qkv_proj(0, g_, xnb_hi[xp_], xnb_lo[xp_], wi_, split_gate=sg_)
        tpar = i % 2
        if i >= 2:
            wait("scalar", "sV", split_done[i - 2])
        if i % 4 == 0 and g >= 2:
            wait("vector", "sT", xnb_consumed[g % 2])
        for grp in range(4):
            bank = psT32 if grp % 2 == 0 else ps5
            if grp == 0:
                wait("tensor", "sA", a_done[i])
            if grp < 2 and i > 0:
                wait("tensor", "sA", evac_done[(i - 1, 2 + grp)])
            if grp >= 2:
                wait("tensor", "sA", evac_done[(i, grp - 2)])
            vtr = None
            for j in range(4):
                ch = 4 * grp + j
                vtr = op("tensor", lambda e, c=ch, j=j, b=bank: e.transpose(
                    b[:, 128 * j:128 * (j + 1)], ynorm[:, 128 * c:128 * (c + 1)], c_identf[:]),
                    "sT" if j == 3 else None)
            wait("scalar", "sT", vtr)
            evac_done[(i, grp)] = op("scalar", lambda e, g2=grp, b=bank, p=tpar: e.activation(
                xnf32[p][:, 512 * g2:512 * (g2 + 1)], b[:, 0:512], AF.Identity), "sA")
        wait("vector", "sA", evac_done[(i, 3)])
        if g > 1 and i % 4 == 0:
            wait_dma("vector", stage_tok[g - 2][0])
            wait_dma("vector", stage_tok[g - 2][1])
        dh = xnb_hi[g % 2][:].rearrange("p (c f) -> p c f", c=NCH)[:, :, 128 * (i % 4):128 * (i % 4) + 128]
        dl = xnb_lo[g % 2][:].rearrange("p (c f) -> p c f", c=NCH)[:, :, 128 * (i % 4):128 * (i % 4) + 128]
        op("vector", lambda e, d=dh, p=tpar: e.tensor_copy(d, xnf32[p][:]))
        split_done[i] = op("vector", lambda e, d=dl, sh=dh, p=tpar: e.tensor_tensor(
            d, xnf32[p][:], sh, OP.subtract), "sV")
        if i % 4 == 3:
            wait("sync", "sV", split_done[i])
            th = dma("sync", xnf_hi[g], xnb_hi[g % 2][:])
            tl = dma("sync", xnf_lo[g], xnb_lo[g % 2][:])
            stage_tok[g] = (th, tl)
            if _DEBUG_TAPS:
                dma("sync", dbg["xnf_hi"][g], xnb_hi[g % 2][:])
                dma("sync", dbg["xnf_lo"][g], xnb_lo[g % 2][:])
            for wi_ in range(3):
                pend_qkv.append((g, g % 2, wi_, split_done[i] if wi_ == 0 else None))
    while pend_qkv:
        g_, xp_, wi_, sg_ = pend_qkv.pop(0)
        emit_qkv_proj(0, g_, xnb_hi[xp_], xnb_lo[xp_], wi_, split_gate=sg_)
    flush_qkv_tails()

    flush(block)
    barrier()
    flush(block)

    def early_out(*stacks):
        ob = stacks[0].enter_context(nc.sbuf_tensor("early_outb", [128, D], F32))
        vz = op("vector", lambda e: e.memset(ob[:], 0.0), "sV")
        wait("sync", "sV", vz)
        toks = []
        for bb in range(B):
            for tb in range(4):
                toks.append(dma("sync", out[bb][128 * tb:128 * (tb + 1), :], ob[:]))
        wait_all_dma("gpsimd")
        flush(block)
        for st_ in stacks:
            st_.close()
        ctx.close()
        return nc

    if _PHASE_LIMIT == 1:
        return early_out(ctxA)

    for h in range(NHL):
        if h >= 1:
            if qkv_mm_end[0] is not None:
                wait("vector", "sT", qkv_mm_end[0])
            if attn_end[0] is not None:
                wait("vector", "sT", attn_end[0])
            emit_w_section(h)
            bound_state["read"] = psav_free[0] if h >= 2 else None
            for tg in range(_NG_LIMIT):
                xpar = tg % 2
                xh_b, xl_b = xnb_hi[xpar], xnb_lo[xpar]
                if xnb_consumed[xpar] is not None:
                    wait("sync", "sT", xnb_consumed[xpar])
                if h == 1:
                    wait_dma("sync", stage_tok[tg][0])
                    wait_dma("sync", stage_tok[tg][1])
                th = dma("sync", xh_b[:], xnf_hi[tg])
                tl = dma("sync", xl_b[:], xnf_lo[tg])
                wait_dma("tensor", th)
                wait_dma("tensor", tl)
                emit_qkv_group(h, tg, xh_b, xl_b)
            flush_qkv_tails()
        qkv_mm_end[0] = cnt["sT"]
        wait("vector", "sA", cnt["sA"])
        if _SKIP_25 or _SKIP_BOUND:
            vneg = op("vector", lambda e: e.memset(negmb[:], -1.0), "sV")
        if not (_SKIP_25 or _SKIP_BOUND):
            vbn = op("vector", lambda e: e.tensor_mul(bnd[0:1, 2:3], bnd[0:1, 0:1], bnd[0:1, 1:2]), "sV")
            wait("scalar", "sV", vbn)
            vsq = op("scalar", lambda e: e.activation(bnd[0:1, 3:4], bnd[0:1, 2:3], AF.Sqrt), "sA")
            wait("vector", "sA", vsq)
            vnm = op("vector", lambda e: e.tensor_scalar(negm11[:], bnd[0:1, 3:4], -1.03, -0.01, OP.mult, OP.add), "sV")
            wait("tensor", "sV", vnm)
            vbm = op("tensor", lambda e: e.matmul(ps3[0:128, 0:1], c_onesf[:], negm11[:], start=True, stop=True), "sT")
            wait("vector", "sT", vbm)
            vneg = op("vector", lambda e: e.tensor_copy(negmb[:], ps3[0:128, 0:1]), "sV")

        if _PHASE_LIMIT == 1.3:
            flush(block); barrier(); flush(block)
            return early_out(ctxA)

        psAV = [ps2, ps3, ps4, ps5]
        wait("tensor", "sA", cnt["sA"])
        wait("tensor", "sV", cnt["sV"])
        wait("scalar", "sV", vneg)

        exp_done = {}
        split3 = {}
        avmm_done = {}
        psav_free = [None]
        kseq = 0

        def emit_epi_tr(qb):
            eh, eqg, vvq = pend_epi[0]
            wait("tensor", "sV", vvq[qb])
            if epi_ev_prev[0] is not None:
                wait("tensor", "sV", epi_ev_prev[0])
            op("tensor", lambda e, t=avbh[qb]: e.transpose(psTb[:, 0:128], t[:], c_identb[:]))
            vt1 = op("tensor", lambda e, t=avbl[qb]: e.transpose(psTb[:, 128:256], t[:], c_identb[:]), "sT")
            epi_wait_tr[qb] = vt1
            wait("vector", "sT", vt1)
            if qb == 0 and stg_tok[0] is not None:
                wait_dma("vector", stg_tok[0])
            op("vector", lambda e, qb=qb: e.tensor_copy(
                avfst[:, 128 * qb:128 * (qb + 1)], psTb[:, 0:128]))
            epi_ev_prev[0] = op("vector", lambda e, qb=qb: e.tensor_copy(
                avfst[:, 512 + 128 * qb:512 + 128 * (qb + 1)], psTb[:, 128:256]), "sV")
            if qb == 3:
                wait("sync", "sV", epi_ev_prev[0])
                stg_tok[0] = dma("sync", a2a_in[eh].ap()[eqg], avfst[:])
                if _DEBUG_TAPS and eh == 0:
                    dma("sync", dbg["av0"][eqg], avfst[:])
                pend_epi[0] = None

        sc_done = {}

        def emit_scores_T(qg, kt, kseq):
            psS = (ps0, ps1, psT32)[kseq % 3]
            diag_j = kt - 4 * qg if kt >= 4 * qg else -1
            off = 128 * diag_j if diag_j > 0 else 0
            if kseq >= 3:
                wait("tensor", "sA", exp_done[kseq - 3])
            kh = Khi[:, 128 * kt:128 * (kt + 1)]
            kl = Klo[:, 128 * kt:128 * (kt + 1)]
            qh_ = Qhi[:, 512 * qg + off:512 * (qg + 1)]
            ql_ = Qlo[:, 512 * qg + off:512 * (qg + 1)]
            pslc = psS[:, off:512]
            op("tensor", lambda e, p=pslc, a=kh, b=qh_: e.matmul(p, a, b, start=True, stop=False))
            op("tensor", lambda e, p=pslc, a=kh, b=ql_: e.matmul(p, a, b, start=False, stop=False))
            sc_done[kseq] = op("tensor", lambda e, p=pslc, a=kl, b=qh_: e.matmul(p, a, b, start=False, stop=True), "sT")

        def emit_softmax_VS(qg, kt, kseq):
            par = kseq % 2
            psS = (ps0, ps1, psT32)[kseq % 3]
            diag_j = kt - 4 * qg if kt >= 4 * qg else -1
            off = 128 * diag_j if diag_j > 0 else 0
            vsc_ = sc_done[kseq]
            if diag_j >= 0:
                wait("vector", "sT", vsc_)
                dslc = psS[:, 128 * diag_j:128 * (diag_j + 1)]
                vmk = op("vector", lambda e, d=dslc: e.tensor_tensor(d, d, c_dmask[:], OP.add), "sV")
                wait("scalar", "sV", vmk)
            else:
                wait("scalar", "sT", vsc_)
            if kseq >= 2:
                wait("scalar", "sV", split3[kseq - 2])
                wait("scalar", "sT", avmm_done[kseq - 2])
            ncols = 512 - off
            op("scalar", lambda e, p=psS[:, off:512], b=p32b[par], o=off, n=ncols: e.activation(
                b[:, o:o + n], p, AF.Exp, bias=negmb[:, 0:1]))
            exp_done[kseq] = op("scalar", lambda e, p=psS[:, off:512], b=phib[par], o=off, n=ncols: e.activation(
                b[:, o:o + n], p, AF.Exp, bias=negmb[:, 0:1]), "sA")
            wait("vector", "sA", exp_done[kseq])
            split3[kseq] = op("vector", lambda e, b=p32b[par], s=phib[par], d=plob[par], o=off, n=ncols: e.tensor_tensor(
                d[:, o:o + n], b[:, o:o + n], s[:, o:o + n], OP.subtract), "sV")

        def emit_av(qg, kt, kseq, nkt):
            par = kseq % 2
            diag_j = kt - 4 * qg if kt >= 4 * qg else -1
            if kt == 0 and psav_free[0] is not None:
                wait("tensor", "sV", psav_free[0])
            wait("tensor", "sV", split3[kseq])
            jmin = diag_j if diag_j >= 0 else 0
            for qb in range(jmin, 4):
                qb_last = (kt == 4 * qg + qb)
                phslc = phib[par][:, 128 * qb:128 * (qb + 1)]
                plslc = plob[par][:, 128 * qb:128 * (qb + 1)]
                vslc = vaug[:, 257 * kt:257 * kt + 257]
                vslc129 = vaug[:, 257 * kt:257 * kt + 129]
                if qb_last and kt == 0:
                    op("tensor", lambda e, p=psAV[qb], a=phslc, v=vslc: e.matmul(
                        p[:, 0:257], a, v, start=True, stop=False))
                    op("tensor", lambda e, p=psAV[qb], a=plslc, v=vslc129: e.matmul(
                        p[:, 0:129], a, v, start=False, stop=False))
                    if qb == 3:
                        avmm_done[kseq] = op("tensor", lambda e, p=psAV[qb], v=vslc: e.matmul(
                            p[:, 0:257], c_zerob[:], v, start=False, stop=True), "sT")
                    else:
                        op("tensor", lambda e, p=psAV[qb], v=vslc: e.matmul(
                            p[:, 0:257], c_zerob[:], v, start=False, stop=True))
                elif qb_last:
                    op("tensor", lambda e, p=psAV[qb], a=plslc, v=vslc129: e.matmul(
                        p[:, 0:129], a, v, start=False, stop=False))
                    if qb == 3:
                        avmm_done[kseq] = op("tensor", lambda e, p=psAV[qb], a=phslc, v=vslc: e.matmul(
                            p[:, 0:257], a, v, start=False, stop=True), "sT")
                    else:
                        op("tensor", lambda e, p=psAV[qb], a=phslc, v=vslc: e.matmul(
                            p[:, 0:257], a, v, start=False, stop=True))
                else:
                    op("tensor", lambda e, p=psAV[qb], a=phslc, v=vslc, st=(kt == 0): e.matmul(
                        p[:, 0:257], a, v, start=st, stop=False))
                    if qb == 3:
                        avmm_done[kseq] = op("tensor", lambda e, p=psAV[qb], a=plslc, v=vslc129: e.matmul(
                            p[:, 0:129], a, v, start=False, stop=False), "sT")
                    else:
                        op("tensor", lambda e, p=psAV[qb], a=plslc, v=vslc129: e.matmul(
                            p[:, 0:129], a, v, start=False, stop=False))

        for qg in range(NG):
            nkt = 4 * qg + 4
            base = kseq
            for kt in range(nkt):
                emit_scores_T(qg, kt, base + kt)
                if pend_epi[0] is not None and 1 <= kt <= 4:
                    emit_epi_tr(kt - 1)
                if kt >= 2:
                    emit_av(qg, kt - 2, base + kt - 2, nkt)
                emit_softmax_VS(qg, kt, base + kt)
            emit_av(qg, nkt - 2, base + nkt - 2, nkt)
            emit_av(qg, nkt - 1, base + nkt - 1, nkt)
            kseq = base + nkt

            wait("vector", "sT", avmm_done[kseq - 1])
            epi_psav = None
            vvq = [None] * 4
            for qb in range(4):
                if epi_wait_tr[qb] is not None:
                    wait("vector", "sT", epi_wait_tr[qb])
                op("vector", lambda e, p=psAV[qb], r=rsumc[qb % 2]: e.reciprocal(r[:], p[:, 0:1]))
                op("vector", lambda e, p=psAV[qb], t=avt32[qb]: e.tensor_copy(t[:], p[:, 1:129]))
                epi_psav = op("vector", lambda e, p=psAV[qb], t=avt32[qb]: e.tensor_tensor(
                    t[:], t[:], p[:, 129:257], OP.add), "sV")
                op("vector", lambda e, t=avt32[qb], r=rsumc[qb % 2]: e.tensor_scalar(
                    t[:], t[:], r[:][:, 0:1], None, OP.mult))
                op("vector", lambda e, t=avt32[qb], d=avbh[qb]: e.tensor_copy(d[:], t[:]))
                vvq[qb] = op("vector", lambda e, t=avt32[qb], hh=avbh[qb], d=avbl[qb]: e.tensor_tensor(
                    d[:], t[:], hh[:], OP.subtract), "sV")
            psav_free[0] = epi_psav
            pend_epi[0] = (h, qg, vvq)

        if _PHASE_LIMIT == 1.6:
            flush(block); barrier(); flush(block)
            return early_out(ctxA)

        for qb in range(4):
            emit_epi_tr(qb)

        wait_all_dma("gpsimd")
        def mk_cc(l):
            def go(e):
                with allow_ici_all_to_all():
                    return e.collective_compute(
                        "AllToAll", OP.bypass,
                        replica_groups=[list(range(NCORES))],
                        ins=[a2a_in[l].ap().opt()],
                        outs=[a2a_out[l].ap().opt()])
            return go
        op("gpsimd", mk_cc(h), "sC")
        attn_end[0] = cnt["sT"]
        psav_free_w[0] = psav_free[0]
        vtev_done["last"] = epi_ev_prev[0]

        flush(block)
        if _PHASE_LIMIT == 2:
            break

    if _PHASE_LIMIT == 2:
        return early_out(ctxA)

    ctxA.close()
    ctxB0 = ExitStack()
    x1slab = ctxB0.enter_context(nc.sbuf_tensor("x1slab", [128, 8 * D], F32))
    stln = ctxB0.enter_context(nc.sbuf_tensor("stln", [128, 16], F32))
    ctxB1 = ExitStack()
    sbB = lambda name, shape, dt: ctxB1.enter_context(nc.sbuf_tensor(name, shape, dt))
    avrhi = sbB("avrhi", [128, NCH * 512], BF16)
    avrlo = sbB("avrlo", [128, NCH * 512], BF16)
    wo32 = [sbB("wo32a", [128, 4 * 512], F32)]
    wohi = sbB("wohi", [128, NCH * 512], BF16)
    wolo = sbB("wolo", [128, NCH * 512], BF16)
    ysball = sbB("ysball", [128, 4 * D], F32)
    bo_bc = sbB("bo_bc", [128, D], F32)
    bosb = sbB("bosb", [1, D], F32)
    ynorm2 = sbB("ynorm2", [128, D], F32)

    barrier()

    tbo = dma("sync", bosb[:], bo.unsqueeze(0))
    wait_dma("tensor", tbo)
    vbcp = None
    for og in range(4):
        if vbcp is not None:
            wait("tensor", "sV", vbcp)
        vbm = op("tensor", lambda e, og=og: e.matmul(
            ps1[:, 0:512], c_onesf[:], bosb[0:1, 512 * og:512 * (og + 1)], start=True, stop=True), "sT")
        wait("vector", "sT", vbm)
        vbcp = op("vector", lambda e, og=og: e.tensor_copy(bo_bc[:, 512 * og:512 * (og + 1)], ps1[:, 0:512]), "sV")

    x1_toks = [None] * 8
    for bb in range(B):
        for tb in range(4):
            slab = 4 * bb + tb
            x1_toks[slab] = dma("gpsimd", x1slab[:, D * slab:D * (slab + 1)],
                                xres[bb][128 * tb:128 * (tb + 1), :])

    def emit_avr(eng_name, bb):
        toks = []
        for l in range(NHL):
            dsth = avrhi[:].rearrange("p (c f) -> p c f", c=NCH)[:, l::4, :]
            dstl = avrlo[:].rearrange("p (c f) -> p c f", c=NCH)[:, l::4, :]
            srcs = a2a_out[l].ap()[4 * bb:4 * bb + 4].transpose([1, 0, 2])
            toks.append(dma(eng_name, dsth, srcs[:, :, 0:512]))
            toks.append(dma(eng_name, dstl, srcs[:, :, 512:1024]))
        return toks

    banks = [ps0, ps1, ps2, psT32]
    mm_mark = {}
    split_mark = {}
    evac_mark = [None]
    avr_toks = [None, None]

    def emit_tail(bb, tb):
        yslc = ysball[:, D * tb:D * (tb + 1)]
        st_, b1_, vb = ln_stats(yslc, 0, double=False)
        wait("scalar", "sV", vb)
        vn = op("scalar", lambda e, y=yslc, s=st_, b=b1_: e.activation(
            ynorm2[:], y, AF.Identity, bias=b, scale=s), "sA")
        slab = 4 * bb + tb
        wait("vector", "sA", vn)
        wait_dma("vector", x1_toks[slab])
        op("vector", lambda e, s2=slab: e.tensor_tensor(
            x1slab[:, D * s2:D * (s2 + 1)], x1slab[:, D * s2:D * (s2 + 1)], ynorm2[:], OP.add), "sV")
        st2_, b2_, _vb2 = ln_stats(x1slab[:, D * slab:D * (slab + 1)], 0, double=False)
        op("vector", lambda e, s2=slab, s=st2_: e.tensor_copy(
            stln[:, 2 * s2:2 * s2 + 2], stat[:, 3:5]), "sV")
        if _DEBUG_TAPS:
            dma("sync", dbg["x1"][slab], x1slab[:, D * slab:D * (slab + 1)])

    for s in range(32):
        bb, og, qtr = s // 16, (s % 16) // 4, s % 4
        if s > 0:
            wait("sync", "sV", split_mark[s - 1])
        if s == 2:
            wait("sync", "sC", cnt["sC"])
            avr_toks[0] = emit_avr("sync", 0)
        tl_ = dma("sync", wo32[0][:].rearrange("p (c f) -> p c f", c=4),
                  wo.rearrange("(c p) o -> p c o", p=128)[:, qtr::4,
                                                         512 * og:512 * (og + 1)])
        if s >= 4:
            wait("vector", "sT", mm_mark[s - 4])
        wait_dma("vector", tl_)
        op("vector", lambda e, q=qtr: e.tensor_copy(
            wohi[:, 2048 * q:2048 * (q + 1)], wo32[0][:]))
        split_mark[s] = op("vector", lambda e, q=qtr: e.tensor_tensor(
            wolo[:, 2048 * q:2048 * (q + 1)], wo32[0][:], wohi[:, 2048 * q:2048 * (q + 1)], OP.subtract), "sV")
        wait("tensor", "sV", split_mark[s])
        if s == 0:
            wait("tensor", "sV", vbcp)
        if og == 0:
            if bb == 0:
                for j in (2 * qtr, 2 * qtr + 1):
                    emit("tensor", lambda e, j=j: e.wait_ge(
                        qG if avr_toks[0][j][0] == "G" else qD[avr_toks[0][j][0]],
                        avr_toks[0][j][1]))
            else:
                for j in (2 * qtr, 2 * qtr + 1):
                    wait_dma("tensor", avr_toks[1][j])
        if qtr == 0 and evac_mark[0] is not None:
            wait("tensor", "sV", evac_mark[0])
        for tb in range(4):
            pso = banks[tb]
            for k in range(4):
                chp = 4 * k + qtr
                slot = 4 * qtr + k
                ah = avrhi[:, 512 * chp + 128 * tb:512 * chp + 128 * (tb + 1)]
                al = avrlo[:, 512 * chp + 128 * tb:512 * chp + 128 * (tb + 1)]
                wh = wohi[:, 512 * slot:512 * (slot + 1)]
                wl = wolo[:, 512 * slot:512 * (slot + 1)]
                op("tensor", lambda e, p=pso, a=ah, w=wh, st=(qtr == 0 and k == 0): e.matmul(
                    p[:], a, w, start=st, stop=False))
                op("tensor", lambda e, p=pso, a=ah, w=wl: e.matmul(p[:], a, w, start=False, stop=False))
                stp = (qtr == 3 and k == 3)
                if k == 3 and tb == 3:
                    mm_mark[s] = op("tensor", lambda e, p=pso, a=al, w=wh, stp=stp: e.matmul(
                        p[:], a, w, start=False, stop=stp), "sT")
                else:
                    op("tensor", lambda e, p=pso, a=al, w=wh, stp=stp: e.matmul(
                        p[:], a, w, start=False, stop=stp))
        if bb == 1 and og == 0:
            emit_tail(0, qtr)
        if qtr == 3:
            wait("vector", "sT", mm_mark[s])
            for tb in range(4):
                evac_mark[0] = op("vector", lambda e, p=banks[tb], tb=tb, og=og: e.scalar_tensor_tensor(
                    ysball[:, D * tb + 512 * og:D * tb + 512 * (og + 1)], p[:], 1.0,
                    bo_bc[:, 512 * og:512 * (og + 1)], OP.mult, OP.add), "sV")
            if og == 3:
                if _DEBUG_TAPS:
                    wait("sync", "sV", cnt["sV"])
                    dma("sync", dbg["ysb"][bb], ysball[:])
                    if bb == 0:
                        for l in range(NHL):
                            for si in range(4):
                                dma("sync", dbg["rcv"][4 * si + l], a2a_out[l].ap()[si])
                if bb == 0:
                    toks = []
                    for l in range(NHL):
                        wait("gpsimd", "sT", mm_mark[12 + l])
                        dsth = avrhi[:].rearrange("p (c f) -> p c f", c=NCH)[:, l::4, :]
                        dstl = avrlo[:].rearrange("p (c f) -> p c f", c=NCH)[:, l::4, :]
                        srcs = a2a_out[l].ap()[4:8].transpose([1, 0, 2])
                        toks.append(dma("gpsimd", dsth, srcs[:, :, 0:512]))
                        toks.append(dma("gpsimd", dstl, srcs[:, :, 512:1024]))
                    avr_toks[1] = toks
                else:
                    for tb in range(4):
                        emit_tail(1, tb)

    flush(block)
    barrier()
    flush(block)

    if _PHASE_LIMIT == 3:
        return early_out(ctxB1, ctxB0)

    ctxB1.close()
    ctxB2 = ExitStack()
    sb2 = lambda name, shape, dt: ctxB2.enter_context(nc.sbuf_tensor(name, shape, dt))
    w8row = [sb2("w8row", [128, D], F8), sb2("w8row2", [128, D], F8)]
    w8rb = [sb2("w8rb", [128, D], BF16), sb2("w8rb2", [128, D], BF16)]
    w8T = sb2("w8T", [128, NCH * D], BF16)
    x2nb = sb2("x2nb", [128, D], F32)
    tqb = sb2("tqb", [128, D], F32)
    q8b = sb2("q8b", [128, D], U8)
    qbb = sb2("qbb", [128, D], BF16)
    xqf = sb2("xqf", [128, NCH * 128], BF16)
    xqf2 = sb2("xqf2", [128, NCH * 128], BF16)
    ysb2 = sb2("ysb2", [128, D], F32)
    outb = sb2("outb", [128, D], F32)
    psT8 = ctxB2.enter_context(nc.psum_tensor("psT8", [128, 1024], BF16))
    psT8b = ctxB2.enter_context(nc.psum_tensor("psT8b", [128, 1024], BF16))

    xq_bufs = [xqf, xqf2]
    t8_ev = [None, None]
    q_mark = {}
    vtt_last = {}
    xq_ev = {}
    mm_mark2 = {}
    ffn_evac = [None, None]
    tail_vb = {}
    tod_tok = {}

    def emit_qdq_v(slab):
        x1s = x1slab[:, D * slab:D * (slab + 1)]
        if slab > 0:
            wait("scalar", "sV", q_mark[slab - 1])
        vn = op("scalar", lambda e, y=x1s, s=stln[:, 2 * slab:2 * slab + 1],
                b=stln[:, 2 * slab + 1:2 * slab + 2]: e.activation(
            x2nb[:], y, AF.Identity, bias=b, scale=s), "sA")
        wait("vector", "sA", vn)
        op("vector", lambda e: e.tensor_scalar(tqb[:], x2nb[:], invsb[:, 0:1], 448.0, OP.mult, OP.min))
        if slab > 0:
            wait("vector", "sT", vtt_last[slab - 1])
        op("vector", lambda e: e.tensor_scalar(q8b[:].bitcast(F8), tqb[:], -448.0, None, OP.max))
        q_mark[slab] = op("vector", lambda e: e.tensor_copy(qbb[:], q8b[:].bitcast(F8)), "sV")

    emit_qdq_v(0)

    w8T_done = None
    vup_mark = {}
    vtt_mark = {}
    for oc in range(NCH):
        pq = oc % 2
        if oc >= 2:
            wait("sync", "sV", vup_mark[oc - 2])
        tw = dma("sync", w8row[pq][:], w8[128 * oc:128 * (oc + 1), :])
        wait_dma("vector", tw)
        if oc >= 2:
            wait("vector", "sT", vtt_mark[oc - 2])
        vup_mark[oc] = vup = op("vector", lambda e, pq=pq: e.tensor_copy(w8rb[pq][:], w8row[pq][:]), "sV")
        wait("tensor", "sV", vup)
        for icq in range(4):
            bank = (psT8, psT8b)[icq % 2]
            if t8_ev[icq % 2] is not None:
                wait("tensor", "sV", t8_ev[icq % 2])
            vtt = None
            for j in range(4):
                ic = 4 * icq + j
                vtt = op("tensor", lambda e, ic=ic, j=j, b=bank, pq=pq: e.transpose(
                    b[:, 128 * j:128 * (j + 1)], w8rb[pq][:, 128 * ic:128 * (ic + 1)], c_identb[:]),
                    "sT" if j == 3 else None)
            wait("vector", "sT", vtt)
            t8_ev[icq % 2] = w8T_done = op("vector", lambda e, icq=icq, oc=oc, b=bank: e.tensor_copy(
                w8T[:].rearrange("p (c f) -> p c f", c=NCH)[:, 4 * icq:4 * (icq + 1),
                                                           128 * oc:128 * (oc + 1)],
                b[:, 0:512].rearrange("p (c f) -> p c f", c=4)), "sV")
        vtt_mark[oc] = vtt

    def emit_xT(slab):
        par = slab % 2
        if slab >= 2:
            wait("vector", "sT", mm_mark2[(slab - 2, 3)])
        wait("tensor", "sV", q_mark[slab])
        for icq in range(4):
            bank = (psT8, psT8b)[icq % 2]
            if t8_ev[icq % 2] is not None:
                wait("tensor", "sV", t8_ev[icq % 2])
            vtt = None
            for j in range(4):
                ic = 4 * icq + j
                vtt = op("tensor", lambda e, ic=ic, j=j, b=bank: e.transpose(
                    b[:, 128 * j:128 * (j + 1)], qbb[:, 128 * ic:128 * (ic + 1)], c_identb[:]),
                    "sT" if j == 3 else None)
            wait("vector", "sT", vtt)
            t8_ev[icq % 2] = xq_ev[slab] = op("vector", lambda e, icq=icq, b=bank, par=par: e.tensor_copy(
                xq_bufs[par][:, 512 * icq:512 * (icq + 1)], b[:, 0:512]), "sV")
        vtt_last[slab] = vtt

    def emit_mm(slab, og):
        par = slab % 2
        psf = (ps0, ps2)[og % 2]
        if slab == 0 and og == 0:
            wait("tensor", "sV", w8T_done)
        if og == 0:
            wait("tensor", "sV", xq_ev[slab])
        if ffn_evac[og % 2] is not None:
            wait("tensor", "sA", ffn_evac[og % 2])
        vmf = None
        for ic in range(NCH):
            if ic < NCH - 1:
                op("tensor", lambda e, p=psf, ic=ic, og=og, par=par, st=(ic == 0): e.matmul(
                    p[:], xq_bufs[par][:, 128 * ic:128 * (ic + 1)],
                    w8T[:, D * ic + 512 * og:D * ic + 512 * (og + 1)], start=st, stop=False))
            else:
                vmf = op("tensor", lambda e, p=psf, ic=ic, og=og, par=par: e.matmul(
                    p[:], xq_bufs[par][:, 128 * ic:128 * (ic + 1)],
                    w8T[:, D * ic + 512 * og:D * ic + 512 * (og + 1)], start=False, stop=True), "sT")
        mm_mark2[(slab, og)] = vmf
        wait("scalar", "sT", vmf)
        if og == 0 and slab >= 1:
            wait("scalar", "sV", tail_vb[slab - 1])
        ffn_evac[og % 2] = op("scalar", lambda e, p=psf, og=og: e.activation(
            ysb2[:, 512 * og:512 * (og + 1)], p[:], AF.Identity, scale=yscb[:, 0:1]), "sA")

    def emit_tail2(slab):
        bb, tb = slab // 4, slab % 4
        x1s = x1slab[:, D * slab:D * (slab + 1)]
        wait("vector", "sA", ffn_evac[1])
        st_, b1_, vb = ln_stats(ysb2[:], 8, double=False)
        tail_vb[slab] = vb
        wait("scalar", "sV", vb)
        if slab >= 1:
            wait_dma("scalar", tod_tok[slab - 1])
        vn2 = op("scalar", lambda e, s=st_, b=b1_: e.activation(
            outb[:], ysb2[:], AF.Identity, bias=b, scale=s), "sA")
        wait("vector", "sA", vn2)
        vout = op("vector", lambda e, y=x1s: e.tensor_tensor(outb[:], outb[:], y, OP.add), "sV")
        wait("sync", "sV", vout)
        tod_tok[slab] = dma("sync", out[bb][128 * tb:128 * (tb + 1), :], outb[:])

    for it in range(9):
        if 0 < it < 8:
            emit_qdq_v(it)
        if it > 0:
            emit_mm(it - 1, 0)
            emit_mm(it - 1, 1)
        if it < 8:
            emit_xT(it)
        if it > 0:
            emit_mm(it - 1, 2)
            emit_mm(it - 1, 3)
            emit_tail2(it - 1)

    wait_all_dma("gpsimd")
    flush(block)

    ctxB2.close()
    ctxB0.close()
    ctx.close()
    return nc


_NC_CACHE = None


def kernel(L0_attn_X, L0_attn_ln_g, L0_attn_ln_b, L0_attn_Wq, L0_attn_bq,
           L0_attn_Wk, L0_attn_bk, L0_attn_Wv, L0_attn_bv, L0_attn_Wo,
           L0_attn_bo, L0_ffn_weight, L0_ffn_weight_scale, L0_ffn_input_scale):
    global _NC_CACHE
    X = np.ascontiguousarray(L0_attn_X, dtype=np.float32)
    Wq = np.ascontiguousarray(L0_attn_Wq, dtype=np.float32)
    Wk = np.ascontiguousarray(L0_attn_Wk, dtype=np.float32)
    Wv = np.ascontiguousarray(L0_attn_Wv, dtype=np.float32)
    Wo = np.ascontiguousarray(L0_attn_Wo, dtype=np.float32)
    w8v = np.ascontiguousarray(L0_ffn_weight).view(ml_dtypes.float8_e4m3)
    c32 = np.float32(np.sqrt(np.float32(128.0), dtype=np.float32))

    identf = np.eye(128, dtype=np.float32)
    identb = np.eye(128, dtype=ml_dtypes.bfloat16)
    ident8 = np.eye(128).astype(ml_dtypes.float8_e4m3)
    pp, ff = np.arange(128)[:, None], np.arange(128)[None, :]
    dmaskv = np.where(pp <= ff, np.float32(0.0), np.float32(NEG)).astype(np.float32)
    onesfv = np.ones((1, 128), dtype=np.float32)

    in_maps = []
    for c in range(NCORES):
        b, hg = c // 4, c % 4
        in_maps.append(dict(
            xb=X[b],
            xres=np.ascontiguousarray(X[:, 512 * c:512 * (c + 1), :]),
            wq=np.ascontiguousarray(Wq[:, 512 * hg:512 * (hg + 1)]),
            wk=np.ascontiguousarray(Wk[:, 512 * hg:512 * (hg + 1)]),
            wv=np.ascontiguousarray(Wv[:, 512 * hg:512 * (hg + 1)]),
            bq=np.ascontiguousarray(L0_attn_bq[512 * hg:512 * (hg + 1)]).astype(np.float32),
            bk=np.ascontiguousarray(L0_attn_bk[512 * hg:512 * (hg + 1)]).astype(np.float32),
            bv=np.ascontiguousarray(L0_attn_bv[512 * hg:512 * (hg + 1)]).astype(np.float32),
            wo=Wo,
            bo=np.ascontiguousarray(L0_attn_bo, dtype=np.float32),
            lng=np.ascontiguousarray(L0_attn_ln_g, dtype=np.float32),
            lnb=np.ascontiguousarray(L0_attn_ln_b, dtype=np.float32),
            w8=w8v,
            wsc=np.asarray(L0_ffn_weight_scale, dtype=np.float32).reshape(1, 1),
            isc=np.asarray(L0_ffn_input_scale).reshape(1, 1),
            c32t=np.asarray(c32, dtype=np.float32).reshape(1, 1),
            identf=identf, identb=identb, ident8=ident8,
            dmask=dmaskv, onesf=onesfv,
        ))

    if _NC_CACHE is None:
        _NC_CACHE = build_program()
    res = run_bass_kernel_spmd(_NC_CACHE, in_maps, core_ids=list(range(NCORES)))

    outf = np.empty((B, S, D), dtype=np.float32)
    for c in range(NCORES):
        outf[:, 512 * c:512 * (c + 1), :] = res.results[c]["out"]
    return outf
